# version 12
# speedup vs baseline: 1.1694x; 1.0180x over previous
import numpy as np
import ml_dtypes
import concourse.bass as bass
import concourse.mybir as mybir
from concourse.bass_isa import ReduceOp
from concourse.bass_utils import run_bass_kernel_spmd
from contextlib import ExitStack

F32 = mybir.dt.float32
BF16 = mybir.dt.bfloat16
I32 = mybir.dt.int32
AF = mybir.ActivationFunctionType
ALU = mybir.AluOpType
AX = mybir.AxisListType.X

D = 1024
TW = 512
NT = 8
SUB = 128
NSUB = 4
H, DKH, DVH = 4, 128, 256
EPS = 1e-5
CRND = float(3 * (2 ** 22))
LNSCALE = float(np.log(np.float32(np.float32(128.0) ** np.float32(-0.5))))
BF = ml_dtypes.bfloat16

_CACHED = {}
DBG_TILES = 8
DBG_PHASE = 99
DBG_PREP = 99
DBG_TAPS = ()


class Sem:
    def __init__(self, handle):
        self.h = handle
        self.n = 0


class B:
    ENG = ("sync", "tensor", "vector", "scalar", "gpsimd")

    def __init__(self, nc, ctx):
        self.nc = nc
        self.ops = {e: [] for e in self.ENG}
        self.sem = {}
        for name in ("dC", "dB2", "dGc", "dStage", "dMr", "dSt",
                     "rs0", "rs1", "rs2", "rs3", "rg0", "rg1", "rg2", "rg3",
                     "sxb0", "sxb1", "sxf", "sst", "stm",
                     "cP", "cV", "cS", "cG"):
            self.sem[name] = Sem(ctx.enter_context(nc.semaphore(name)))
        self._lastw = {}

    def wait(self, eng, sname, val):
        if val <= 0:
            return
        key = (eng, sname)
        if self._lastw.get(key, -1) >= val:
            return
        self._lastw[key] = val
        s = self.sem[sname]
        self.ops[eng].append(lambda e, h=s.h, v=val: e.wait_ge(h, v))

    def emit(self, eng, fn):
        self.ops[eng].append(fn)

    def dma(self, qeng, sname, out_ap, in_ap, **kw):
        s = self.sem[sname]

        def _f(e, o=out_ap, i=in_ap, h=s.h, kw=kw):
            e.dma_start(out=o, in_=i, **kw).then_inc(h, 16)
        self.emit(qeng, _f)
        s.n += 16
        return s.n

    def dmat(self, qeng, sname, out_ap, in_ap):
        s = self.sem[sname]

        def _f(e, o=out_ap, i=in_ap, h=s.h):
            e.dma_start_transpose(o, i).then_inc(h, 16)
        self.emit(qeng, _f)
        s.n += 16
        return s.n

    def mm(self, out, lhsT, rhs, start, stop, inc=False):
        s = self.sem["cP"]
        if inc:
            def _f(t, o=out, l=lhsT, r=rhs, a=start, b=stop, h=s.h):
                t.matmul(o, l, r, start=a, stop=b).then_inc(h, 1)
            self.emit("tensor", _f)
            s.n += 1
            return s.n

        def _g(t, o=out, l=lhsT, r=rhs, a=start, b=stop):
            t.matmul(o, l, r, start=a, stop=b)
        self.emit("tensor", _g)
        return s.n

    def trans(self, out, in_, ident, inc=False):
        s = self.sem["cP"]
        if inc:
            def _f(t, o=out, i=in_, d=ident, h=s.h):
                t.transpose(o, i, d).then_inc(h, 1)
            self.emit("tensor", _f)
            s.n += 1
            return s.n

        def _g(t, o=out, i=in_, d=ident):
            t.transpose(o, i, d)
        self.emit("tensor", _g)
        return s.n

    def act(self, out, in_, func, bias=0.0, scale=1.0, accum=None, inc=True):
        s = self.sem["cS"]

        def _f(e, o=out, i=in_, f=func, b=bias, sc=scale, a=accum, h=s.h, ic=inc):
            ins = e.activation(o, i, f, bias=b, scale=sc, accum_out=a)
            if ic:
                ins.then_inc(h, 1)
        self.emit("scalar", _f)
        if inc:
            s.n += 1
        return s.n

    def scopy(self, out, in_, inc=True):
        s = self.sem["cS"]

        def _f(e, o=out, i=in_, h=s.h, ic=inc):
            ins = e.copy(o, i)
            if ic:
                ins.then_inc(h, 1)
        self.emit("scalar", _f)
        if inc:
            s.n += 1
        return s.n

    def v(self, name, *args, inc=True, **kw):
        s = self.sem["cV"]

        def _f(e, nm=name, a=args, k=kw, h=s.h, ic=inc):
            ins = getattr(e, nm)(*a, **k)
            if ic:
                ins.then_inc(h, 1)
        self.emit("vector", _f)
        if inc:
            s.n += 1
        return s.n

    def g(self, name, *args, inc=True, **kw):
        s = self.sem["cG"]

        def _f(e, nm=name, a=args, k=kw, h=s.h, ic=inc):
            ins = getattr(e, nm)(*a, **k)
            if ic:
                ins.then_inc(h, 1)
        self.emit("gpsimd", _f)
        if inc:
            s.n += 1
        return s.n

    def fence(self, qeng, dsem_name, dval):
        self.wait(qeng, dsem_name, dval)
        st = self.sem["stm"]

        def _f(e, h=st.h):
            e.sem_inc(h, 1)
        self.emit(qeng, _f)
        st.n += 1
        return st.n

    def pe_mark(self):
        s = self.sem["cP"]

        def _f(t, h=s.h):
            t.nop().then_inc(h, 1)
        self.emit("tensor", _f)
        s.n += 1
        return s.n


def magic_rsqrt(bb, out, inp, t1, t2, eps):
    cV = bb.sem["cV"].n
    bb.wait("scalar", "cV", cV)
    actc = bb._actc
    p = inp.shape[0]
    lnr = bb.act(t1, inp, AF.Ln, bias=actc[0:p, 6:7], scale=1.0)
    bb.wait("scalar", "cS", lnr)
    r = bb.act(out, t1, AF.Exp, bias=0.0, scale=actc[0:p, 7:8])
    bb.wait("vector", "cS", r)
    return r


def build_program():
    nc = bass.Bass("TRN2", target_bir_lowering=False, detect_race_conditions=False)

    def inp(name, shape, dt=F32):
        return nc.dram_tensor(name, shape, dt, kind="ExternalInput").ap()

    xT = inp("xT", [D, NT * TW])
    c2 = inp("c2", [2, D])
    wmod = inp("wmod", [D, 6 * D])
    wq_h = inp("wq", [D, 512]); wk_h = inp("wk", [D, 512])
    wv_h = inp("wv", [D, 1024]); wg_h = inp("wg", [D, 1024]); wo_h = inp("wo", [D, 1024])
    wgk1_h = inp("wgk1", [D, 16]); wgk2_h = inp("wgk2", [16, 512])
    bgk2_h = inp("bgk2", [512]); gnw_h = inp("gnw", [256])
    ln1g_h = inp("ln1g", [D]); ln1b_h = inp("ln1b", [D])
    w1_h = inp("w1", [4096, 1024]); b1_h = inp("b1", [4096])
    ln2g_h = inp("ln2g", [4096]); ln2b_h = inp("ln2b", [4096])
    w2_h = inp("w2", [1024, 4096]); b2_h = inp("b2", [1024])
    identb_h = inp("identb", [128, 128], BF16)
    identf_h = inp("identf", [128, 128], F32)
    triu_h = inp("triu", [128, 128], BF16)
    ones1_h = inp("ones1", [1, 128], BF16)
    onesr_h = inp("onesr", [1, 512], BF16)
    onessc_h = inp("onessc", [128, 128], F32)
    actc_h = inp("actc", [128, 10], F32)

    outT = nc.dram_tensor("outT", [D, NT * TW], F32, kind="ExternalOutput").ap()
    dbgf = nc.dram_tensor("dbgf", [128, 8192], F32, kind="ExternalOutput").ap()
    dbgb = nc.dram_tensor("dbgb", [128, 16384], BF16, kind="ExternalOutput").ap()

    s_wq = nc.dram_tensor("s_wq", [D, 512], BF16).ap()
    s_wk = nc.dram_tensor("s_wk", [D, 512], BF16).ap()
    s_wv = nc.dram_tensor("s_wv", [D, 1024], BF16).ap()
    s_wg = nc.dram_tensor("s_wg", [D, 1024], BF16).ap()
    s_wo = nc.dram_tensor("s_wo", [D, 1024], BF16).ap()
    s_w1q = nc.dram_tensor("s_w1q", [4096, 1024], BF16).ap()
    s_w1qT = nc.dram_tensor("s_w1qT", [1024, 4096], BF16).ap()
    s_w2q = nc.dram_tensor("s_w2q", [1024, 4096], BF16).ap()
    s_w2qT = nc.dram_tensor("s_w2qT", [4096, 1024], BF16).ap()

    ctx = ExitStack()
    e = ctx.enter_context

    def sb(name, shape, dt):
        return e(nc.sbuf_tensor("sb_" + name, shape, dt)).ap()

    identb = sb("identb", [128, 128], BF16)
    identf = sb("identf", [128, 128], F32)
    triu = sb("triu", [128, 128], BF16)
    ones1 = sb("ones1", [1, 128], BF16)
    onesr = sb("onesr", [1, 512], BF16)
    onessc = sb("onessc", [128, 128], F32)
    actc = sb("actc", [128, 10], F32)
    wgk1s = sb("wgk1s", [128, 128], BF16)
    wgk2s = sb("wgk2s", [16, 512], BF16)
    nbgk2 = sb("nbgk2", [128, 4], F32)
    gnwc = sb("gnwc", [128, 2], F32)
    b1row = sb("b1row", [1, 4096], BF16)
    b2row = sb("b2row", [1, 1024], BF16)
    g1B = sb("g1B", [128, 1024], BF16)
    b1B = sb("b1B", [128, 1024], BF16)
    g2B = sb("g2B", [128, 4096], BF16)
    b2B = sb("b2B", [128, 4096], BF16)
    cbB = [sb(f"cbB{s}", [128, 1024], BF16) for s in range(2)]
    sbB = [sb(f"sbB{s}", [128, 1024], BF16) for s in range(2)]
    modrows = sb("modrows", [1, 4096], BF16)
    gcols = sb("gcols", [128, 32], F32)
    rowsb = sb("rowsb", [1, 1024], BF16)
    wmean = sb("wmean", [128, 2], F32)
    wscale = sb("wscale", [128, 2], F32)
    macc = sb("macc", [128, 40], F32)
    colA = sb("colA", [128, 48], F32)
    colB = sb("colB", [128, 16], F32)
    sumP = sb("sumP", [128, 16], F32)
    ygsum = sb("ygsum", [128, 32], F32)
    glall = sb("glall", [128, 16], F32)
    ssqo = sb("ssqo", [128, 16], F32)
    scT = sb("scT", [128, 16], F32)

    ring = [sb(f"ring{i}", [128, 4096], BF16) for i in range(4)]
    ringf = [r.bitcast(F32) for r in ring]

    def cm_src(ap2d):
        return ap2d.rearrange("(c p) j -> p c j", p=128)

    def cm_dst(sb_ap, C):
        return sb_ap.rearrange("p (c j) -> p c j", c=C)

    x_f = sb("x_f", [128, 4096], F32)
    x_b = [sb("x_b0", [128, 4096], BF16)]
    h_b = sb("h_b", [128, 4096], BF16)
    hscr = h_b.bitcast(F32)
    actA = sb("actA", [128, 16384], BF16)
    scrA = sb("scrA", [128, 512], F32)
    scrB = sb("scrB", [128, 512], F32)
    scrAb = scrA.bitcast(BF16)
    kgtm = sb("kgtm", [128, 512], BF16)
    atb = sb("atb", [128, 512], BF16)
    o_n = sb("o_n", [128, 1024], BF16)
    S_f = sb("S_f", [128, 1024], F32)
    S_b = sb("S_b", [128, 1024], BF16)
    ttm4 = sb("ttm4", [128, 4096], BF16)
    dqtm = sb("dqtm", [128, 1024], BF16)
    h2qT = sb("h2qT", [128, 4096], BF16)
    y1qT = sb("y1qT", [128, 8192], BF16)
    stage1 = y1qT[0:1, 0:4096]
    xtm4 = y1qT[:, 0:4096]
    rowsf = y1qT.bitcast(F32)[0:1, 2048:4096]

    qg = actA[:, 0:2048]
    ki = actA[:, 2048:4096]
    GpT = sb("GpT", [128, 2048], F32)
    Gp = GpT[:, :]
    v_tm = actA[:, 8192:12288]
    gsig = actA[:, 12288:16384]
    y1g = [actA[:, s * 4096:(s + 1) * 4096] for s in range(NSUB)]
    modraw = actA[0:2, 0:12288].bitcast(F32)
    prow = actA[0:2, 12288:16384].bitcast(F32)

    pf = [e(nc.psum_tensor(f"pf{i}", [128, 512], F32)).ap() for i in range(6)]
    pb = [e(nc.psum_tensor(f"pbb{i}", [128, 1024], BF16)).ap() for i in range(2)]

    bb = B(nc, ctx)
    bb._actc = actc
    cP, cV, cS, cG = "cP", "cV", "cS", "cG"

    def max_tree(out_col, buf_bf, w):
        while w > 1:
            h = w // 2
            bb.v("tensor_tensor", buf_bf[:, 0:h], buf_bf[:, 0:h], buf_bf[:, h:w],
                 op=ALU.max, inc=False)
            w = h
        return bb.v("tensor_copy", out_col, buf_bf[:, 0:1])

    def add_fold(out_col, buf_f32, w):
        while w > 1:
            h = w // 2
            bb.v("tensor_tensor", buf_f32[:, 0:h], buf_f32[:, 0:h], buf_f32[:, h:w],
                 op=ALU.add, inc=False)
            w = h
        return bb.v("tensor_copy", out_col, buf_f32[:, 0:1])

    def acc_fence():
        return bb.scopy(macc[0:1, 39:40], actc[0:1, 0:1])

    tapn = {"f": 0, "b": 0}
    tap_offsets = {}

    def tap(name, src_ap, wait_sem=None, wait_val=0, bf=False):
        if name not in DBG_TAPS:
            return
        if wait_sem is not None:
            bb.wait("sync", wait_sem, wait_val)
        w = src_ap.shape[1]
        key = "b" if bf else "f"
        dst = (dbgb if bf else dbgf)
        o = tapn[key]
        tapn[key] += w
        tap_offsets[name] = (key, o, w, src_ap.shape[0])
        st = bb.dma("sync", "dSt", dst[0:src_ap.shape[0], o:o + w], src_ap)
        fv = bb.fence("sync", "dSt", st)
        for eng in ("vector", "scalar", "tensor", "gpsimd"):
            bb.wait(eng, "stm", fv)

    slot_rel = [("cP", 0)] * 4
    slot_idx = [0]

    def ring_load(qeng, src_ap, cols=4096, f32view=False):
        si = slot_idx[0] % 4
        slot_idx[0] += 1
        rsem, rval = slot_rel[si]
        bb.wait(qeng, rsem, rval)
        sname = (f"rg{si}" if qeng == "gpsimd" else f"rs{si}")
        dst = ringf[si][:, 0:cols] if f32view else ring[si][:, 0:cols]
        ld = bb.dma(qeng, sname, dst, src_ap)
        return si, sname, ld

    def ring_release(si, sname, val):
        slot_rel[si] = (sname, val)

    for dst, src in ((identb, identb_h), (identf, identf_h), (triu, triu_h),
                     (ones1, ones1_h), (onesr, onesr_h), (onessc, onessc_h),
                     (actc, actc_h)):
        bb.dma("sync", "dC", dst, src)
    bb.dma("sync", "dC", nbgk2, bgk2_h.rearrange("(c p) -> p c", p=128),
           allow_slow_non_contiguous=True)
    bb.dma("sync", "dC", gnwc, gnw_h.rearrange("(c p) -> p c", p=128),
           allow_slow_non_contiguous=True)
    consts_done = bb.dma("sync", "dC", prow[:, 0:1024], c2)

    prep_on = lambda k: DBG_PREP >= k
    bb.dma("gpsimd", "dGc", wgk1s, cm_src(wgk1_h))
    bb.dma("gpsimd", "dGc", wgk2s, wgk2_h)
    bb.dma("gpsimd", "dGc", b1row, b1_h[None, :])
    gk_loaded = bb.dma("gpsimd", "dGc", b2row, b2_h[None, :])

    bb.wait("vector", "dC", consts_done)
    nbg_done = bb.v("tensor_scalar", nbgk2, nbgk2, -1.0, None, op0=ALU.mult)

    if prep_on(1):
        bb.wait("scalar", "dC", consts_done)
        silu_done = bb.act(prow[:, 0:1024], prow[:, 0:1024], AF.Silu)
        prev_ev = 0
        for c in range(8):
            bb.wait("tensor", cS, silu_done)
            if c >= 2:
                bb.wait("tensor", cV, prev_ev)
            pe_t = bb.trans(pf[c % 2][:, 0:2], prow[:, c * 128:(c + 1) * 128],
                            identf[0:2, 0:2], inc=True)
            bb.wait("vector", cP, pe_t)
            prev_ev = bb.v("tensor_copy", scT[:, c * 2:(c + 1) * 2], pf[c % 2][:, 0:2])
        scT_done = prev_ev

    if prep_on(2):
        mod_banks = [pf[2], pf[3], pf[4], pf[5]]
        for ng in range(3):
            for c in range(8):
                si, sn, ld = ring_load("sync",
                                   wmod[c * 128:(c + 1) * 128, ng * 2048:(ng + 1) * 2048],
                                   cols=2048, f32view=True)
                bb.wait("tensor", sn, ld)
                bb.wait("tensor", cV, scT_done)
                for n in range(4):
                    pm = bb.mm(mod_banks[n][0:2, :], scT[:, c * 2:(c + 1) * 2],
                               ringf[si][:, n * 512:(n + 1) * 512],
                               start=(c == 0), stop=(c == 7), inc=(n == 3))
                ring_release(si, cP, pm)
            bb.wait("vector", cP, bb.sem["cP"].n)
            for n in range(4):
                ev = bb.v("tensor_copy",
                          modraw[:, ng * 2048 + n * 512: ng * 2048 + (n + 1) * 512],
                          mod_banks[n][0:2, :])
            bb.wait("tensor", cV, ev)
        mod_rows_done = bb.sem["cV"].n
    tap("scT", scT, "cV", scT_done)
    tap("mod", modraw, "cV", mod_rows_done)

    if prep_on(3):
        sa_r = modraw[:, 0:1024]
        ca_r = modraw[:, 1024:2048]
        ga_r = modraw[:, 2048:3072]
        cb_r = modraw[:, 4096:5120]
        sb_r = modraw[:, 3072:4096]
        gb_r = modraw[:, 5120:6144]
        bb.wait("scalar", cV, mod_rows_done)
        for item_ap in (ga_r, gb_r):
            bb.act(prow[:, 0:1024], item_ap, AF.Sign)
            sgn = bb.act(prow[:, 1024:2048], item_ap, AF.Abs)
            bb.wait("vector", cS, sgn)
            bb.v("tensor_scalar", item_ap, prow[:, 1024:2048], 0.25, None, op0=ALU.max)
            cl_done = bb.v("tensor_tensor", item_ap, item_ap, prow[:, 0:1024], op=ALU.mult)
            bb.wait("scalar", cV, cl_done)
        bb.v("tensor_scalar", prow[:, 0:1024], ca_r, 1.0, None, op0=ALU.add)
        pr_done = bb.v("tensor_scalar", prow[:, 1024:2048], sa_r, -1.0, None,
                       op0=ALU.mult)
        bb.wait("gpsimd", cV, pr_done)
        for sq in range(2):
            modrows_dma = bb.dma("gpsimd", "dMr", modrows[:, sq * 2048:(sq + 1) * 2048],
                                 prow[sq:sq + 1, 0:2048])
        modrows_done = modrows_dma

        prev_ev = 0
        for item_ap, base in ((ga_r, 0), (gb_r, 16)):
            for c in range(8):
                bb.wait("tensor", cV, cl_done if prev_ev == 0 else prev_ev)
                pe_t = bb.trans(pf[0][:, 0:2], item_ap[:, c * 128:(c + 1) * 128],
                                identf[0:2, 0:2], inc=True)
                bb.wait("vector", cP, pe_t)
                prev_ev = bb.v("tensor_copy", gcols[:, base + c * 2:base + c * 2 + 2],
                               pf[0][:, 0:2])
        gcols_done = prev_ev

    if prep_on(4):
        def bcast_fill(dst, src_row_bf, width, rows_ready_sem, rows_ready):
            last = 0
            for n0 in range(0, width, 512):
                w = min(512, width - n0)
                bb.wait("tensor", rows_ready_sem, rows_ready)
                if last:
                    bb.wait("tensor", cV, last)
                pm = bb.mm(pf[1][:, 0:w], ones1, src_row_bf[:, n0:n0 + w],
                           start=True, stop=True, inc=True)
                bb.wait("vector", cP, pm)
                last = bb.v("tensor_copy", dst[:, n0:n0 + w], pf[1][:, 0:w])
            return last

        bb.wait("vector", "dMr", modrows_done)
        ocb_done = bb.v("tensor_scalar", prow[:, 0:1024], cb_r, 1.0, None, op0=ALU.add)
        for s in range(2):
            bb.wait("gpsimd", cV, ocb_done)
            r1 = bb.dma("gpsimd", "dStage", stage1[:, 0:1024], prow[s:s + 1, 0:1024])
            lastb = bcast_fill(cbB[s], stage1[:, 0:1024], 1024, "dStage", r1)
            bb.wait("gpsimd", cV, lastb)
            r2 = bb.dma("gpsimd", "dStage", stage1[:, 0:1024], sb_r[s:s + 1, :])
            lastb = bcast_fill(sbB[s], stage1[:, 0:1024], 1024, "dStage", r2)
            bb.wait("gpsimd", cV, lastb)
        for dst, src_h, width in ((g1B, ln1g_h, 1024), (b1B, ln1b_h, 1024),
                                  (g2B, ln2g_h, 4096), (b2B, ln2b_h, 4096)):
            ld = bb.dma("gpsimd", "dStage", stage1[:, 0:width], src_h[None, :])
            lastb = bcast_fill(dst, stage1[:, 0:width], width, "dStage", ld)
            bb.wait("gpsimd", cV, lastb)

    if prep_on(5):
        def cast_roundtrip(src_ap, dst_ap):
            si, sn, ld = ring_load("gpsimd", cm_src(src_ap))
            bb.wait("sync", sn, ld)
            st = bb.dma("sync", "dSt", cm_src(dst_ap), ring[si])
            fv = bb.fence("sync", "dSt", st)
            ring_release(si, "stm", fv)
            return st

        cast_roundtrip(wq_h, s_wq)
        cast_roundtrip(wk_h, s_wk)
        cast_roundtrip(wv_h[:, 0:512], s_wv[:, 0:512])
        cast_roundtrip(wv_h[:, 512:1024], s_wv[:, 512:1024])
        cast_roundtrip(wg_h[:, 0:512], s_wg[:, 0:512])
        cast_roundtrip(wg_h[:, 512:1024], s_wg[:, 512:1024])
        for half in range(2):
            si, sn, ld = ring_load("gpsimd",
                               cm_src(wo_h[:, half * 512:(half + 1) * 512]))
            bb.wait("vector", sn, ld)
            for c in range(8):
                sc_done = bb.v("tensor_scalar", ring[si][:, c * 512:(c + 1) * 512],
                               ring[si][:, c * 512:(c + 1) * 512],
                               gnwc[:, c % 2:c % 2 + 1], None, op0=ALU.mult)
            bb.wait("sync", cV, sc_done)
            st = bb.dma("sync", "dSt", cm_src(s_wo[:, half * 512:(half + 1) * 512]),
                        ring[si])
            fv = bb.fence("sync", "dSt", st)
            ring_release(si, "stm", fv)

    if prep_on(6):
        def quant(w_hbm, rows, cols, s_q, widx):
            if not hasattr(quant, "store_mark"):
                quant.store_mark = [0, 0, 0, 0]
            tiles = []
            if cols == 1024:
                for li in range(rows // 256):
                    tiles.append((cm_src(w_hbm[li * 256:(li + 1) * 256, :]),
                                  cm_src(s_q[li * 256:(li + 1) * 256, :])))
            else:
                for rc in range(rows // 128):
                    for hf in range(2):
                        tiles.append((w_hbm[rc * 128:(rc + 1) * 128,
                                            hf * 2048:(hf + 1) * 2048],
                                      s_q[rc * 128:(rc + 1) * 128,
                                          hf * 2048:(hf + 1) * 2048]))
            for li, (src, _d) in enumerate(tiles):
                si, sn, ld = ring_load("sync" if li % 2 == 0 else "scalar", src,
                                       cols=2048, f32view=True)
                bb.wait("vector", sn, ld)
                rd = bb.v("tensor_reduce", macc[:, li:li + 1], ringf[si][:, 0:2048],
                          axis=AX, op=ALU.add, apply_absolute_value=True)
                ring_release(si, cV, rd)
            bb.wait("vector", cV, rd)
            red_done = bb.v("tensor_reduce", macc[:, 33:34], macc[:, 0:16],
                            axis=AX, op=ALU.add)
            bb.wait("tensor", cV, red_done)
            tot_pe = bb.trans(pf[0][0:1, 0:128], macc[:, 33:34], identf, inc=True)
            bb.wait("vector", cP, tot_pe)
            m_ = bb.v("tensor_reduce", macc[0:1, 34:35], pf[0][0:1, 0:128], axis=AX,
                      op=ALU.add)
            bb.wait("vector", cV, m_)
            m_ = bb.v("tensor_scalar", macc[0:1, 35:36], macc[0:1, 34:35],
                 1.0 / (rows * cols), 1e-5, op0=ALU.mult, op1=ALU.max)
            bb.wait("vector", cV, m_)
            sc_done = bb.v("reciprocal", macc[0:1, 36:37], macc[0:1, 35:36])
            bb.wait("tensor", cV, sc_done)
            bb.mm(pf[1][:, 0:1], onessc[0:1, :], macc[0:1, 36:37], start=True, stop=True)
            bc_pe = bb.mm(pf[1][:, 1:2], onessc[0:1, :], macc[0:1, 35:36],
                          start=True, stop=True, inc=True)
            bb.wait("vector", cP, bc_pe)
            bb.v("tensor_copy", wscale[:, widx:widx + 1], pf[1][:, 0:1])
            bcast_done = bb.v("tensor_copy", wmean[:, widx:widx + 1], pf[1][:, 1:2])
            last_store = 0
            for li, (src, dst) in enumerate(tiles):
                si, sn, ld = ring_load("sync" if li % 2 == 0 else "gpsimd", src,
                                       cols=2048, f32view=True)
                bb.wait("vector", "sst", quant.store_mark[si])
                bb.wait("scalar", sn, ld)
                bb.wait("scalar", cV, bcast_done)
                bb.act(ringf[si][:, 0:2048], ringf[si][:, 0:2048], AF.Identity,
                       bias=actc[:, 0:1], scale=wscale[:, widx:widx + 1])
                acc_fence()
                bb.wait("vector", cS, bb.sem["cS"].n)
                bb.v("tensor_scalar", actA[:, si * 2048:si * 2048 + 2048],
                     ringf[si][:, 0:2048], CRND, None, op0=ALU.subtract)
                qd = bb.v("tensor_scalar", actA[:, si * 2048:si * 2048 + 2048],
                          actA[:, si * 2048:si * 2048 + 2048],
                          1.0, -1.0, op0=ALU.min, op1=ALU.max)
                bb.wait("scalar", cV, qd)
                last_store = bb.dma("scalar", "sst", dst,
                                    actA[:, si * 2048:si * 2048 + 2048])
                quant.store_mark[si] = last_store
                ring_release(si, cV, qd)
            return last_store

    if prep_on(7):
        q1_done = quant(w1_h, 4096, 1024, s_w1q, 0)
        q2_done = quant(w2_h, 1024, 4096, s_w2q, 1)

    if prep_on(8):
        bb.wait("scalar", "sst", q1_done)
        for c in range(8):
            si = slot_idx[0] % 4
            slot_idx[0] += 1
            rsem, rval = slot_rel[si]
            bb.wait("scalar", rsem, rval)
            ld = bb.dmat("scalar", f"rs{si}", ring[si], s_w1q[:, c * 128:(c + 1) * 128])
            bb.wait("sync", f"rs{si}", ld)
            st = bb.dma("sync", f"rs{si}", s_w1qT[c * 128:(c + 1) * 128, :], ring[si])
            slot_rel[si] = (f"rs{si}", st)
        bb.wait("scalar", "sst", q2_done)
        for hc in range(32):
            si = slot_idx[0] % 4
            slot_idx[0] += 1
            rsem, rval = slot_rel[si]
            bb.wait("scalar", rsem, rval)
            ld = bb.dmat("scalar", f"rs{si}", ring[si][:, 0:1024],
                         s_w2q[:, hc * 128:(hc + 1) * 128])
            bb.wait("sync", f"rs{si}", ld)
            st = bb.dma("sync", f"rs{si}", s_w2qT[hc * 128:(hc + 1) * 128, :],
                        ring[si][:, 0:1024])
            slot_rel[si] = (f"rs{si}", st)

    tap("w1q0", s_w1q[0:128, :], "stm", bb.sem["stm"].n, bf=True)
    tap("w1qT0", s_w1qT[0:128, :], "stm", bb.sem["stm"].n, bf=True)
    bb.wait("sync", "sst", bb.sem["sst"].n)
    for _si in range(4):
        bb.wait("sync", f"rs{_si}", bb.sem[f"rs{_si}"].n)
    all_st = bb.fence("sync", "dSt", bb.sem["dSt"].n)
    for _si in range(4):
        if slot_rel[_si][0] == "dSt":
            slot_rel[_si] = ("stm", all_st)

    def slice_src(idx):
        if idx == 0:
            return cm_src(s_wq)
        if idx == 1:
            return cm_src(s_wk)
        if idx in (2, 3):
            return cm_src(s_wv[:, (idx - 2) * 512:(idx - 1) * 512])
        if idx in (4, 5):
            return cm_src(s_wg[:, (idx - 4) * 512:(idx - 3) * 512])
        if idx in (6, 7):
            return cm_src(s_wo[:, (idx - 6) * 512:(idx - 5) * 512])
        if 8 <= idx < 16:
            nsl = idx - 8
            return cm_src(s_w1qT[:, nsl * 512:(nsl + 1) * 512])
        ec = (idx - 16) % 8
        return cm_src(s_w2qT[:, ec * 128:(ec + 1) * 128])

    def wslice(idx):
        si, sn, ld = ring_load("sync", slice_src(idx))
        bb.wait("tensor", sn, ld)
        return si

    def wslice_end(si):
        ring_release(si, cP, bb.sem["cP"].n)

    out_store_done = {}
    xb_loaded = {}
    xf_loaded = {}
    h_done_prev = [0]

    bb.dma("gpsimd", "sxb0", x_b[0], xT[:, 0:TW].rearrange("(c p) t -> p c t", p=128))
    xb_loaded[0] = ("sxb0", bb.sem["sxb0"].n)
    bb.dma("scalar", "sxf", x_f, xT[:, 0:TW].rearrange("(c p) t -> p c t", p=128))
    xf_loaded[0] = bb.sem["sxf"].n

    for t in range(min(NT, DBG_TILES)):
        seq = t // 4
        col0 = t * TW
        xb = x_b[0]

        if t % 4 == 0:
            bb.v("memset", S_f, 0.0)
            bb.v("memset", S_b, 0.0)
        s_bf_ready = bb.sem["cV"].n

        if DBG_PHASE < 1:
            continue
        bb.wait("scalar", xb_loaded[t][0], xb_loaded[t][1])
        bb.wait("tensor", xb_loaded[t][0], xb_loaded[t][1])
        ssq_pe = {}
        for c in range(8):
            if c >= 2:
                bb.wait("scalar", cP, ssq_pe[c - 2])
            sq = bb.act(scrAb[:, (c % 2) * 512:(c % 2) * 512 + 512],
                        xb[:, c * 512:(c + 1) * 512], AF.Square)
            bb.wait("tensor", cS, sq)
            bb.mm(pf[0][0:1, :], triu[:, 127:128], xb[:, c * 512:(c + 1) * 512],
                  start=(c == 0), stop=(c == 7))
            ssq_pe[c] = bb.mm(pf[1][0:1, :], triu[:, 127:128],
                              scrAb[:, (c % 2) * 512:(c % 2) * 512 + 512],
                              start=(c == 0), stop=(c == 7), inc=True)
        stats_done = bb.sem["cP"].n
        bb.wait("vector", cP, stats_done)
        mu_r = rowsf[:, 0:512]
        ve_r = rowsf[:, 512:1024]
        t1_r = rowsf[:, 1024:1536]
        t2_r = rowsf[:, 1536:2048]
        bb.v("tensor_scalar", mu_r, pf[0][0:1, :], 1.0 / D, None, op0=ALU.mult)
        bb.v("tensor_scalar", ve_r, pf[1][0:1, :], 1.0 / D, None, op0=ALU.mult)
        bb.v("tensor_tensor", t1_r, mu_r, mu_r, op=ALU.mult)
        bb.v("tensor_tensor", ve_r, ve_r, t1_r, op=ALU.subtract)
        magic_rsqrt(bb, ve_r, ve_r, t1_r, t2_r, EPS)
        bb.v("tensor_tensor", t1_r, mu_r, ve_r, op=ALU.mult)
        bb.v("tensor_copy", rowsb[:, 0:512], ve_r)
        rows_bf_done = bb.v("tensor_copy", rowsb[:, 512:1024], t1_r)

        ocar = modrows[:, seq * 2048:seq * 2048 + 1024]
        nsar = modrows[:, seq * 2048 + 1024:seq * 2048 + 2048]
        bb.wait("tensor", "dMr", modrows_done)
        lnapply = {}
        for c in range(8):
            bA = pf[2 + 2 * (c % 2)]
            bB2 = pf[3 + 2 * (c % 2)]
            bb.wait("tensor", cV, rows_bf_done)
            if c >= 2:
                bb.wait("tensor", cV, lnapply[c - 2])
            bb.mm(bA, ocar[:, c * 128:(c + 1) * 128], rowsb[:, 0:512],
                  start=True, stop=True)
            bb.mm(bB2, ocar[:, c * 128:(c + 1) * 128], rowsb[:, 512:1024],
                  start=True, stop=False)
            pe_d = bb.mm(bB2, nsar[:, c * 128:(c + 1) * 128], onesr,
                         start=False, stop=True, inc=True)
            bb.wait("vector", cP, pe_d)
            bb.v("tensor_tensor", h_b[:, c * 512:(c + 1) * 512],
                 xb[:, c * 512:(c + 1) * 512], bA, op=ALU.mult)
            lnapply[c] = bb.v("scalar_tensor_tensor", h_b[:, c * 512:(c + 1) * 512],
                              bB2, -1.0, h_b[:, c * 512:(c + 1) * 512],
                              op0=ALU.mult, op1=ALU.add)
        h_done = bb.sem["cV"].n
        h_done_prev[0] = h_done
        if t + 1 < NT:
            bb.wait("gpsimd", cV, h_done)
            bb.dma("gpsimd", "sxb0", x_b[0],
                   xT[:, (t + 1) * TW:(t + 2) * TW].rearrange("(c p) t -> p c t", p=128))
            xb_loaded[t + 1] = ("sxb0", bb.sem["sxb0"].n)
        if t == 0:
            tap("xb", xb, "cV", h_done, bf=True)
            tap("h", h_b, "cV", h_done, bf=True)
            tap("rows", rowsb, "cV", h_done, bf=True)

        if DBG_PHASE < 2:
            continue
        evac_tick = [0]

        def evac(dst, src_ps, pewait):
            k = evac_tick[0]
            evac_tick[0] += 1
            if k % 2 == 0:
                bb.wait("vector", cP, pewait)
                return ("cV", bb.v("tensor_copy", dst, src_ps))
            bb.wait("scalar", cP, pewait)
            return ("cS", bb.scopy(dst, src_ps))

        def pe_wait(mark):
            bb.wait("tensor", mark[0], mark[1])

        bb.wait("tensor", cV, h_done)
        marks = []
        for which, dstbuf, slc in ((0, qg, 0), (1, ki, 1)):
            si = wslice(slc)
            for m in range(4):
                bank = pf[(which * 2) + m % 2]
                if len(marks) >= 2:
                    pe_wait(marks[-2])
                for c in range(8):
                    pe_d = bb.mm(bank,
                                 ring[si][:, c * 512 + m * 128: c * 512 + (m + 1) * 128],
                                 h_b[:, c * 512:(c + 1) * 512],
                                 start=(c == 0), stop=(c == 7), inc=(c == 7))
                marks.append(evac(dstbuf[:, m * 512:(m + 1) * 512], bank, pe_d))
            wslice_end(si)
        bb.wait("tensor", "dGc", gk_loaded)
        for c in range(8):
            pe_d = bb.mm(pf[4][0:16, :], wgk1s[:, c * 16:(c + 1) * 16],
                         h_b[:, c * 512:(c + 1) * 512],
                         start=(c == 0), stop=(c == 7), inc=(c == 7))
        zmark = evac(atb[0:16, 0:512], pf[4][0:16, :], pe_d)
        pe_wait(zmark)
        bb.v("tensor_tensor_scan", colB[:, 12:14], onessc[:, 0:2], onessc[:, 0:2],
             0.0, op0=ALU.mult, op1=ALU.add, inc=False)
        lgm = {}
        for m in range(4):
            bank = pf[5] if m % 2 == 0 else pf[4]
            if m >= 2:
                bb.wait("tensor", cV, lgm[m - 2])
            pe_d = bb.mm(bank, wgk2s[:, m * 128:(m + 1) * 128], atb[0:16, 0:512],
                         start=True, stop=True, inc=True)
            bb.wait("scalar", cP, pe_d)
            if m >= 1:
                bb.wait("scalar", cV, lgm[m - 1])
            bb.act(scrA, bank, AF.Exp, bias=nbgk2[:, m:m + 1], scale=actc[:, 3:4])
            lg = bb.act(scrB, scrA, AF.Ln, bias=actc[:, 1:2], scale=1.0)
            if t == 0 and m == 0:
                tap("scrA", scrA, "cS", lg)
                tap("scrB", scrB, "cS", lg)
            bb.wait("vector", cS, lg)
            for sg in range(4):
                bb.v("tensor_tensor_scan",
                     Gp[:, m * 512 + sg * 128: m * 512 + (sg + 1) * 128],
                     onessc, scrB[:, sg * 128:(sg + 1) * 128], 0.0,
                     op0=ALU.mult, op1=ALU.add, inc=(sg == 3))
            lgm[m] = bb.sem["cV"].n
        gp_done = bb.sem["cV"].n
        if t == 0:
            tap("gp", Gp, "cV", gp_done)
            tap("qkraw", qg, "cV", gp_done, bf=True)
            tap("vtm", v_tm, "cV", gp_done, bf=True)
        vmarks = []
        for half in range(2):
            si = wslice(2 + half)
            for s in range(NSUB):
                bank = pf[s % 2]
                if len(vmarks) >= 2:
                    pe_wait(vmarks[-2])
                for c in range(8):
                    pe_d = bb.mm(bank,
                                 h_b[:, c * 512 + s * 128: c * 512 + (s + 1) * 128],
                                 ring[si][:, c * 512:(c + 1) * 512],
                                 start=(c == 0), stop=(c == 7), inc=(c == 7))
                vmarks.append(evac(v_tm[:, s * 1024 + half * 512:
                                        s * 1024 + (half + 1) * 512], bank, pe_d))
            wslice_end(si)
        gmarks = []
        for half in range(2):
            si = wslice(4 + half)
            for m in range(4):
                mc = half * 4 + m
                bank = pf[2 + mc % 2]
                if len(gmarks) >= 2:
                    pe_wait(gmarks[-2])
                for c in range(8):
                    pe_d = bb.mm(bank,
                                 ring[si][:, c * 512 + m * 128: c * 512 + (m + 1) * 128],
                                 h_b[:, c * 512:(c + 1) * 512],
                                 start=(c == 0), stop=(c == 7), inc=(c == 7))
                gmarks.append(evac(gsig[:, mc * 512:(mc + 1) * 512], bank, pe_d))
            wslice_end(si)

        if DBG_PHASE < 3:
            continue
        bb.wait("scalar", cV, gp_done)
        qgki = 0
        for m in range(4):
            eq = bb.act(scrA, Gp[:, m * 512:(m + 1) * 512], AF.Exp,
                        scale=actc[:, 4:5], bias=actc[:, 2:3])
            bb.wait("vector", cS, eq)
            bb.v("tensor_tensor", qg[:, m * 512:(m + 1) * 512],
                 qg[:, m * 512:(m + 1) * 512], scrA, op=ALU.mult)
            ek = bb.act(scrB, Gp[:, m * 512:(m + 1) * 512], AF.Exp,
                        scale=actc[:, 5:6], bias=0.0)
            bb.wait("vector", cS, ek)
            qgki = bb.v("tensor_tensor", ki[:, m * 512:(m + 1) * 512],
                        ki[:, m * 512:(m + 1) * 512], scrB, op=ALU.mult)
            bb.wait("scalar", cV, qgki)
        glp = bass.AP(Gp.tensor, Gp.offset + 127,
                      [[2048, 128], [128, 4], [512, 4]])
        glo = bass.AP(glall.tensor, glall.offset, [[16, 128], [4, 4], [1, 4]])
        pk = bb.v("tensor_copy", glo, glp)
        bb.wait("scalar", cV, pk)
        bb.act(glall[:, 0:16], glall[:, 0:16], AF.Exp, scale=actc[:, 4:5], bias=0.0)
        gl_done = bb.sem["cS"].n

        o_pe_prev = 0
        for s in range(NSUB):
            bb.wait("vector", cS, gl_done)
            if s >= 1:
                bb.wait("vector", cP, o_pe_prev)
            for hh in range(H):
                kg_d = bb.v("tensor_scalar", atb[:, hh * 128:(hh + 1) * 128],
                            ki[:, hh * 512 + s * 128: hh * 512 + (s + 1) * 128],
                            glall[:, s * 4 + hh:s * 4 + hh + 1], None, op0=ALU.mult,
                            inc=(hh == H - 1))
            bb.wait("tensor", cV, kg_d)
            for hh in range(H):
                kgt_pe = bb.trans(pb[0][:, hh * 128:(hh + 1) * 128],
                                  atb[:, hh * 128:(hh + 1) * 128], identb,
                                  inc=(hh == H - 1))
            bb.wait("vector", cP, kgt_pe)
            kgtm_d = bb.v("tensor_copy", kgtm, pb[0][:, 0:512])

            atm = {}
            for hh in range(H):
                bank = pf[2 + hh % 2]
                if hh >= 2:
                    bb.wait("tensor", cV, atm[hh - 2])
                at_pe = bb.mm(bank[:, 0:128],
                              ki[:, hh * 512 + s * 128: hh * 512 + (s + 1) * 128],
                              qg[:, hh * 512 + s * 128: hh * 512 + (s + 1) * 128],
                              start=True, stop=True, inc=True)
                bb.wait("vector", cP, at_pe)
                atm[hh] = bb.v("tensor_tensor", atb[:, hh * 128:(hh + 1) * 128],
                               bank[:, 0:128], triu, op=ALU.mult)
            at_done = bb.sem["cV"].n

            bb.wait("tensor", cV, at_done)
            for hh in range(H):
                bank = pf[hh // 2]
                off = (hh % 2) * 256
                bb.mm(bank[:, off:off + 256], atb[:, hh * 128:(hh + 1) * 128],
                      v_tm[:, s * 1024 + hh * 256: s * 1024 + (hh + 1) * 256],
                      start=True, stop=False)
                o_pe = bb.mm(bank[:, off:off + 256],
                             qg[:, hh * 512 + s * 128: hh * 512 + (s + 1) * 128],
                             S_b[:, hh * 256:(hh + 1) * 256], start=False, stop=True,
                             inc=(hh == H - 1))
            o_pe_prev = o_pe
            bb.wait("tensor", cV, kgtm_d)
            for hh in range(H):
                bank = pf[4 + hh // 2]
                off = (hh % 2) * 256
                kv_pe = bb.mm(bank[:, off:off + 256], kgtm[:, hh * 128:(hh + 1) * 128],
                              v_tm[:, s * 1024 + hh * 256: s * 1024 + (hh + 1) * 256],
                              start=True, stop=True, inc=(hh == H - 1))

            bb.wait("scalar", cP, o_pe)
            for hh in range(H):
                bank = pf[hh // 2]
                off = (hh % 2) * 256
                gn = bb.act(scrAb[:, 0:256], bank[:, off:off + 256], AF.Square,
                            accum=ssqo[:, s * 4 + hh:s * 4 + hh + 1])
            gn = acc_fence()
            bb.wait("vector", cS, gn)
            bb.v("tensor_scalar", colA[:, s * 4:(s + 1) * 4],
                 ssqo[:, s * 4:(s + 1) * 4], 1.0 / DVH, None, op0=ALU.mult)
            magic_rsqrt(bb, colA[:, s * 4:(s + 1) * 4], colA[:, s * 4:(s + 1) * 4],
                        colB[:, 0:4], colB[:, 4:8], EPS)
            for hh in range(H):
                bank = pf[hh // 2]
                off = (hh % 2) * 256
                on_d = bb.v("tensor_scalar", o_n[:, hh * 256:(hh + 1) * 256],
                            bank[:, off:off + 256],
                            colA[:, s * 4 + hh:s * 4 + hh + 1], None, op0=ALU.mult,
                            inc=(hh == H - 1))
            bb.wait("vector", cP, kv_pe)
            for hh in range(H):
                bank = pf[4 + hh // 2]
                off = (hh % 2) * 256
                bb.v("tensor_scalar", S_f[:, hh * 256:(hh + 1) * 256],
                     S_f[:, hh * 256:(hh + 1) * 256],
                     glall[:, s * 4 + hh:s * 4 + hh + 1], None, op0=ALU.mult)
                bb.v("tensor_tensor", S_f[:, hh * 256:(hh + 1) * 256],
                     S_f[:, hh * 256:(hh + 1) * 256], bank[:, off:off + 256],
                     op=ALU.add)
            s_bf_ready = bb.v("tensor_copy", S_b, S_f)

            if s == 0:
                for c in range(8):
                    pe_wait_mark = gmarks[c]
                    if pe_wait_mark[0] == "cV":
                        bb.wait("scalar", cV, pe_wait_mark[1])
                    gs = bb.act(gsig[:, c * 512:(c + 1) * 512],
                                gsig[:, c * 512:(c + 1) * 512], AF.Silu)
                gsil_done = gs
                bb.wait("vector", cS, gsil_done)

            bb.wait("tensor", cV, on_d)
            for cblk in range(8):
                ont_pe = bb.trans(pb[1][:, cblk * 128:(cblk + 1) * 128],
                                  o_n[:, cblk * 128:(cblk + 1) * 128], identb,
                                  inc=(cblk == 7))
            bb.wait("vector", cP, ont_pe)
            if s == 0:
                bb.wait("vector", cS, gsil_done)
            og_src = bass.AP(pb[1].tensor, pb[1].offset, [[1024, 128], [128, 8], [1, 128]])
            og_gs = bass.AP(gsig.tensor, gsig.offset + s * 128,
                            [[16384, 128], [512, 8], [1, 128]])
            og_gs2 = bass.AP(gsig.tensor, gsig.offset + s * 128,
                             [[16384, 128], [512, 8], [1, 128]])
            og_d = bb.v("tensor_tensor", og_gs, og_src, og_gs2, op=ALU.mult)
            bb.wait("tensor", cV, og_d)
        og_all = bb.sem["cV"].n

        if DBG_PHASE < 4:
            continue
        womarks = []
        bb.wait("tensor", cV, og_all)
        for half in range(2):
            si = wslice(6 + half)
            for m in range(4):
                mc = half * 4 + m
                bank = pf[mc % 2]
                if len(womarks) >= 2:
                    bb.wait("tensor", cV, womarks[-2])
                for c in range(8):
                    wo_pe = bb.mm(bank,
                                  ring[si][:, c * 512 + m * 128: c * 512 + (m + 1) * 128],
                                  gsig[:, c * 512:(c + 1) * 512],
                                  start=(c == 0), stop=(c == 7), inc=(c == 7))
                bb.wait("vector", cP, wo_pe)
                bb.wait("vector", "sxf", xf_loaded[t])
                womarks.append(bb.v("scalar_tensor_tensor",
                                    x_f[:, mc * 512:(mc + 1) * 512], bank,
                                    gcols[:, mc * 2 + seq:mc * 2 + seq + 1],
                                    x_f[:, mc * 512:(mc + 1) * 512],
                                    op0=ALU.mult, op1=ALU.add))
            wslice_end(si)
        x1_done = bb.sem["cV"].n
        if t == 0:
            tap("x1", x_f, "cV", x1_done)
            tap("og", gsig, "cV", x1_done, bf=True)
            tap("ssqo", ssqo, "cV", x1_done)
            tap("glall", glall, "cV", x1_done)
            tap("colA", colA, "cV", x1_done)
            tap("Sf", S_f, "cV", x1_done)
            tap("on", o_n, "cV", x1_done, bf=True)

        if DBG_PHASE < 5:
            continue
        evm = {}
        for s in range(NSUB):
            xc_src = bass.AP(x_f.tensor, x_f.offset + s * 128,
                             [[4096, 128], [512, 8], [1, 128]])
            xc_d = bb.v("tensor_copy", ttm4[:, s * 1024:(s + 1) * 1024], xc_src)
            bb.wait("tensor", cV, xc_d)
            for blk in range(8):
                x1t_pe = bb.trans(pb[0][:, blk * 128:(blk + 1) * 128],
                                  ttm4[:, s * 1024 + blk * 128:
                                       s * 1024 + (blk + 1) * 128],
                                  identb, inc=(blk == 7))
            bb.wait("vector", cP, x1t_pe)
            evm[s] = bb.v("tensor_copy", xtm4[:, s * 1024:(s + 1) * 1024],
                          pb[0][:, 0:1024])
        for s in range(NSUB):
            bb.wait("scalar", cV, evm[s])
            bb.act(scrAb[:, 0:1024], xtm4[:, s * 1024:(s + 1) * 1024], AF.Square,
                   accum=sumP[:, 0 + s:1 + s])
            bb.act(scrAb[:, 0:1024], xtm4[:, s * 1024:(s + 1) * 1024], AF.Copy,
                   accum=sumP[:, 4 + s:5 + s])
        ssq2 = acc_fence()
        bb.wait("vector", cS, ssq2)
        bb.v("tensor_scalar", colA[:, 0:4], sumP[:, 4:8], 1.0 / D, None,
             op0=ALU.mult)
        m_ = bb.v("tensor_scalar", colA[:, 4:8], sumP[:, 0:4], 1.0 / D, None,
                  op0=ALU.mult)
        bb.wait("vector", cV, m_)
        m_ = bb.v("tensor_tensor", colB[:, 8:12], colA[:, 0:4], colA[:, 0:4],
                  op=ALU.mult)
        bb.wait("vector", cV, m_)
        bb.v("tensor_tensor", colA[:, 4:8], colA[:, 4:8], colB[:, 8:12],
             op=ALU.subtract)
        magic_rsqrt(bb, colA[:, 4:8], colA[:, 4:8], colB[:, 8:12],
                    colB[:, 12:16], EPS)
        h2m = {}
        for s in range(NSUB):
            sl = slice(s * 1024, (s + 1) * 1024)
            bb.v("tensor_scalar", ttm4[:, sl], xtm4[:, sl], colA[:, s:s + 1],
                 colA[:, 4 + s:5 + s], op0=ALU.subtract, op1=ALU.mult)
            bb.v("tensor_tensor", ttm4[:, sl], ttm4[:, sl], cbB[seq], op=ALU.mult)
            h2m[s] = bb.v("scalar_tensor_tensor", ttm4[:, sl], ttm4[:, sl], 1.0,
                          sbB[seq], op0=ALU.mult, op1=ALU.add)
        for s in range(NSUB):
            bb.wait("scalar", cV, h2m[s])
            bb.act(scrAb[:, 0:1024], ttm4[:, s * 1024:(s + 1) * 1024], AF.Square,
                   accum=sumP[:, 8 + s:9 + s])
            bb.act(scrAb[:, 0:1024], ttm4[:, s * 1024:(s + 1) * 1024], AF.Copy,
                   accum=sumP[:, 12 + s:13 + s])
        s3 = acc_fence()
        bb.wait("vector", cS, s3)
        bb.v("tensor_scalar", colA[:, 8:12], sumP[:, 12:16], 1.0 / D, None,
             op0=ALU.mult)
        m_ = bb.v("tensor_scalar", colA[:, 12:16], sumP[:, 8:12], 1.0 / D, None,
                  op0=ALU.mult)
        bb.wait("vector", cV, m_)
        m_ = bb.v("tensor_tensor", colB[:, 8:12], colA[:, 8:12], colA[:, 8:12],
                  op=ALU.mult)
        bb.wait("vector", cV, m_)
        bb.v("tensor_tensor", colA[:, 12:16], colA[:, 12:16], colB[:, 8:12],
             op=ALU.subtract)
        magic_rsqrt(bb, colA[:, 12:16], colA[:, 12:16], colB[:, 8:12],
                    colB[:, 12:16], 1e-5)
        h3m = {}
        for s in range(NSUB):
            sl = slice(s * 1024, (s + 1) * 1024)
            bb.v("tensor_scalar", ttm4[:, sl], ttm4[:, sl], colA[:, 8 + s:9 + s],
                 colA[:, 12 + s:13 + s], op0=ALU.subtract, op1=ALU.mult)
            bb.v("tensor_tensor", ttm4[:, sl], ttm4[:, sl], g1B, op=ALU.mult)
            h3m[s] = bb.v("tensor_tensor", ttm4[:, sl], ttm4[:, sl], b1B, op=ALU.add)
        for pair in ((0, 1), (2, 3)):
            bb.wait("scalar", cV, h3m[pair[1]])
            for k, s in enumerate(pair):
                bb.act(hscr[:, k * 1024:k * 1024 + 1024],
                       ttm4[:, s * 1024:(s + 1) * 1024], AF.Abs)
            absm = acc_fence()
            bb.wait("vector", cS, absm)
            for k, s in enumerate(pair):
                bb.v("reduce_max", ssqo[:, 2 * s:2 * s + 1],
                     hscr[:, k * 1024:k * 1024 + 512], axis=AX)
                bb.v("reduce_max", ssqo[:, 2 * s + 1:2 * s + 2],
                     hscr[:, k * 1024 + 512:k * 1024 + 1024], axis=AX)
        m_ = bb.sem["cV"].n
        bb.wait("vector", cV, m_)
        for s in range(NSUB):
            mxd = bb.v("scalar_tensor_tensor", colA[:, 16 + s:17 + s],
                       ssqo[:, 2 * s:2 * s + 1], 1e-5, ssqo[:, 2 * s + 1:2 * s + 2],
                       op0=ALU.max, op1=ALU.max)
        bb.wait("scalar", cV, mxd)
        lnr = bb.act(colB[:, 8:12], colA[:, 16:20], AF.Ln)
        bb.wait("scalar", cS, lnr)
        sxp = bb.act(colA[:, 20:24], colB[:, 8:12], AF.Exp,
                     scale=actc[:, 3:4], bias=actc[:, 8:9])
        bb.wait("vector", cS, sxp)
        bb.v("tensor_scalar", colB[:, 12:16], colA[:, 16:20], wmean[:, 0:1],
             1.0 / 127.0, op0=ALU.mult, op1=ALU.mult)
        for s in range(NSUB):
            sl = slice(s * 1024, (s + 1) * 1024)
            bb.v("tensor_scalar", hscr[:, 0:1024], ttm4[:, sl], colA[:, 20 + s:21 + s],
                 CRND, op0=ALU.mult, op1=ALU.add)
            dq_d = bb.v("tensor_scalar", dqtm, hscr[:, 0:1024], CRND,
                        colB[:, 12 + s:13 + s], op0=ALU.subtract, op1=ALU.mult)
            if t == 0 and s == 0:
                tap("midcols", colA, "cV", dq_d)
                tap("h3", ttm4, "cV", dq_d, bf=True)
                tap("dq0", dqtm, "cV", dq_d, bf=True)
            bb.wait("tensor", cV, dq_d)
            for cblk in range(8):
                dqt_pe = bb.trans(pb[0][:, cblk * 128:(cblk + 1) * 128],
                                  dqtm[:, cblk * 128:(cblk + 1) * 128], identb,
                                  inc=(cblk == 7))
            bb.wait("vector", cP, dqt_pe)
            hq_src = bass.AP(pb[0].tensor, pb[0].offset, [[1024, 128], [128, 8], [1, 128]])
            hq_dst = bass.AP(h2qT.tensor, h2qT.offset + s * 128,
                             [[4096, 128], [512, 8], [1, 128]])
            hq_d = bb.v("tensor_copy", hq_dst, hq_src)
            bb.wait("tensor", cV, hq_d)
        h2qT_done = bb.sem["cV"].n
        if t == 0:
            tap("h2qT", h2qT, "cV", h2qT_done, bf=True)

        if DBG_PHASE < 6:
            continue
        gelum = {}
        for nsl in range(8):
            si = wslice(8 + nsl)
            if nsl == 0:
                bb.wait("tensor", cV, h2qT_done)
                if t == 0:
                    tap("w1slot", ring[si], f"rs{si}", bb.sem[f"rs{si}"].n, bf=True)
                    tap("b1row", b1row, None, 0, bf=True)
            for s in range(NSUB):
                bank = pf[s]
                if nsl >= 1:
                    bb.wait("tensor", cS, gelum[(nsl - 1, s)])
                bb.mm(bank, ones1, b1row[:, nsl * 512:(nsl + 1) * 512],
                      start=True, stop=False)
                for c in range(8):
                    y1_pe = bb.mm(bank,
                                  h2qT[:, c * 512 + s * 128: c * 512 + (s + 1) * 128],
                                  ring[si][:, c * 512:(c + 1) * 512],
                                  start=False, stop=(c == 7), inc=(c == 7))
                bb.wait("scalar", cP, y1_pe)
                if t == 0 and nsl == 0 and s == 0 and "y1pe" in DBG_TAPS:
                    cpm = bb.scopy(scrA[:, 0:512], bank)
                    tap("y1pe", scrA, "cS", cpm)
                gelum[(nsl, s)] = bb.act(y1g[s][:, nsl * 512:(nsl + 1) * 512], bank,
                                         AF.Gelu,
                                         accum=ygsum[:, s * 8 + nsl:s * 8 + nsl + 1])
            wslice_end(si)
            if nsl % 2 == 1:
                q4 = nsl // 2
                for s in range(NSUB):
                    bb.act(scrAb[:, 0:1024], y1g[s][:, q4 * 1024:(q4 + 1) * 1024],
                           AF.Square, accum=sumP[:, s * 4 + q4:s * 4 + q4 + 1],
                           inc=(s == 3))

        if DBG_PHASE < 7:
            continue
        if t == 0:
            if "y1g0" in DBG_TAPS:
                bb.wait("sync", "cS", gelum[(7, 0)])
            tap("y1g0", y1g[0], None, 0, bf=True)
        ttm4f = ttm4.bitcast(F32)
        bb.wait("scalar", cS, gelum[(7, 3)])
        acc_fence()
        bb.wait("vector", cS, bb.sem["cS"].n)
        for s in range(NSUB):
            bb.v("tensor_reduce", colA[:, 24 + s:25 + s],
                 ygsum[:, s * 8:(s + 1) * 8], axis=AX, op=ALU.add)
            bb.v("tensor_reduce", colA[:, 28 + s:29 + s],
                 sumP[:, s * 4:(s + 1) * 4], axis=AX, op=ALU.add)
        bb.wait("vector", cV, bb.sem["cV"].n)
        bb.v("tensor_scalar", colA[:, 24:28], colA[:, 24:28], 1.0 / 4096.0, None,
             op0=ALU.mult)
        m_ = bb.v("tensor_scalar", colA[:, 28:32], colA[:, 28:32], 1.0 / 4096.0,
                  None, op0=ALU.mult)
        bb.wait("vector", cV, m_)
        m_ = bb.v("tensor_tensor", colB[:, 8:12], colA[:, 24:28], colA[:, 24:28],
                  op=ALU.mult)
        bb.wait("vector", cV, m_)
        bb.v("tensor_tensor", colA[:, 28:32], colA[:, 28:32], colB[:, 8:12],
             op=ALU.subtract)
        magic_rsqrt(bb, colA[:, 28:32], colA[:, 28:32], colB[:, 8:12],
                    colB[:, 12:16], 1e-5)
        h4m = {}
        for s in range(NSUB):
            bb.v("tensor_scalar", y1g[s], y1g[s], colA[:, 24 + s:25 + s],
                 colA[:, 28 + s:29 + s], op0=ALU.subtract, op1=ALU.mult)
            bb.v("tensor_tensor", y1g[s], y1g[s], g2B, op=ALU.mult)
            h4m[s] = bb.v("tensor_tensor", y1g[s], y1g[s], b2B, op=ALU.add)
        for s in range(NSUB):
            bb.wait("scalar", cV, h4m[s])
            bb.act(hscr[:, 0:1024], y1g[s][:, 0:1024], AF.Abs)
            bb.act(hscr[:, 1024:2048], y1g[s][:, 1024:2048], AF.Abs)
            bb.act(ttm4f[:, 0:1024], y1g[s][:, 2048:3072], AF.Abs)
            a4 = bb.act(ttm4f[:, 1024:2048], y1g[s][:, 3072:4096], AF.Abs)
            bb.wait("vector", cS, a4)
            for q4, src in enumerate((hscr[:, 0:1024], hscr[:, 1024:2048],
                                      ttm4f[:, 0:1024], ttm4f[:, 1024:2048])):
                rm = bb.v("reduce_max", ssqo[:, s * 4 + q4:s * 4 + q4 + 1], src,
                          axis=AX)
            bb.wait("scalar", cV, rm)
        bb.wait("vector", cV, bb.sem["cV"].n)
        for s in range(NSUB):
            bb.v("scalar_tensor_tensor", colB[:, 8 + s:9 + s],
                 ssqo[:, s * 4:s * 4 + 1], 1e-5, ssqo[:, s * 4 + 1:s * 4 + 2],
                 op0=ALU.max, op1=ALU.max)
            m_ = bb.v("scalar_tensor_tensor", colB[:, 12 + s:13 + s],
                      ssqo[:, s * 4 + 2:s * 4 + 3], 1e-5,
                      ssqo[:, s * 4 + 3:s * 4 + 4], op0=ALU.max, op1=ALU.max)
        bb.wait("vector", cV, m_)
        mxd = bb.v("tensor_tensor", colA[:, 32:36], colB[:, 8:12], colB[:, 12:16],
                   op=ALU.max)
        bb.wait("scalar", cV, mxd)
        lnr = bb.act(colB[:, 8:12], colA[:, 32:36], AF.Ln)
        bb.wait("scalar", cS, lnr)
        sxp = bb.act(colA[:, 36:40], colB[:, 8:12], AF.Exp,
                     scale=actc[:, 3:4], bias=actc[:, 8:9])
        bb.wait("vector", cS, sxp)
        ks2m = bb.v("tensor_scalar", colA[:, 40:44], colA[:, 32:36], wmean[:, 1:2],
                    1.0 / 127.0, op0=ALU.mult, op1=ALU.mult)
        if t == 0:
            tap("jcols", colA, "cV", ks2m)
        w2marks = []
        for half in range(2):
            for s in (2 * half, 2 * half + 1):
                for hf in range(4):
                    bb.v("tensor_scalar", hscr[:, (hf % 2) * 1024:(hf % 2) * 1024 + 1024],
                         y1g[s][:, hf * 1024:(hf + 1) * 1024],
                         colA[:, 36 + s:37 + s], CRND, op0=ALU.mult, op1=ALU.add)
                    dq2_d = bb.v("tensor_scalar",
                                 y1g[s][:, hf * 1024:(hf + 1) * 1024],
                                 hscr[:, (hf % 2) * 1024:(hf % 2) * 1024 + 1024], CRND,
                                 colA[:, 40 + s:41 + s],
                                 op0=ALU.subtract, op1=ALU.mult)
                bb.wait("tensor", cV, dq2_d)
                for grp in range(4):
                    for blk in range(8):
                        hcix = grp * 8 + blk
                        tp = bb.trans(pb[grp % 2][:, blk * 128:(blk + 1) * 128],
                                      y1g[s][:, hcix * 128:(hcix + 1) * 128], identb,
                                      inc=(blk == 7))
                    bb.wait("vector", cP, tp)
                    yt_src = bass.AP(pb[grp % 2].tensor, pb[grp % 2].offset,
                                     [[1024, 128], [128, 8], [1, 128]])
                    yt_dst = bass.AP(y1qT.tensor,
                                     y1qT.offset + grp * 8 * 256 + (s % 2) * 128,
                                     [[8192, 128], [256, 8], [1, 128]])
                    yt_d = bb.v("tensor_copy", yt_dst, yt_src)
                    bb.wait("tensor", cV, yt_d)
            y1qT_half = bb.sem["cV"].n
            if t == 0 and half == 0:
                tap("y1qT", y1qT, "cV", y1qT_half, bf=True)

            for ec in range(8):
                si = wslice(16 + half * 8 + ec)
                if ec == 0:
                    bb.wait("tensor", cV, y1qT_half)
                bank = pf[4 + ec % 2]
                if len(w2marks) >= 2:
                    bb.wait("tensor", cV, w2marks[-2])
                bb.mm(bank[:, 0:256], b2row[:, ec * 128:(ec + 1) * 128],
                      onesr[:, 0:256], start=True, stop=False)
                for hc in range(32):
                    y2_pe = bb.mm(bank[:, 0:256],
                                  ring[si][:, hc * 128:(hc + 1) * 128],
                                  y1qT[:, hc * 256:(hc + 1) * 256],
                                  start=False, stop=(hc == 31), inc=(hc == 31))
                bb.wait("vector", cP, y2_pe)
                w2marks.append(bb.v("scalar_tensor_tensor",
                                    x_f[:, ec * 512 + half * 256:
                                        ec * 512 + half * 256 + 256],
                                    bank[:, 0:256],
                                    gcols[:, 16 + ec * 2 + seq:16 + ec * 2 + seq + 1],
                                    x_f[:, ec * 512 + half * 256:
                                        ec * 512 + half * 256 + 256],
                                    op0=ALU.mult, op1=ALU.add))
                wslice_end(si)
        x2_done = bb.sem["cV"].n

        bb.wait("scalar", cV, x2_done)
        out_store_done[t] = bb.dma("scalar", "sst",
                                   outT[:, col0:col0 + TW]
                                   .rearrange("(c p) t -> p c t", p=128), x_f)
        if t + 1 < NT:
            bb.wait("scalar", "sst", out_store_done[t])
            bb.dma("scalar", "sxf", x_f,
                   xT[:, (t + 1) * TW:(t + 2) * TW]
                   .rearrange("(c p) t -> p c t", p=128))
            xf_loaded[t + 1] = bb.sem["sxf"].n

    if out_store_done:
        bb.wait("scalar", "sst", out_store_done[max(out_store_done)])
    else:
        mk = bb.dma("scalar", "sst", outT[:, 0:TW].rearrange("(c p) t -> p c t", p=128),
                    x_f)
        bb.wait("scalar", "sst", mk)

    for _sn in ("dC", "dSt", "rs0", "rs1", "rs2", "rs3"):
        bb.wait("sync", _sn, bb.sem[_sn].n)
    for _sn in ("rg0", "rg1", "rg2", "rg3", "dGc", "dStage", "dMr", "sxb0", "sxb1"):
        bb.wait("gpsimd", _sn, bb.sem[_sn].n)
    for _sn in ("sxf", "sst"):
        bb.wait("scalar", _sn, bb.sem[_sn].n)

    with nc.Block() as block:
        @block.sync
        def _(eng):
            for fn in bb.ops["sync"]:
                fn(eng)

        @block.tensor
        def _(eng):
            for fn in bb.ops["tensor"]:
                fn(eng)

        @block.vector
        def _(eng):
            for fn in bb.ops["vector"]:
                fn(eng)

        @block.scalar
        def _(eng):
            for fn in bb.ops["scalar"]:
                fn(eng)

        @block.gpsimd
        def _(eng):
            for fn in bb.ops["gpsimd"]:
                fn(eng)

    ctx.close()
    nc._tap_offsets = tap_offsets
    return nc


def kernel(**inputs):
    if "prog" not in _CACHED:
        _CACHED["prog"] = build_program()
    nc = _CACHED["prog"]

    x = np.ascontiguousarray(np.asarray(inputs["L0_attn_x"], np.float32))
    cfull = np.ascontiguousarray(np.asarray(inputs["L0_c"], np.float32))

    shared = dict(
        wmod=np.ascontiguousarray(inputs["L0_wmod"], np.float32),
        wq=np.ascontiguousarray(inputs["L0_attn_Wq"], np.float32),
        wk=np.ascontiguousarray(inputs["L0_attn_Wk"], np.float32),
        wv=np.ascontiguousarray(inputs["L0_attn_Wv"], np.float32),
        wg=np.ascontiguousarray(inputs["L0_attn_Wg"], np.float32),
        wo=np.ascontiguousarray(inputs["L0_attn_Wo"], np.float32),
        wgk1=np.ascontiguousarray(inputs["L0_attn_Wgk1"], np.float32),
        wgk2=np.ascontiguousarray(inputs["L0_attn_Wgk2"], np.float32),
        bgk2=np.ascontiguousarray(inputs["L0_attn_bgk2"], np.float32),
        gnw=np.ascontiguousarray(inputs["L0_attn_gnw"], np.float32),
        ln1g=np.ascontiguousarray(inputs["L0_ffn_ln1_g"], np.float32),
        ln1b=np.ascontiguousarray(inputs["L0_ffn_ln1_b"], np.float32),
        w1=np.ascontiguousarray(inputs["L0_ffn_w1"], np.float32),
        b1=np.ascontiguousarray(inputs["L0_ffn_b1"], np.float32),
        ln2g=np.ascontiguousarray(inputs["L0_ffn_ln2_g"], np.float32),
        ln2b=np.ascontiguousarray(inputs["L0_ffn_ln2_b"], np.float32),
        w2=np.ascontiguousarray(inputs["L0_ffn_w2"], np.float32),
        b2=np.ascontiguousarray(inputs["L0_ffn_b2"], np.float32),
        identb=np.eye(128, dtype=BF),
        identf=np.eye(128, dtype=np.float32),
        triu=np.triu(np.ones((128, 128), np.float32)).astype(BF),
        ones1=np.ones((1, 128), BF),
        onesr=np.ones((1, 512), BF),
        onessc=np.ones((128, 128), np.float32),
        actc=np.tile(np.array([[CRND, 1.0, LNSCALE, -1.0, -1.0 / 16.0, 1.0 / 16.0,
                                1e-5, -0.5, float(np.log(127.0)), 0.5]],
                              np.float32), (128, 1)),
    )
    in_maps = []
    for core in range(8):
        xTc = np.ascontiguousarray(
            np.concatenate([x[2 * core].T, x[2 * core + 1].T], axis=1))
        m = dict(shared)
        m["xT"] = xTc
        m["c2"] = np.ascontiguousarray(cfull[2 * core:2 * core + 2])
        in_maps.append(m)

    res = run_bass_kernel_spmd(nc, in_maps, core_ids=list(range(8)))
    out = np.empty_like(x)
    for core in range(8):
        oT = res.results[core]["outT"]
        out[2 * core] = oT[:, :2048].T
        out[2 * core + 1] = oT[:, 2048:].T
    return out


# version 13
# speedup vs baseline: 1.1697x; 1.0003x over previous
import numpy as np
import ml_dtypes
import concourse.bass as bass
import concourse.mybir as mybir
from concourse.bass_isa import ReduceOp
from concourse.bass_utils import run_bass_kernel_spmd
from contextlib import ExitStack

F32 = mybir.dt.float32
BF16 = mybir.dt.bfloat16
I32 = mybir.dt.int32
AF = mybir.ActivationFunctionType
ALU = mybir.AluOpType
AX = mybir.AxisListType.X

D = 1024
TW = 512
NT = 8
SUB = 128
NSUB = 4
H, DKH, DVH = 4, 128, 256
EPS = 1e-5
CRND = float(3 * (2 ** 22))
LNSCALE = float(np.log(np.float32(np.float32(128.0) ** np.float32(-0.5))))
BF = ml_dtypes.bfloat16

_CACHED = {}
DBG_TILES = 8
DBG_PHASE = 99
DBG_PREP = 99
DBG_TAPS = ()


class Sem:
    def __init__(self, handle):
        self.h = handle
        self.n = 0


class B:
    ENG = ("sync", "tensor", "vector", "scalar", "gpsimd")

    def __init__(self, nc, ctx):
        self.nc = nc
        self.ops = {e: [] for e in self.ENG}
        self.sem = {}
        for name in ("dC", "dB2", "dGc", "dStage", "dMr", "dSt",
                     "rs0", "rs1", "rs2", "rs3", "rg0", "rg1", "rg2", "rg3",
                     "sxb0", "sxb1", "sxf", "sst", "stm",
                     "cP", "cV", "cS", "cG"):
            self.sem[name] = Sem(ctx.enter_context(nc.semaphore(name)))
        self._lastw = {}

    def wait(self, eng, sname, val):
        if val <= 0:
            return
        key = (eng, sname)
        if self._lastw.get(key, -1) >= val:
            return
        self._lastw[key] = val
        s = self.sem[sname]
        self.ops[eng].append(lambda e, h=s.h, v=val: e.wait_ge(h, v))

    def emit(self, eng, fn):
        self.ops[eng].append(fn)

    def dma(self, qeng, sname, out_ap, in_ap, **kw):
        s = self.sem[sname]

        def _f(e, o=out_ap, i=in_ap, h=s.h, kw=kw):
            e.dma_start(out=o, in_=i, **kw).then_inc(h, 16)
        self.emit(qeng, _f)
        s.n += 16
        return s.n

    def dmat(self, qeng, sname, out_ap, in_ap):
        s = self.sem[sname]

        def _f(e, o=out_ap, i=in_ap, h=s.h):
            e.dma_start_transpose(o, i).then_inc(h, 16)
        self.emit(qeng, _f)
        s.n += 16
        return s.n

    def mm(self, out, lhsT, rhs, start, stop, inc=False):
        s = self.sem["cP"]
        if inc:
            def _f(t, o=out, l=lhsT, r=rhs, a=start, b=stop, h=s.h):
                t.matmul(o, l, r, start=a, stop=b).then_inc(h, 1)
            self.emit("tensor", _f)
            s.n += 1
            return s.n

        def _g(t, o=out, l=lhsT, r=rhs, a=start, b=stop):
            t.matmul(o, l, r, start=a, stop=b)
        self.emit("tensor", _g)
        return s.n

    def trans(self, out, in_, ident, inc=False):
        s = self.sem["cP"]
        if inc:
            def _f(t, o=out, i=in_, d=ident, h=s.h):
                t.transpose(o, i, d).then_inc(h, 1)
            self.emit("tensor", _f)
            s.n += 1
            return s.n

        def _g(t, o=out, i=in_, d=ident):
            t.transpose(o, i, d)
        self.emit("tensor", _g)
        return s.n

    def act(self, out, in_, func, bias=0.0, scale=1.0, accum=None, inc=True):
        s = self.sem["cS"]

        def _f(e, o=out, i=in_, f=func, b=bias, sc=scale, a=accum, h=s.h, ic=inc):
            ins = e.activation(o, i, f, bias=b, scale=sc, accum_out=a)
            if ic:
                ins.then_inc(h, 1)
        self.emit("scalar", _f)
        if inc:
            s.n += 1
        return s.n

    def scopy(self, out, in_, inc=True):
        s = self.sem["cS"]

        def _f(e, o=out, i=in_, h=s.h, ic=inc):
            ins = e.copy(o, i)
            if ic:
                ins.then_inc(h, 1)
        self.emit("scalar", _f)
        if inc:
            s.n += 1
        return s.n

    def v(self, name, *args, inc=True, **kw):
        s = self.sem["cV"]

        def _f(e, nm=name, a=args, k=kw, h=s.h, ic=inc):
            ins = getattr(e, nm)(*a, **k)
            if ic:
                ins.then_inc(h, 1)
        self.emit("vector", _f)
        if inc:
            s.n += 1
        return s.n

    def g(self, name, *args, inc=True, **kw):
        s = self.sem["cG"]

        def _f(e, nm=name, a=args, k=kw, h=s.h, ic=inc):
            ins = getattr(e, nm)(*a, **k)
            if ic:
                ins.then_inc(h, 1)
        self.emit("gpsimd", _f)
        if inc:
            s.n += 1
        return s.n

    def fence(self, qeng, dsem_name, dval):
        self.wait(qeng, dsem_name, dval)
        st = self.sem["stm"]

        def _f(e, h=st.h):
            e.sem_inc(h, 1)
        self.emit(qeng, _f)
        st.n += 1
        return st.n

    def pe_mark(self):
        s = self.sem["cP"]

        def _f(t, h=s.h):
            t.nop().then_inc(h, 1)
        self.emit("tensor", _f)
        s.n += 1
        return s.n


def magic_rsqrt(bb, out, inp, t1, t2, eps):
    cV = bb.sem["cV"].n
    bb.wait("scalar", "cV", cV)
    actc = bb._actc
    p = inp.shape[0]
    lnr = bb.act(t1, inp, AF.Ln, bias=actc[0:p, 6:7], scale=1.0)
    bb.wait("scalar", "cS", lnr)
    r = bb.act(out, t1, AF.Exp, bias=0.0, scale=actc[0:p, 7:8])
    bb.wait("vector", "cS", r)
    return r


def build_program():
    nc = bass.Bass("TRN2", target_bir_lowering=False, detect_race_conditions=False)

    def inp(name, shape, dt=F32):
        return nc.dram_tensor(name, shape, dt, kind="ExternalInput").ap()

    xT = inp("xT", [D, NT * TW])
    c2 = inp("c2", [2, D])
    wmod = inp("wmod", [D, 6 * D])
    wq_h = inp("wq", [D, 512]); wk_h = inp("wk", [D, 512])
    wv_h = inp("wv", [D, 1024]); wg_h = inp("wg", [D, 1024]); wo_h = inp("wo", [D, 1024])
    wgk1_h = inp("wgk1", [D, 16]); wgk2_h = inp("wgk2", [16, 512])
    bgk2_h = inp("bgk2", [512]); gnw_h = inp("gnw", [256])
    ln1g_h = inp("ln1g", [D]); ln1b_h = inp("ln1b", [D])
    w1_h = inp("w1", [4096, 1024]); b1_h = inp("b1", [4096])
    ln2g_h = inp("ln2g", [4096]); ln2b_h = inp("ln2b", [4096])
    w2_h = inp("w2", [1024, 4096]); b2_h = inp("b2", [1024])
    identb_h = inp("identb", [128, 128], BF16)
    identf_h = inp("identf", [128, 128], F32)
    triu_h = inp("triu", [128, 128], BF16)
    ones1_h = inp("ones1", [1, 128], BF16)
    onesr_h = inp("onesr", [1, 512], BF16)
    onessc_h = inp("onessc", [128, 128], F32)
    actc_h = inp("actc", [128, 10], F32)

    outT = nc.dram_tensor("outT", [D, NT * TW], F32, kind="ExternalOutput").ap()
    dbgf = nc.dram_tensor("dbgf", [128, 8192], F32, kind="ExternalOutput").ap()
    dbgb = nc.dram_tensor("dbgb", [128, 16384], BF16, kind="ExternalOutput").ap()

    s_wq = nc.dram_tensor("s_wq", [D, 512], BF16).ap()
    s_wk = nc.dram_tensor("s_wk", [D, 512], BF16).ap()
    s_wv = nc.dram_tensor("s_wv", [D, 1024], BF16).ap()
    s_wg = nc.dram_tensor("s_wg", [D, 1024], BF16).ap()
    s_wo = nc.dram_tensor("s_wo", [D, 1024], BF16).ap()
    s_w1q = nc.dram_tensor("s_w1q", [4096, 1024], BF16).ap()
    s_w1qT = nc.dram_tensor("s_w1qT", [1024, 4096], BF16).ap()
    s_w2q = nc.dram_tensor("s_w2q", [1024, 4096], BF16).ap()
    s_w2qT = nc.dram_tensor("s_w2qT", [4096, 1024], BF16).ap()

    ctx = ExitStack()
    e = ctx.enter_context

    def sb(name, shape, dt):
        return e(nc.sbuf_tensor("sb_" + name, shape, dt)).ap()

    identb = sb("identb", [128, 128], BF16)
    identf = sb("identf", [128, 128], F32)
    triu = sb("triu", [128, 128], BF16)
    ones1 = sb("ones1", [1, 128], BF16)
    onesr = sb("onesr", [1, 512], BF16)
    onessc = sb("onessc", [128, 128], F32)
    actc = sb("actc", [128, 10], F32)
    wgk1s = sb("wgk1s", [128, 128], BF16)
    wgk2s = sb("wgk2s", [16, 512], BF16)
    nbgk2 = sb("nbgk2", [128, 4], F32)
    gnwc = sb("gnwc", [128, 2], F32)
    b1row = sb("b1row", [1, 4096], BF16)
    b2row = sb("b2row", [1, 1024], BF16)
    g1B = sb("g1B", [128, 1024], BF16)
    b1B = sb("b1B", [128, 1024], BF16)
    g2B = sb("g2B", [128, 4096], BF16)
    b2B = sb("b2B", [128, 4096], BF16)
    cbB = [sb(f"cbB{s}", [128, 1024], BF16) for s in range(2)]
    sbB = [sb(f"sbB{s}", [128, 1024], BF16) for s in range(2)]
    modrows = sb("modrows", [1, 4096], BF16)
    gcols = sb("gcols", [128, 32], F32)
    rowsb = sb("rowsb", [1, 1024], BF16)
    wmean = sb("wmean", [128, 2], F32)
    wscale = sb("wscale", [128, 2], F32)
    macc = sb("macc", [128, 40], F32)
    colA = sb("colA", [128, 48], F32)
    colB = sb("colB", [128, 16], F32)
    sumP = sb("sumP", [128, 16], F32)
    ygsum = sb("ygsum", [128, 32], F32)
    glall = sb("glall", [128, 16], F32)
    ssqo = sb("ssqo", [128, 16], F32)
    scT = sb("scT", [128, 16], F32)

    ring = [sb(f"ring{i}", [128, 4096], BF16) for i in range(4)]
    ringf = [r.bitcast(F32) for r in ring]

    def cm_src(ap2d):
        return ap2d.rearrange("(c p) j -> p c j", p=128)

    def cm_dst(sb_ap, C):
        return sb_ap.rearrange("p (c j) -> p c j", c=C)

    x_f = sb("x_f", [128, 4096], F32)
    x_b = [sb("x_b0", [128, 4096], BF16)]
    h_b = sb("h_b", [128, 4096], BF16)
    hscr = h_b.bitcast(F32)
    actA = sb("actA", [128, 16384], BF16)
    scrA = sb("scrA", [128, 512], F32)
    scrB = sb("scrB", [128, 512], F32)
    scrAb = scrA.bitcast(BF16)
    kgtm = sb("kgtm", [128, 512], BF16)
    atb = sb("atb", [128, 512], BF16)
    o_n = sb("o_n", [128, 1024], BF16)
    S_f = sb("S_f", [128, 1024], F32)
    S_b = sb("S_b", [128, 1024], BF16)
    ttm4 = sb("ttm4", [128, 4096], BF16)
    dqtm = sb("dqtm", [128, 1024], BF16)
    h2qT = sb("h2qT", [128, 4096], BF16)
    y1qT = sb("y1qT", [128, 8192], BF16)
    stage1 = y1qT[0:1, 0:4096]
    xtm4 = y1qT[:, 0:4096]
    rowsf = y1qT.bitcast(F32)[0:1, 2048:4096]

    qg = actA[:, 0:2048]
    ki = actA[:, 2048:4096]
    GpT = sb("GpT", [128, 2048], F32)
    Gp = GpT[:, :]
    v_tm = actA[:, 8192:12288]
    gsig = actA[:, 12288:16384]
    y1g = [actA[:, s * 4096:(s + 1) * 4096] for s in range(NSUB)]
    modraw = actA[0:2, 0:12288].bitcast(F32)
    prow = actA[0:2, 12288:16384].bitcast(F32)

    pf = [e(nc.psum_tensor(f"pf{i}", [128, 512], F32)).ap() for i in range(6)]
    pb = [e(nc.psum_tensor(f"pbb{i}", [128, 1024], BF16)).ap() for i in range(2)]

    bb = B(nc, ctx)
    bb._actc = actc
    cP, cV, cS, cG = "cP", "cV", "cS", "cG"

    def max_tree(out_col, buf_bf, w):
        while w > 1:
            h = w // 2
            bb.v("tensor_tensor", buf_bf[:, 0:h], buf_bf[:, 0:h], buf_bf[:, h:w],
                 op=ALU.max, inc=False)
            w = h
        return bb.v("tensor_copy", out_col, buf_bf[:, 0:1])

    def add_fold(out_col, buf_f32, w):
        while w > 1:
            h = w // 2
            bb.v("tensor_tensor", buf_f32[:, 0:h], buf_f32[:, 0:h], buf_f32[:, h:w],
                 op=ALU.add, inc=False)
            w = h
        return bb.v("tensor_copy", out_col, buf_f32[:, 0:1])

    def acc_fence():
        return bb.scopy(macc[0:1, 39:40], actc[0:1, 0:1])

    tapn = {"f": 0, "b": 0}
    tap_offsets = {}

    def tap(name, src_ap, wait_sem=None, wait_val=0, bf=False):
        if name not in DBG_TAPS:
            return
        if wait_sem is not None:
            bb.wait("sync", wait_sem, wait_val)
        w = src_ap.shape[1]
        key = "b" if bf else "f"
        dst = (dbgb if bf else dbgf)
        o = tapn[key]
        tapn[key] += w
        tap_offsets[name] = (key, o, w, src_ap.shape[0])
        st = bb.dma("sync", "dSt", dst[0:src_ap.shape[0], o:o + w], src_ap)
        fv = bb.fence("sync", "dSt", st)
        for eng in ("vector", "scalar", "tensor", "gpsimd"):
            bb.wait(eng, "stm", fv)

    slot_rel = [("cP", 0)] * 4
    slot_idx = [0]

    def ring_load(qeng, src_ap, cols=4096, f32view=False):
        si = slot_idx[0] % 4
        slot_idx[0] += 1
        rsem, rval = slot_rel[si]
        bb.wait(qeng, rsem, rval)
        sname = (f"rg{si}" if qeng == "gpsimd" else f"rs{si}")
        dst = ringf[si][:, 0:cols] if f32view else ring[si][:, 0:cols]
        ld = bb.dma(qeng, sname, dst, src_ap)
        return si, sname, ld

    def ring_release(si, sname, val):
        slot_rel[si] = (sname, val)

    for dst, src in ((identb, identb_h), (identf, identf_h), (triu, triu_h),
                     (ones1, ones1_h), (onesr, onesr_h), (onessc, onessc_h),
                     (actc, actc_h)):
        bb.dma("sync", "dC", dst, src)
    bb.dma("sync", "dC", nbgk2, bgk2_h.rearrange("(c p) -> p c", p=128),
           allow_slow_non_contiguous=True)
    bb.dma("sync", "dC", gnwc, gnw_h.rearrange("(c p) -> p c", p=128),
           allow_slow_non_contiguous=True)
    consts_done = bb.dma("sync", "dC", prow[:, 0:1024], c2)

    prep_on = lambda k: DBG_PREP >= k
    bb.dma("gpsimd", "dGc", wgk1s, cm_src(wgk1_h))
    bb.dma("gpsimd", "dGc", wgk2s, wgk2_h)
    bb.dma("gpsimd", "dGc", b1row, b1_h[None, :])
    gk_loaded = bb.dma("gpsimd", "dGc", b2row, b2_h[None, :])

    bb.wait("vector", "dC", consts_done)
    nbg_done = bb.v("tensor_scalar", nbgk2, nbgk2, -1.0, None, op0=ALU.mult)

    if prep_on(1):
        bb.wait("scalar", "dC", consts_done)
        silu_done = bb.act(prow[:, 0:1024], prow[:, 0:1024], AF.Silu)
        prev_ev = 0
        for c in range(8):
            bb.wait("tensor", cS, silu_done)
            if c >= 2:
                bb.wait("tensor", cV, prev_ev)
            pe_t = bb.trans(pf[c % 2][:, 0:2], prow[:, c * 128:(c + 1) * 128],
                            identf[0:2, 0:2], inc=True)
            bb.wait("vector", cP, pe_t)
            prev_ev = bb.v("tensor_copy", scT[:, c * 2:(c + 1) * 2], pf[c % 2][:, 0:2])
        scT_done = prev_ev

    if prep_on(2):
        mod_banks = [pf[2], pf[3], pf[4], pf[5]]
        for ng in range(3):
            for c in range(8):
                si, sn, ld = ring_load("sync",
                                   wmod[c * 128:(c + 1) * 128, ng * 2048:(ng + 1) * 2048],
                                   cols=2048, f32view=True)
                bb.wait("tensor", sn, ld)
                bb.wait("tensor", cV, scT_done)
                for n in range(4):
                    pm = bb.mm(mod_banks[n][0:2, :], scT[:, c * 2:(c + 1) * 2],
                               ringf[si][:, n * 512:(n + 1) * 512],
                               start=(c == 0), stop=(c == 7), inc=(n == 3))
                ring_release(si, cP, pm)
            bb.wait("vector", cP, bb.sem["cP"].n)
            for n in range(4):
                ev = bb.v("tensor_copy",
                          modraw[:, ng * 2048 + n * 512: ng * 2048 + (n + 1) * 512],
                          mod_banks[n][0:2, :])
            bb.wait("tensor", cV, ev)
        mod_rows_done = bb.sem["cV"].n
    tap("scT", scT, "cV", scT_done)
    tap("mod", modraw, "cV", mod_rows_done)

    if prep_on(3):
        sa_r = modraw[:, 0:1024]
        ca_r = modraw[:, 1024:2048]
        ga_r = modraw[:, 2048:3072]
        cb_r = modraw[:, 4096:5120]
        sb_r = modraw[:, 3072:4096]
        gb_r = modraw[:, 5120:6144]
        bb.wait("scalar", cV, mod_rows_done)
        for item_ap in (ga_r, gb_r):
            bb.act(prow[:, 0:1024], item_ap, AF.Sign)
            sgn = bb.act(prow[:, 1024:2048], item_ap, AF.Abs)
            bb.wait("vector", cS, sgn)
            bb.v("tensor_scalar", item_ap, prow[:, 1024:2048], 0.25, None, op0=ALU.max)
            cl_done = bb.v("tensor_tensor", item_ap, item_ap, prow[:, 0:1024], op=ALU.mult)
            bb.wait("scalar", cV, cl_done)
        bb.v("tensor_scalar", prow[:, 0:1024], ca_r, 1.0, None, op0=ALU.add)
        pr_done = bb.v("tensor_scalar", prow[:, 1024:2048], sa_r, -1.0, None,
                       op0=ALU.mult)
        bb.wait("gpsimd", cV, pr_done)
        for sq in range(2):
            modrows_dma = bb.dma("gpsimd", "dMr", modrows[:, sq * 2048:(sq + 1) * 2048],
                                 prow[sq:sq + 1, 0:2048])
        modrows_done = modrows_dma

        prev_ev = 0
        for item_ap, base in ((ga_r, 0), (gb_r, 16)):
            for c in range(8):
                bb.wait("tensor", cV, cl_done if prev_ev == 0 else prev_ev)
                pe_t = bb.trans(pf[0][:, 0:2], item_ap[:, c * 128:(c + 1) * 128],
                                identf[0:2, 0:2], inc=True)
                bb.wait("vector", cP, pe_t)
                prev_ev = bb.v("tensor_copy", gcols[:, base + c * 2:base + c * 2 + 2],
                               pf[0][:, 0:2])
        gcols_done = prev_ev

    if prep_on(4):
        def bcast_fill(dst, src_row_bf, width, rows_ready_sem, rows_ready):
            last = 0
            for n0 in range(0, width, 512):
                w = min(512, width - n0)
                bb.wait("tensor", rows_ready_sem, rows_ready)
                if last:
                    bb.wait("tensor", cV, last)
                pm = bb.mm(pf[1][:, 0:w], ones1, src_row_bf[:, n0:n0 + w],
                           start=True, stop=True, inc=True)
                bb.wait("vector", cP, pm)
                last = bb.v("tensor_copy", dst[:, n0:n0 + w], pf[1][:, 0:w])
            return last

        bb.wait("vector", "dMr", modrows_done)
        ocb_done = bb.v("tensor_scalar", prow[:, 0:1024], cb_r, 1.0, None, op0=ALU.add)
        for s in range(2):
            bb.wait("gpsimd", cV, ocb_done)
            r1 = bb.dma("gpsimd", "dStage", stage1[:, 0:1024], prow[s:s + 1, 0:1024])
            lastb = bcast_fill(cbB[s], stage1[:, 0:1024], 1024, "dStage", r1)
            bb.wait("gpsimd", cV, lastb)
            r2 = bb.dma("gpsimd", "dStage", stage1[:, 0:1024], sb_r[s:s + 1, :])
            lastb = bcast_fill(sbB[s], stage1[:, 0:1024], 1024, "dStage", r2)
            bb.wait("gpsimd", cV, lastb)
        for dst, src_h, width in ((g1B, ln1g_h, 1024), (b1B, ln1b_h, 1024),
                                  (g2B, ln2g_h, 4096), (b2B, ln2b_h, 4096)):
            ld = bb.dma("gpsimd", "dStage", stage1[:, 0:width], src_h[None, :])
            lastb = bcast_fill(dst, stage1[:, 0:width], width, "dStage", ld)
            bb.wait("gpsimd", cV, lastb)

    if prep_on(5):
        def cast_roundtrip(src_ap, dst_ap):
            si, sn, ld = ring_load("gpsimd", cm_src(src_ap))
            bb.wait("sync", sn, ld)
            st = bb.dma("sync", "dSt", cm_src(dst_ap), ring[si])
            fv = bb.fence("sync", "dSt", st)
            ring_release(si, "stm", fv)
            return st

        cast_roundtrip(wq_h, s_wq)
        cast_roundtrip(wk_h, s_wk)
        cast_roundtrip(wv_h[:, 0:512], s_wv[:, 0:512])
        cast_roundtrip(wv_h[:, 512:1024], s_wv[:, 512:1024])
        cast_roundtrip(wg_h[:, 0:512], s_wg[:, 0:512])
        cast_roundtrip(wg_h[:, 512:1024], s_wg[:, 512:1024])
        for half in range(2):
            si, sn, ld = ring_load("gpsimd",
                               cm_src(wo_h[:, half * 512:(half + 1) * 512]))
            bb.wait("vector", sn, ld)
            for c in range(8):
                sc_done = bb.v("tensor_scalar", ring[si][:, c * 512:(c + 1) * 512],
                               ring[si][:, c * 512:(c + 1) * 512],
                               gnwc[:, c % 2:c % 2 + 1], None, op0=ALU.mult)
            bb.wait("sync", cV, sc_done)
            st = bb.dma("sync", "dSt", cm_src(s_wo[:, half * 512:(half + 1) * 512]),
                        ring[si])
            fv = bb.fence("sync", "dSt", st)
            ring_release(si, "stm", fv)

    if prep_on(6):
        def quant(w_hbm, rows, cols, s_q, widx):
            if not hasattr(quant, "store_mark"):
                quant.store_mark = [0, 0, 0, 0]
            tiles = []
            if cols == 1024:
                for li in range(rows // 256):
                    tiles.append((cm_src(w_hbm[li * 256:(li + 1) * 256, :]),
                                  cm_src(s_q[li * 256:(li + 1) * 256, :])))
            else:
                for rc in range(rows // 128):
                    for hf in range(2):
                        tiles.append((w_hbm[rc * 128:(rc + 1) * 128,
                                            hf * 2048:(hf + 1) * 2048],
                                      s_q[rc * 128:(rc + 1) * 128,
                                          hf * 2048:(hf + 1) * 2048]))
            for li, (src, _d) in enumerate(tiles):
                si, sn, ld = ring_load("sync" if li % 2 == 0 else "scalar", src,
                                       cols=2048, f32view=True)
                bb.wait("vector", sn, ld)
                rd = bb.v("tensor_reduce", macc[:, li:li + 1], ringf[si][:, 0:2048],
                          axis=AX, op=ALU.add, apply_absolute_value=True)
                ring_release(si, cV, rd)
            bb.wait("vector", cV, rd)
            red_done = bb.v("tensor_reduce", macc[:, 33:34], macc[:, 0:16],
                            axis=AX, op=ALU.add)
            bb.wait("tensor", cV, red_done)
            tot_pe = bb.trans(pf[0][0:1, 0:128], macc[:, 33:34], identf, inc=True)
            bb.wait("vector", cP, tot_pe)
            m_ = bb.v("tensor_reduce", macc[0:1, 34:35], pf[0][0:1, 0:128], axis=AX,
                      op=ALU.add)
            bb.wait("vector", cV, m_)
            m_ = bb.v("tensor_scalar", macc[0:1, 35:36], macc[0:1, 34:35],
                 1.0 / (rows * cols), 1e-5, op0=ALU.mult, op1=ALU.max)
            bb.wait("vector", cV, m_)
            sc_done = bb.v("reciprocal", macc[0:1, 36:37], macc[0:1, 35:36])
            bb.wait("tensor", cV, sc_done)
            bb.mm(pf[1][:, 0:1], onessc[0:1, :], macc[0:1, 36:37], start=True, stop=True)
            bc_pe = bb.mm(pf[1][:, 1:2], onessc[0:1, :], macc[0:1, 35:36],
                          start=True, stop=True, inc=True)
            bb.wait("vector", cP, bc_pe)
            bb.v("tensor_copy", wscale[:, widx:widx + 1], pf[1][:, 0:1])
            bcast_done = bb.v("tensor_copy", wmean[:, widx:widx + 1], pf[1][:, 1:2])
            last_store = 0
            for li, (src, dst) in enumerate(tiles):
                si, sn, ld = ring_load("sync" if li % 2 == 0 else "gpsimd", src,
                                       cols=2048, f32view=True)
                bb.wait("vector", "sst", quant.store_mark[si])
                bb.wait("scalar", sn, ld)
                bb.wait("scalar", cV, bcast_done)
                bb.act(ringf[si][:, 0:2048], ringf[si][:, 0:2048], AF.Identity,
                       bias=actc[:, 0:1], scale=wscale[:, widx:widx + 1])
                acc_fence()
                bb.wait("vector", cS, bb.sem["cS"].n)
                bb.v("tensor_scalar", actA[:, si * 2048:si * 2048 + 2048],
                     ringf[si][:, 0:2048], CRND, None, op0=ALU.subtract)
                qd = bb.v("tensor_scalar", actA[:, si * 2048:si * 2048 + 2048],
                          actA[:, si * 2048:si * 2048 + 2048],
                          1.0, -1.0, op0=ALU.min, op1=ALU.max)
                bb.wait("scalar", cV, qd)
                last_store = bb.dma("scalar", "sst", dst,
                                    actA[:, si * 2048:si * 2048 + 2048])
                quant.store_mark[si] = last_store
                ring_release(si, cV, qd)
            return last_store

    if prep_on(7):
        q1_done = quant(w1_h, 4096, 1024, s_w1q, 0)
        q2_done = quant(w2_h, 1024, 4096, s_w2q, 1)

    if prep_on(8):
        bb.wait("scalar", "sst", q1_done)
        for c in range(8):
            si = slot_idx[0] % 4
            slot_idx[0] += 1
            rsem, rval = slot_rel[si]
            bb.wait("scalar", rsem, rval)
            ld = bb.dmat("scalar", f"rs{si}", ring[si], s_w1q[:, c * 128:(c + 1) * 128])
            bb.wait("sync", f"rs{si}", ld)
            st = bb.dma("sync", f"rs{si}", s_w1qT[c * 128:(c + 1) * 128, :], ring[si])
            slot_rel[si] = (f"rs{si}", st)
        bb.wait("scalar", "sst", q2_done)
        for hc in range(32):
            si = slot_idx[0] % 4
            slot_idx[0] += 1
            rsem, rval = slot_rel[si]
            bb.wait("scalar", rsem, rval)
            ld = bb.dmat("scalar", f"rs{si}", ring[si][:, 0:1024],
                         s_w2q[:, hc * 128:(hc + 1) * 128])
            bb.wait("sync", f"rs{si}", ld)
            st = bb.dma("sync", f"rs{si}", s_w2qT[hc * 128:(hc + 1) * 128, :],
                        ring[si][:, 0:1024])
            slot_rel[si] = (f"rs{si}", st)

    tap("w1q0", s_w1q[0:128, :], "stm", bb.sem["stm"].n, bf=True)
    tap("w1qT0", s_w1qT[0:128, :], "stm", bb.sem["stm"].n, bf=True)
    bb.wait("sync", "sst", bb.sem["sst"].n)
    for _si in range(4):
        bb.wait("sync", f"rs{_si}", bb.sem[f"rs{_si}"].n)
    all_st = bb.fence("sync", "dSt", bb.sem["dSt"].n)
    for _si in range(4):
        if slot_rel[_si][0] == "dSt":
            slot_rel[_si] = ("stm", all_st)

    def slice_src(idx):
        if idx == 0:
            return cm_src(s_wq)
        if idx == 1:
            return cm_src(s_wk)
        if idx in (2, 3):
            return cm_src(s_wv[:, (idx - 2) * 512:(idx - 1) * 512])
        if idx in (4, 5):
            return cm_src(s_wg[:, (idx - 4) * 512:(idx - 3) * 512])
        if idx in (6, 7):
            return cm_src(s_wo[:, (idx - 6) * 512:(idx - 5) * 512])
        if 8 <= idx < 16:
            nsl = idx - 8
            return cm_src(s_w1qT[:, nsl * 512:(nsl + 1) * 512])
        ec = (idx - 16) % 8
        return cm_src(s_w2qT[:, ec * 128:(ec + 1) * 128])

    def wslice(idx):
        si, sn, ld = ring_load("sync", slice_src(idx))
        bb.wait("tensor", sn, ld)
        return si

    def wslice_end(si):
        ring_release(si, cP, bb.sem["cP"].n)

    out_store_done = {}
    xb_loaded = {}
    xf_loaded = {}
    h_done_prev = [0]

    bb.dma("gpsimd", "sxb0", x_b[0], xT[:, 0:TW].rearrange("(c p) t -> p c t", p=128))
    xb_loaded[0] = ("sxb0", bb.sem["sxb0"].n)
    bb.dma("scalar", "sxf", x_f, xT[:, 0:TW].rearrange("(c p) t -> p c t", p=128))
    xf_loaded[0] = bb.sem["sxf"].n

    for t in range(min(NT, DBG_TILES)):
        seq = t // 4
        col0 = t * TW
        xb = x_b[0]

        if t % 4 == 0:
            bb.v("memset", S_f, 0.0)
            bb.v("memset", S_b, 0.0)
        s_bf_ready = bb.sem["cV"].n

        if DBG_PHASE < 1:
            continue
        bb.wait("scalar", xb_loaded[t][0], xb_loaded[t][1])
        bb.wait("tensor", xb_loaded[t][0], xb_loaded[t][1])
        ssq_pe = {}
        for c in range(8):
            if c >= 2:
                bb.wait("scalar", cP, ssq_pe[c - 2])
            sq = bb.act(scrAb[:, (c % 2) * 512:(c % 2) * 512 + 512],
                        xb[:, c * 512:(c + 1) * 512], AF.Square)
            bb.wait("tensor", cS, sq)
            bb.mm(pf[0][0:1, :], triu[:, 127:128], xb[:, c * 512:(c + 1) * 512],
                  start=(c == 0), stop=(c == 7))
            ssq_pe[c] = bb.mm(pf[1][0:1, :], triu[:, 127:128],
                              scrAb[:, (c % 2) * 512:(c % 2) * 512 + 512],
                              start=(c == 0), stop=(c == 7), inc=True)
        stats_done = bb.sem["cP"].n
        bb.wait("vector", cP, stats_done)
        mu_r = rowsf[:, 0:512]
        ve_r = rowsf[:, 512:1024]
        t1_r = rowsf[:, 1024:1536]
        t2_r = rowsf[:, 1536:2048]
        bb.v("tensor_scalar", mu_r, pf[0][0:1, :], 1.0 / D, None, op0=ALU.mult)
        bb.v("tensor_scalar", ve_r, pf[1][0:1, :], 1.0 / D, None, op0=ALU.mult)
        bb.v("tensor_tensor", t1_r, mu_r, mu_r, op=ALU.mult)
        bb.v("tensor_tensor", ve_r, ve_r, t1_r, op=ALU.subtract)
        magic_rsqrt(bb, ve_r, ve_r, t1_r, t2_r, EPS)
        bb.v("tensor_tensor", t1_r, mu_r, ve_r, op=ALU.mult)
        bb.v("tensor_copy", rowsb[:, 0:512], ve_r)
        rows_bf_done = bb.v("tensor_copy", rowsb[:, 512:1024], t1_r)

        ocar = modrows[:, seq * 2048:seq * 2048 + 1024]
        nsar = modrows[:, seq * 2048 + 1024:seq * 2048 + 2048]
        bb.wait("tensor", "dMr", modrows_done)
        lnapply = {}
        for c in range(8):
            bA = pf[2 + 2 * (c % 2)]
            bB2 = pf[3 + 2 * (c % 2)]
            bb.wait("tensor", cV, rows_bf_done)
            if c >= 2:
                bb.wait("tensor", cV, lnapply[c - 2])
            bb.mm(bA, ocar[:, c * 128:(c + 1) * 128], rowsb[:, 0:512],
                  start=True, stop=True)
            bb.mm(bB2, ocar[:, c * 128:(c + 1) * 128], rowsb[:, 512:1024],
                  start=True, stop=False)
            pe_d = bb.mm(bB2, nsar[:, c * 128:(c + 1) * 128], onesr,
                         start=False, stop=True, inc=True)
            bb.wait("vector", cP, pe_d)
            bb.v("tensor_tensor", h_b[:, c * 512:(c + 1) * 512],
                 xb[:, c * 512:(c + 1) * 512], bA, op=ALU.mult)
            lnapply[c] = bb.v("scalar_tensor_tensor", h_b[:, c * 512:(c + 1) * 512],
                              bB2, -1.0, h_b[:, c * 512:(c + 1) * 512],
                              op0=ALU.mult, op1=ALU.add)
        h_done = bb.sem["cV"].n
        h_done_prev[0] = h_done
        if t + 1 < NT:
            bb.wait("gpsimd", cV, h_done)
            bb.dma("gpsimd", "sxb0", x_b[0],
                   xT[:, (t + 1) * TW:(t + 2) * TW].rearrange("(c p) t -> p c t", p=128))
            xb_loaded[t + 1] = ("sxb0", bb.sem["sxb0"].n)
        if t == 0:
            tap("xb", xb, "cV", h_done, bf=True)
            tap("h", h_b, "cV", h_done, bf=True)
            tap("rows", rowsb, "cV", h_done, bf=True)

        if DBG_PHASE < 2:
            continue
        evac_tick = [0]

        def evac(dst, src_ps, pewait):
            k = evac_tick[0]
            evac_tick[0] += 1
            if k % 2 == 0:
                bb.wait("vector", cP, pewait)
                return ("cV", bb.v("tensor_copy", dst, src_ps))
            bb.wait("scalar", cP, pewait)
            return ("cS", bb.scopy(dst, src_ps))

        def pe_wait(mark):
            bb.wait("tensor", mark[0], mark[1])

        bb.wait("tensor", cV, h_done)
        marks = []
        for which, dstbuf, slc in ((0, qg, 0), (1, ki, 1)):
            si = wslice(slc)
            for m in range(4):
                bank = pf[(which * 2) + m % 2]
                if len(marks) >= 2:
                    pe_wait(marks[-2])
                for c in range(8):
                    pe_d = bb.mm(bank,
                                 ring[si][:, c * 512 + m * 128: c * 512 + (m + 1) * 128],
                                 h_b[:, c * 512:(c + 1) * 512],
                                 start=(c == 0), stop=(c == 7), inc=(c == 7))
                marks.append(evac(dstbuf[:, m * 512:(m + 1) * 512], bank, pe_d))
            wslice_end(si)
        bb.wait("tensor", "dGc", gk_loaded)
        for c in range(8):
            pe_d = bb.mm(pf[4][0:16, :], wgk1s[:, c * 16:(c + 1) * 16],
                         h_b[:, c * 512:(c + 1) * 512],
                         start=(c == 0), stop=(c == 7), inc=(c == 7))
        zmark = evac(atb[0:16, 0:512], pf[4][0:16, :], pe_d)
        pe_wait(zmark)
        bb.v("tensor_tensor_scan", colB[:, 12:14], onessc[:, 0:2], onessc[:, 0:2],
             0.0, op0=ALU.mult, op1=ALU.add, inc=False)
        lgm = {}
        for m in range(4):
            bank = pf[5] if m % 2 == 0 else pf[4]
            if m >= 2:
                bb.wait("tensor", cV, lgm[m - 2])
            pe_d = bb.mm(bank, wgk2s[:, m * 128:(m + 1) * 128], atb[0:16, 0:512],
                         start=True, stop=True, inc=True)
            bb.wait("scalar", cP, pe_d)
            if m >= 1:
                bb.wait("scalar", cV, lgm[m - 1])
            bb.act(scrA, bank, AF.Exp, bias=nbgk2[:, m:m + 1], scale=actc[:, 3:4])
            lg = bb.act(scrB, scrA, AF.Ln, bias=actc[:, 1:2], scale=1.0)
            if t == 0 and m == 0:
                tap("scrA", scrA, "cS", lg)
                tap("scrB", scrB, "cS", lg)
            bb.wait("vector", cS, lg)
            for sg in range(4):
                bb.v("tensor_tensor_scan",
                     Gp[:, m * 512 + sg * 128: m * 512 + (sg + 1) * 128],
                     onessc, scrB[:, sg * 128:(sg + 1) * 128], 0.0,
                     op0=ALU.mult, op1=ALU.add, inc=(sg == 3))
            lgm[m] = bb.sem["cV"].n
        gp_done = bb.sem["cV"].n
        if t == 0:
            tap("gp", Gp, "cV", gp_done)
            tap("qkraw", qg, "cV", gp_done, bf=True)
            tap("vtm", v_tm, "cV", gp_done, bf=True)
        vmarks = []
        for half in range(2):
            si = wslice(2 + half)
            for s in range(NSUB):
                bank = pf[s % 2]
                if len(vmarks) >= 2:
                    pe_wait(vmarks[-2])
                for c in range(8):
                    pe_d = bb.mm(bank,
                                 h_b[:, c * 512 + s * 128: c * 512 + (s + 1) * 128],
                                 ring[si][:, c * 512:(c + 1) * 512],
                                 start=(c == 0), stop=(c == 7), inc=(c == 7))
                vmarks.append(evac(v_tm[:, s * 1024 + half * 512:
                                        s * 1024 + (half + 1) * 512], bank, pe_d))
            wslice_end(si)
        gmarks = []
        for half in range(2):
            si = wslice(4 + half)
            for m in range(4):
                mc = half * 4 + m
                bank = pf[2 + mc % 2]
                if len(gmarks) >= 2:
                    pe_wait(gmarks[-2])
                for c in range(8):
                    pe_d = bb.mm(bank,
                                 ring[si][:, c * 512 + m * 128: c * 512 + (m + 1) * 128],
                                 h_b[:, c * 512:(c + 1) * 512],
                                 start=(c == 0), stop=(c == 7), inc=(c == 7))
                gmarks.append(evac(gsig[:, mc * 512:(mc + 1) * 512], bank, pe_d))
            wslice_end(si)

        if DBG_PHASE < 3:
            continue
        bb.wait("scalar", cV, gp_done)
        qgki = 0
        for m in range(4):
            eq = bb.act(scrA, Gp[:, m * 512:(m + 1) * 512], AF.Exp,
                        scale=actc[:, 4:5], bias=actc[:, 2:3])
            bb.wait("vector", cS, eq)
            bb.v("tensor_tensor", qg[:, m * 512:(m + 1) * 512],
                 qg[:, m * 512:(m + 1) * 512], scrA, op=ALU.mult)
            ek = bb.act(scrB, Gp[:, m * 512:(m + 1) * 512], AF.Exp,
                        scale=actc[:, 5:6], bias=0.0)
            bb.wait("vector", cS, ek)
            qgki = bb.v("tensor_tensor", ki[:, m * 512:(m + 1) * 512],
                        ki[:, m * 512:(m + 1) * 512], scrB, op=ALU.mult)
            bb.wait("scalar", cV, qgki)
        glp = bass.AP(Gp.tensor, Gp.offset + 127,
                      [[2048, 128], [128, 4], [512, 4]])
        glo = bass.AP(glall.tensor, glall.offset, [[16, 128], [4, 4], [1, 4]])
        pk = bb.v("tensor_copy", glo, glp)
        bb.wait("scalar", cV, pk)
        bb.act(glall[:, 0:16], glall[:, 0:16], AF.Exp, scale=actc[:, 4:5], bias=0.0)
        gl_done = bb.sem["cS"].n

        o_pe_prev = 0
        for s in range(NSUB):
            bb.wait("vector", cS, gl_done)
            if s >= 1:
                bb.wait("vector", cP, o_pe_prev)
            for hh in range(H):
                kg_d = bb.v("tensor_scalar", atb[:, hh * 128:(hh + 1) * 128],
                            ki[:, hh * 512 + s * 128: hh * 512 + (s + 1) * 128],
                            glall[:, s * 4 + hh:s * 4 + hh + 1], None, op0=ALU.mult,
                            inc=(hh == H - 1))
            bb.wait("tensor", cV, kg_d)
            for hh in range(H):
                kgt_pe = bb.trans(pb[0][:, hh * 128:(hh + 1) * 128],
                                  atb[:, hh * 128:(hh + 1) * 128], identb,
                                  inc=(hh == H - 1))
            bb.wait("vector", cP, kgt_pe)
            kgtm_d = bb.v("tensor_copy", kgtm, pb[0][:, 0:512])

            atm = {}
            for hh in range(H):
                bank = pf[2 + hh % 2]
                if hh >= 2:
                    bb.wait("tensor", cV, atm[hh - 2])
                at_pe = bb.mm(bank[:, 0:128],
                              ki[:, hh * 512 + s * 128: hh * 512 + (s + 1) * 128],
                              qg[:, hh * 512 + s * 128: hh * 512 + (s + 1) * 128],
                              start=True, stop=True, inc=True)
                bb.wait("vector", cP, at_pe)
                atm[hh] = bb.v("tensor_tensor", atb[:, hh * 128:(hh + 1) * 128],
                               bank[:, 0:128], triu, op=ALU.mult)
            at_done = bb.sem["cV"].n

            bb.wait("tensor", cV, at_done)
            for hh in range(H):
                bank = pf[hh // 2]
                off = (hh % 2) * 256
                bb.mm(bank[:, off:off + 256], atb[:, hh * 128:(hh + 1) * 128],
                      v_tm[:, s * 1024 + hh * 256: s * 1024 + (hh + 1) * 256],
                      start=True, stop=False)
                o_pe = bb.mm(bank[:, off:off + 256],
                             qg[:, hh * 512 + s * 128: hh * 512 + (s + 1) * 128],
                             S_b[:, hh * 256:(hh + 1) * 256], start=False, stop=True,
                             inc=(hh == H - 1))
            o_pe_prev = o_pe
            bb.wait("tensor", cV, kgtm_d)
            for hh in range(H):
                bank = pf[4 + hh // 2]
                off = (hh % 2) * 256
                kv_pe = bb.mm(bank[:, off:off + 256], kgtm[:, hh * 128:(hh + 1) * 128],
                              v_tm[:, s * 1024 + hh * 256: s * 1024 + (hh + 1) * 256],
                              start=True, stop=True, inc=(hh == H - 1))

            bb.wait("scalar", cP, o_pe)
            for hh in range(H):
                bank = pf[hh // 2]
                off = (hh % 2) * 256
                gn = bb.act(scrAb[:, 0:256], bank[:, off:off + 256], AF.Square,
                            accum=ssqo[:, s * 4 + hh:s * 4 + hh + 1])
            gn = acc_fence()
            bb.wait("vector", cS, gn)
            bb.v("tensor_scalar", colA[:, s * 4:(s + 1) * 4],
                 ssqo[:, s * 4:(s + 1) * 4], 1.0 / DVH, None, op0=ALU.mult)
            magic_rsqrt(bb, colA[:, s * 4:(s + 1) * 4], colA[:, s * 4:(s + 1) * 4],
                        colB[:, 0:4], colB[:, 4:8], EPS)
            for hh in range(H):
                bank = pf[hh // 2]
                off = (hh % 2) * 256
                on_d = bb.v("tensor_scalar", o_n[:, hh * 256:(hh + 1) * 256],
                            bank[:, off:off + 256],
                            colA[:, s * 4 + hh:s * 4 + hh + 1], None, op0=ALU.mult,
                            inc=(hh == H - 1))
            bb.wait("vector", cP, kv_pe)
            for hh in range(H):
                bank = pf[4 + hh // 2]
                off = (hh % 2) * 256
                bb.v("tensor_scalar", S_f[:, hh * 256:(hh + 1) * 256],
                     S_f[:, hh * 256:(hh + 1) * 256],
                     glall[:, s * 4 + hh:s * 4 + hh + 1], None, op0=ALU.mult)
                bb.v("tensor_tensor", S_f[:, hh * 256:(hh + 1) * 256],
                     S_f[:, hh * 256:(hh + 1) * 256], bank[:, off:off + 256],
                     op=ALU.add)
            s_bf_ready = bb.v("tensor_copy", S_b, S_f)

            if s == 0:
                for c in range(8):
                    pe_wait_mark = gmarks[c]
                    if pe_wait_mark[0] == "cV":
                        bb.wait("scalar", cV, pe_wait_mark[1])
                    gs = bb.act(gsig[:, c * 512:(c + 1) * 512],
                                gsig[:, c * 512:(c + 1) * 512], AF.Silu)
                gsil_done = gs
                bb.wait("vector", cS, gsil_done)

            bb.wait("tensor", cV, on_d)
            for cblk in range(8):
                ont_pe = bb.trans(pb[1][:, cblk * 128:(cblk + 1) * 128],
                                  o_n[:, cblk * 128:(cblk + 1) * 128], identb,
                                  inc=(cblk == 7))
            bb.wait("vector", cP, ont_pe)
            if s == 0:
                bb.wait("vector", cS, gsil_done)
            og_src = bass.AP(pb[1].tensor, pb[1].offset, [[1024, 128], [128, 8], [1, 128]])
            og_gs = bass.AP(gsig.tensor, gsig.offset + s * 128,
                            [[16384, 128], [512, 8], [1, 128]])
            og_gs2 = bass.AP(gsig.tensor, gsig.offset + s * 128,
                             [[16384, 128], [512, 8], [1, 128]])
            og_d = bb.v("tensor_tensor", og_gs, og_src, og_gs2, op=ALU.mult)
            bb.wait("tensor", cV, og_d)
        og_all = bb.sem["cV"].n

        if DBG_PHASE < 4:
            continue
        womarks = []
        bb.wait("tensor", cV, og_all)
        for half in range(2):
            si = wslice(6 + half)
            for m in range(4):
                mc = half * 4 + m
                bank = pf[mc % 2]
                if len(womarks) >= 2:
                    bb.wait("tensor", cV, womarks[-2])
                for c in range(8):
                    wo_pe = bb.mm(bank,
                                  ring[si][:, c * 512 + m * 128: c * 512 + (m + 1) * 128],
                                  gsig[:, c * 512:(c + 1) * 512],
                                  start=(c == 0), stop=(c == 7), inc=(c == 7))
                bb.wait("vector", cP, wo_pe)
                bb.wait("vector", "sxf", xf_loaded[t])
                womarks.append(bb.v("scalar_tensor_tensor",
                                    x_f[:, mc * 512:(mc + 1) * 512], bank,
                                    gcols[:, mc * 2 + seq:mc * 2 + seq + 1],
                                    x_f[:, mc * 512:(mc + 1) * 512],
                                    op0=ALU.mult, op1=ALU.add))
            wslice_end(si)
        x1_done = bb.sem["cV"].n
        if t == 0:
            tap("x1", x_f, "cV", x1_done)
            tap("og", gsig, "cV", x1_done, bf=True)
            tap("ssqo", ssqo, "cV", x1_done)
            tap("glall", glall, "cV", x1_done)
            tap("colA", colA, "cV", x1_done)
            tap("Sf", S_f, "cV", x1_done)
            tap("on", o_n, "cV", x1_done, bf=True)

        if DBG_PHASE < 5:
            continue
        evm = {}
        xcm = {}
        for s in range(NSUB):
            xc_src = bass.AP(x_f.tensor, x_f.offset + s * 128,
                             [[4096, 128], [512, 8], [1, 128]])
            xcm[s] = bb.v("tensor_copy", ttm4[:, s * 1024:(s + 1) * 1024], xc_src)
        for s in range(NSUB):
            bb.wait("tensor", cV, xcm[s])
            if s >= 2:
                bb.wait("tensor", cV, evm[s - 2])
            for blk in range(8):
                x1t_pe = bb.trans(pb[s % 2][:, blk * 128:(blk + 1) * 128],
                                  ttm4[:, s * 1024 + blk * 128:
                                       s * 1024 + (blk + 1) * 128],
                                  identb, inc=(blk == 7))
            bb.wait("vector", cP, x1t_pe)
            evm[s] = bb.v("tensor_copy", xtm4[:, s * 1024:(s + 1) * 1024],
                          pb[s % 2][:, 0:1024])
        for s in range(NSUB):
            bb.wait("scalar", cV, evm[s])
            bb.act(scrAb[:, 0:1024], xtm4[:, s * 1024:(s + 1) * 1024], AF.Square,
                   accum=sumP[:, 0 + s:1 + s])
            bb.act(scrAb[:, 0:1024], xtm4[:, s * 1024:(s + 1) * 1024], AF.Copy,
                   accum=sumP[:, 4 + s:5 + s])
        ssq2 = acc_fence()
        bb.wait("vector", cS, ssq2)
        bb.v("tensor_scalar", colA[:, 0:4], sumP[:, 4:8], 1.0 / D, None,
             op0=ALU.mult)
        m_ = bb.v("tensor_scalar", colA[:, 4:8], sumP[:, 0:4], 1.0 / D, None,
                  op0=ALU.mult)
        bb.wait("vector", cV, m_)
        m_ = bb.v("tensor_tensor", colB[:, 8:12], colA[:, 0:4], colA[:, 0:4],
                  op=ALU.mult)
        bb.wait("vector", cV, m_)
        bb.v("tensor_tensor", colA[:, 4:8], colA[:, 4:8], colB[:, 8:12],
             op=ALU.subtract)
        magic_rsqrt(bb, colA[:, 4:8], colA[:, 4:8], colB[:, 8:12],
                    colB[:, 12:16], EPS)
        h2m = {}
        for s in range(NSUB):
            sl = slice(s * 1024, (s + 1) * 1024)
            bb.v("tensor_scalar", ttm4[:, sl], xtm4[:, sl], colA[:, s:s + 1],
                 colA[:, 4 + s:5 + s], op0=ALU.subtract, op1=ALU.mult)
            bb.v("tensor_tensor", ttm4[:, sl], ttm4[:, sl], cbB[seq], op=ALU.mult)
            h2m[s] = bb.v("scalar_tensor_tensor", ttm4[:, sl], ttm4[:, sl], 1.0,
                          sbB[seq], op0=ALU.mult, op1=ALU.add)
        for s in range(NSUB):
            bb.wait("scalar", cV, h2m[s])
            bb.act(scrAb[:, 0:1024], ttm4[:, s * 1024:(s + 1) * 1024], AF.Square,
                   accum=sumP[:, 8 + s:9 + s])
            bb.act(scrAb[:, 0:1024], ttm4[:, s * 1024:(s + 1) * 1024], AF.Copy,
                   accum=sumP[:, 12 + s:13 + s])
        s3 = acc_fence()
        bb.wait("vector", cS, s3)
        bb.v("tensor_scalar", colA[:, 8:12], sumP[:, 12:16], 1.0 / D, None,
             op0=ALU.mult)
        m_ = bb.v("tensor_scalar", colA[:, 12:16], sumP[:, 8:12], 1.0 / D, None,
                  op0=ALU.mult)
        bb.wait("vector", cV, m_)
        m_ = bb.v("tensor_tensor", colB[:, 8:12], colA[:, 8:12], colA[:, 8:12],
                  op=ALU.mult)
        bb.wait("vector", cV, m_)
        bb.v("tensor_tensor", colA[:, 12:16], colA[:, 12:16], colB[:, 8:12],
             op=ALU.subtract)
        magic_rsqrt(bb, colA[:, 12:16], colA[:, 12:16], colB[:, 8:12],
                    colB[:, 12:16], 1e-5)
        h3m = {}
        for s in range(NSUB):
            sl = slice(s * 1024, (s + 1) * 1024)
            bb.v("tensor_scalar", ttm4[:, sl], ttm4[:, sl], colA[:, 8 + s:9 + s],
                 colA[:, 12 + s:13 + s], op0=ALU.subtract, op1=ALU.mult)
            bb.v("tensor_tensor", ttm4[:, sl], ttm4[:, sl], g1B, op=ALU.mult)
            h3m[s] = bb.v("tensor_tensor", ttm4[:, sl], ttm4[:, sl], b1B, op=ALU.add)
        for pair in ((0, 1), (2, 3)):
            bb.wait("scalar", cV, h3m[pair[1]])
            for k, s in enumerate(pair):
                bb.act(hscr[:, k * 1024:k * 1024 + 1024],
                       ttm4[:, s * 1024:(s + 1) * 1024], AF.Abs)
            absm = acc_fence()
            bb.wait("vector", cS, absm)
            for k, s in enumerate(pair):
                bb.v("reduce_max", ssqo[:, 2 * s:2 * s + 1],
                     hscr[:, k * 1024:k * 1024 + 512], axis=AX)
                bb.v("reduce_max", ssqo[:, 2 * s + 1:2 * s + 2],
                     hscr[:, k * 1024 + 512:k * 1024 + 1024], axis=AX)
        m_ = bb.sem["cV"].n
        bb.wait("vector", cV, m_)
        for s in range(NSUB):
            mxd = bb.v("scalar_tensor_tensor", colA[:, 16 + s:17 + s],
                       ssqo[:, 2 * s:2 * s + 1], 1e-5, ssqo[:, 2 * s + 1:2 * s + 2],
                       op0=ALU.max, op1=ALU.max)
        bb.wait("scalar", cV, mxd)
        lnr = bb.act(colB[:, 8:12], colA[:, 16:20], AF.Ln)
        bb.wait("scalar", cS, lnr)
        sxp = bb.act(colA[:, 20:24], colB[:, 8:12], AF.Exp,
                     scale=actc[:, 3:4], bias=actc[:, 8:9])
        bb.wait("vector", cS, sxp)
        bb.v("tensor_scalar", colB[:, 12:16], colA[:, 16:20], wmean[:, 0:1],
             1.0 / 127.0, op0=ALU.mult, op1=ALU.mult)
        for s in range(NSUB):
            sl = slice(s * 1024, (s + 1) * 1024)
            bb.v("tensor_scalar", hscr[:, 0:1024], ttm4[:, sl], colA[:, 20 + s:21 + s],
                 CRND, op0=ALU.mult, op1=ALU.add)
            dq_d = bb.v("tensor_scalar", dqtm, hscr[:, 0:1024], CRND,
                        colB[:, 12 + s:13 + s], op0=ALU.subtract, op1=ALU.mult)
            if t == 0 and s == 0:
                tap("midcols", colA, "cV", dq_d)
                tap("h3", ttm4, "cV", dq_d, bf=True)
                tap("dq0", dqtm, "cV", dq_d, bf=True)
            bb.wait("tensor", cV, dq_d)
            for cblk in range(8):
                dqt_pe = bb.trans(pb[s % 2][:, cblk * 128:(cblk + 1) * 128],
                                  dqtm[:, cblk * 128:(cblk + 1) * 128], identb,
                                  inc=(cblk == 7))
            bb.wait("vector", cP, dqt_pe)
            hq_src = bass.AP(pb[s % 2].tensor, pb[s % 2].offset,
                             [[1024, 128], [128, 8], [1, 128]])
            hq_dst = bass.AP(h2qT.tensor, h2qT.offset + s * 128,
                             [[4096, 128], [512, 8], [1, 128]])
            hq_d = bb.v("tensor_copy", hq_dst, hq_src)
            bb.wait("tensor", cV, hq_d)
        h2qT_done = bb.sem["cV"].n
        if t == 0:
            tap("h2qT", h2qT, "cV", h2qT_done, bf=True)

        if DBG_PHASE < 6:
            continue
        gelum = {}
        for nsl in range(8):
            si = wslice(8 + nsl)
            if nsl == 0:
                bb.wait("tensor", cV, h2qT_done)
                if t == 0:
                    tap("w1slot", ring[si], f"rs{si}", bb.sem[f"rs{si}"].n, bf=True)
                    tap("b1row", b1row, None, 0, bf=True)
            for s in range(NSUB):
                bank = pf[s]
                if nsl >= 1:
                    bb.wait("tensor", cS, gelum[(nsl - 1, s)])
                bb.mm(bank, ones1, b1row[:, nsl * 512:(nsl + 1) * 512],
                      start=True, stop=False)
                for c in range(8):
                    y1_pe = bb.mm(bank,
                                  h2qT[:, c * 512 + s * 128: c * 512 + (s + 1) * 128],
                                  ring[si][:, c * 512:(c + 1) * 512],
                                  start=False, stop=(c == 7), inc=(c == 7))
                bb.wait("scalar", cP, y1_pe)
                if t == 0 and nsl == 0 and s == 0 and "y1pe" in DBG_TAPS:
                    cpm = bb.scopy(scrA[:, 0:512], bank)
                    tap("y1pe", scrA, "cS", cpm)
                gelum[(nsl, s)] = bb.act(y1g[s][:, nsl * 512:(nsl + 1) * 512], bank,
                                         AF.Gelu,
                                         accum=ygsum[:, s * 8 + nsl:s * 8 + nsl + 1])
            wslice_end(si)
            if nsl % 2 == 1:
                q4 = nsl // 2
                for s in range(NSUB):
                    bb.act(scrAb[:, 0:1024], y1g[s][:, q4 * 1024:(q4 + 1) * 1024],
                           AF.Square, accum=sumP[:, s * 4 + q4:s * 4 + q4 + 1],
                           inc=(s == 3))

        if DBG_PHASE < 7:
            continue
        if t == 0:
            if "y1g0" in DBG_TAPS:
                bb.wait("sync", "cS", gelum[(7, 0)])
            tap("y1g0", y1g[0], None, 0, bf=True)
        ttm4f = ttm4.bitcast(F32)
        bb.wait("scalar", cS, gelum[(7, 3)])
        acc_fence()
        bb.wait("vector", cS, bb.sem["cS"].n)
        for s in range(NSUB):
            bb.v("tensor_reduce", colA[:, 24 + s:25 + s],
                 ygsum[:, s * 8:(s + 1) * 8], axis=AX, op=ALU.add)
            bb.v("tensor_reduce", colA[:, 28 + s:29 + s],
                 sumP[:, s * 4:(s + 1) * 4], axis=AX, op=ALU.add)
        bb.wait("vector", cV, bb.sem["cV"].n)
        bb.v("tensor_scalar", colA[:, 24:28], colA[:, 24:28], 1.0 / 4096.0, None,
             op0=ALU.mult)
        m_ = bb.v("tensor_scalar", colA[:, 28:32], colA[:, 28:32], 1.0 / 4096.0,
                  None, op0=ALU.mult)
        bb.wait("vector", cV, m_)
        m_ = bb.v("tensor_tensor", colB[:, 8:12], colA[:, 24:28], colA[:, 24:28],
                  op=ALU.mult)
        bb.wait("vector", cV, m_)
        bb.v("tensor_tensor", colA[:, 28:32], colA[:, 28:32], colB[:, 8:12],
             op=ALU.subtract)
        magic_rsqrt(bb, colA[:, 28:32], colA[:, 28:32], colB[:, 8:12],
                    colB[:, 12:16], 1e-5)
        h4m = {}
        for s in range(NSUB):
            bb.v("tensor_scalar", y1g[s], y1g[s], colA[:, 24 + s:25 + s],
                 colA[:, 28 + s:29 + s], op0=ALU.subtract, op1=ALU.mult)
            bb.v("tensor_tensor", y1g[s], y1g[s], g2B, op=ALU.mult)
            h4m[s] = bb.v("tensor_tensor", y1g[s], y1g[s], b2B, op=ALU.add)
        for s in range(NSUB):
            bb.wait("scalar", cV, h4m[s])
            bb.act(hscr[:, 0:1024], y1g[s][:, 0:1024], AF.Abs)
            bb.act(hscr[:, 1024:2048], y1g[s][:, 1024:2048], AF.Abs)
            bb.act(ttm4f[:, 0:1024], y1g[s][:, 2048:3072], AF.Abs)
            a4 = bb.act(ttm4f[:, 1024:2048], y1g[s][:, 3072:4096], AF.Abs)
            bb.wait("vector", cS, a4)
            for q4, src in enumerate((hscr[:, 0:1024], hscr[:, 1024:2048],
                                      ttm4f[:, 0:1024], ttm4f[:, 1024:2048])):
                rm = bb.v("reduce_max", ssqo[:, s * 4 + q4:s * 4 + q4 + 1], src,
                          axis=AX)
            bb.wait("scalar", cV, rm)
        bb.wait("vector", cV, bb.sem["cV"].n)
        for s in range(NSUB):
            bb.v("scalar_tensor_tensor", colB[:, 8 + s:9 + s],
                 ssqo[:, s * 4:s * 4 + 1], 1e-5, ssqo[:, s * 4 + 1:s * 4 + 2],
                 op0=ALU.max, op1=ALU.max)
            m_ = bb.v("scalar_tensor_tensor", colB[:, 12 + s:13 + s],
                      ssqo[:, s * 4 + 2:s * 4 + 3], 1e-5,
                      ssqo[:, s * 4 + 3:s * 4 + 4], op0=ALU.max, op1=ALU.max)
        bb.wait("vector", cV, m_)
        mxd = bb.v("tensor_tensor", colA[:, 32:36], colB[:, 8:12], colB[:, 12:16],
                   op=ALU.max)
        bb.wait("scalar", cV, mxd)
        lnr = bb.act(colB[:, 8:12], colA[:, 32:36], AF.Ln)
        bb.wait("scalar", cS, lnr)
        sxp = bb.act(colA[:, 36:40], colB[:, 8:12], AF.Exp,
                     scale=actc[:, 3:4], bias=actc[:, 8:9])
        bb.wait("vector", cS, sxp)
        ks2m = bb.v("tensor_scalar", colA[:, 40:44], colA[:, 32:36], wmean[:, 1:2],
                    1.0 / 127.0, op0=ALU.mult, op1=ALU.mult)
        if t == 0:
            tap("jcols", colA, "cV", ks2m)
        w2marks = []
        for half in range(2):
            for s in (2 * half, 2 * half + 1):
                for hf in range(4):
                    bb.v("tensor_scalar", hscr[:, (hf % 2) * 1024:(hf % 2) * 1024 + 1024],
                         y1g[s][:, hf * 1024:(hf + 1) * 1024],
                         colA[:, 36 + s:37 + s], CRND, op0=ALU.mult, op1=ALU.add)
                    dq2_d = bb.v("tensor_scalar",
                                 y1g[s][:, hf * 1024:(hf + 1) * 1024],
                                 hscr[:, (hf % 2) * 1024:(hf % 2) * 1024 + 1024], CRND,
                                 colA[:, 40 + s:41 + s],
                                 op0=ALU.subtract, op1=ALU.mult)
                bb.wait("tensor", cV, dq2_d)
                for grp in range(4):
                    for blk in range(8):
                        hcix = grp * 8 + blk
                        tp = bb.trans(pb[grp % 2][:, blk * 128:(blk + 1) * 128],
                                      y1g[s][:, hcix * 128:(hcix + 1) * 128], identb,
                                      inc=(blk == 7))
                    bb.wait("vector", cP, tp)
                    yt_src = bass.AP(pb[grp % 2].tensor, pb[grp % 2].offset,
                                     [[1024, 128], [128, 8], [1, 128]])
                    yt_dst = bass.AP(y1qT.tensor,
                                     y1qT.offset + grp * 8 * 256 + (s % 2) * 128,
                                     [[8192, 128], [256, 8], [1, 128]])
                    yt_d = bb.v("tensor_copy", yt_dst, yt_src)
                    bb.wait("tensor", cV, yt_d)
            y1qT_half = bb.sem["cV"].n
            if t == 0 and half == 0:
                tap("y1qT", y1qT, "cV", y1qT_half, bf=True)

            for ec in range(8):
                si = wslice(16 + half * 8 + ec)
                if ec == 0:
                    bb.wait("tensor", cV, y1qT_half)
                bank = pf[4 + ec % 2]
                if len(w2marks) >= 2:
                    bb.wait("tensor", cV, w2marks[-2])
                bb.mm(bank[:, 0:256], b2row[:, ec * 128:(ec + 1) * 128],
                      onesr[:, 0:256], start=True, stop=False)
                for hc in range(32):
                    y2_pe = bb.mm(bank[:, 0:256],
                                  ring[si][:, hc * 128:(hc + 1) * 128],
                                  y1qT[:, hc * 256:(hc + 1) * 256],
                                  start=False, stop=(hc == 31), inc=(hc == 31))
                bb.wait("vector", cP, y2_pe)
                w2marks.append(bb.v("scalar_tensor_tensor",
                                    x_f[:, ec * 512 + half * 256:
                                        ec * 512 + half * 256 + 256],
                                    bank[:, 0:256],
                                    gcols[:, 16 + ec * 2 + seq:16 + ec * 2 + seq + 1],
                                    x_f[:, ec * 512 + half * 256:
                                        ec * 512 + half * 256 + 256],
                                    op0=ALU.mult, op1=ALU.add))
                wslice_end(si)
        x2_done = bb.sem["cV"].n

        bb.wait("scalar", cV, x2_done)
        out_store_done[t] = bb.dma("scalar", "sst",
                                   outT[:, col0:col0 + TW]
                                   .rearrange("(c p) t -> p c t", p=128), x_f)
        if t + 1 < NT:
            bb.wait("scalar", "sst", out_store_done[t])
            bb.dma("scalar", "sxf", x_f,
                   xT[:, (t + 1) * TW:(t + 2) * TW]
                   .rearrange("(c p) t -> p c t", p=128))
            xf_loaded[t + 1] = bb.sem["sxf"].n

    if out_store_done:
        bb.wait("scalar", "sst", out_store_done[max(out_store_done)])
    else:
        mk = bb.dma("scalar", "sst", outT[:, 0:TW].rearrange("(c p) t -> p c t", p=128),
                    x_f)
        bb.wait("scalar", "sst", mk)

    for _sn in ("dC", "dSt", "rs0", "rs1", "rs2", "rs3"):
        bb.wait("sync", _sn, bb.sem[_sn].n)
    for _sn in ("rg0", "rg1", "rg2", "rg3", "dGc", "dStage", "dMr", "sxb0", "sxb1"):
        bb.wait("gpsimd", _sn, bb.sem[_sn].n)
    for _sn in ("sxf", "sst"):
        bb.wait("scalar", _sn, bb.sem[_sn].n)

    with nc.Block() as block:
        @block.sync
        def _(eng):
            for fn in bb.ops["sync"]:
                fn(eng)

        @block.tensor
        def _(eng):
            for fn in bb.ops["tensor"]:
                fn(eng)

        @block.vector
        def _(eng):
            for fn in bb.ops["vector"]:
                fn(eng)

        @block.scalar
        def _(eng):
            for fn in bb.ops["scalar"]:
                fn(eng)

        @block.gpsimd
        def _(eng):
            for fn in bb.ops["gpsimd"]:
                fn(eng)

    ctx.close()
    nc._tap_offsets = tap_offsets
    return nc


def kernel(**inputs):
    if "prog" not in _CACHED:
        _CACHED["prog"] = build_program()
    nc = _CACHED["prog"]

    x = np.ascontiguousarray(np.asarray(inputs["L0_attn_x"], np.float32))
    cfull = np.ascontiguousarray(np.asarray(inputs["L0_c"], np.float32))

    shared = dict(
        wmod=np.ascontiguousarray(inputs["L0_wmod"], np.float32),
        wq=np.ascontiguousarray(inputs["L0_attn_Wq"], np.float32),
        wk=np.ascontiguousarray(inputs["L0_attn_Wk"], np.float32),
        wv=np.ascontiguousarray(inputs["L0_attn_Wv"], np.float32),
        wg=np.ascontiguousarray(inputs["L0_attn_Wg"], np.float32),
        wo=np.ascontiguousarray(inputs["L0_attn_Wo"], np.float32),
        wgk1=np.ascontiguousarray(inputs["L0_attn_Wgk1"], np.float32),
        wgk2=np.ascontiguousarray(inputs["L0_attn_Wgk2"], np.float32),
        bgk2=np.ascontiguousarray(inputs["L0_attn_bgk2"], np.float32),
        gnw=np.ascontiguousarray(inputs["L0_attn_gnw"], np.float32),
        ln1g=np.ascontiguousarray(inputs["L0_ffn_ln1_g"], np.float32),
        ln1b=np.ascontiguousarray(inputs["L0_ffn_ln1_b"], np.float32),
        w1=np.ascontiguousarray(inputs["L0_ffn_w1"], np.float32),
        b1=np.ascontiguousarray(inputs["L0_ffn_b1"], np.float32),
        ln2g=np.ascontiguousarray(inputs["L0_ffn_ln2_g"], np.float32),
        ln2b=np.ascontiguousarray(inputs["L0_ffn_ln2_b"], np.float32),
        w2=np.ascontiguousarray(inputs["L0_ffn_w2"], np.float32),
        b2=np.ascontiguousarray(inputs["L0_ffn_b2"], np.float32),
        identb=np.eye(128, dtype=BF),
        identf=np.eye(128, dtype=np.float32),
        triu=np.triu(np.ones((128, 128), np.float32)).astype(BF),
        ones1=np.ones((1, 128), BF),
        onesr=np.ones((1, 512), BF),
        onessc=np.ones((128, 128), np.float32),
        actc=np.tile(np.array([[CRND, 1.0, LNSCALE, -1.0, -1.0 / 16.0, 1.0 / 16.0,
                                1e-5, -0.5, float(np.log(127.0)), 0.5]],
                              np.float32), (128, 1)),
    )
    in_maps = []
    for core in range(8):
        xTc = np.ascontiguousarray(
            np.concatenate([x[2 * core].T, x[2 * core + 1].T], axis=1))
        m = dict(shared)
        m["xT"] = xTc
        m["c2"] = np.ascontiguousarray(cfull[2 * core:2 * core + 2])
        in_maps.append(m)

    res = run_bass_kernel_spmd(nc, in_maps, core_ids=list(range(8)))
    out = np.empty_like(x)
    for core in range(8):
        oT = res.results[core]["outT"]
        out[2 * core] = oT[:, :2048].T
        out[2 * core + 1] = oT[:, 2048:].T
    return out


# version 14
# speedup vs baseline: 1.1842x; 1.0124x over previous
import numpy as np
import ml_dtypes
import concourse.bass as bass
import concourse.mybir as mybir
from concourse.bass_isa import ReduceOp
from concourse.bass_utils import run_bass_kernel_spmd
from contextlib import ExitStack

F32 = mybir.dt.float32
BF16 = mybir.dt.bfloat16
I32 = mybir.dt.int32
AF = mybir.ActivationFunctionType
ALU = mybir.AluOpType
AX = mybir.AxisListType.X

D = 1024
TW = 512
NT = 8
SUB = 128
NSUB = 4
H, DKH, DVH = 4, 128, 256
EPS = 1e-5
CRND = float(3 * (2 ** 22))
LNSCALE = float(np.log(np.float32(np.float32(128.0) ** np.float32(-0.5))))
BF = ml_dtypes.bfloat16

_CACHED = {}
DBG_TILES = 8
DBG_PHASE = 99
DBG_PREP = 99
DBG_TAPS = ()


class Sem:
    def __init__(self, handle):
        self.h = handle
        self.n = 0


class B:
    ENG = ("sync", "tensor", "vector", "scalar", "gpsimd")

    def __init__(self, nc, ctx):
        self.nc = nc
        self.ops = {e: [] for e in self.ENG}
        self.sem = {}
        for name in ("dC", "dB2", "dGc", "dStage", "dMr", "dSt",
                     "rs0", "rs1", "rs2", "rs3", "rg0", "rg1", "rg2", "rg3",
                     "sxb0", "sxb1", "sxf", "sst", "stm",
                     "cP", "cV", "cS", "cG"):
            self.sem[name] = Sem(ctx.enter_context(nc.semaphore(name)))
        self._lastw = {}

    def wait(self, eng, sname, val):
        if val <= 0:
            return
        key = (eng, sname)
        if self._lastw.get(key, -1) >= val:
            return
        self._lastw[key] = val
        s = self.sem[sname]
        self.ops[eng].append(lambda e, h=s.h, v=val: e.wait_ge(h, v))

    def emit(self, eng, fn):
        self.ops[eng].append(fn)

    def dma(self, qeng, sname, out_ap, in_ap, **kw):
        s = self.sem[sname]

        def _f(e, o=out_ap, i=in_ap, h=s.h, kw=kw):
            e.dma_start(out=o, in_=i, **kw).then_inc(h, 16)
        self.emit(qeng, _f)
        s.n += 16
        return s.n

    def dmat(self, qeng, sname, out_ap, in_ap):
        s = self.sem[sname]

        def _f(e, o=out_ap, i=in_ap, h=s.h):
            e.dma_start_transpose(o, i).then_inc(h, 16)
        self.emit(qeng, _f)
        s.n += 16
        return s.n

    def mm(self, out, lhsT, rhs, start, stop, inc=False):
        s = self.sem["cP"]
        if inc:
            def _f(t, o=out, l=lhsT, r=rhs, a=start, b=stop, h=s.h):
                t.matmul(o, l, r, start=a, stop=b).then_inc(h, 1)
            self.emit("tensor", _f)
            s.n += 1
            return s.n

        def _g(t, o=out, l=lhsT, r=rhs, a=start, b=stop):
            t.matmul(o, l, r, start=a, stop=b)
        self.emit("tensor", _g)
        return s.n

    def trans(self, out, in_, ident, inc=False):
        s = self.sem["cP"]
        if inc:
            def _f(t, o=out, i=in_, d=ident, h=s.h):
                t.transpose(o, i, d).then_inc(h, 1)
            self.emit("tensor", _f)
            s.n += 1
            return s.n

        def _g(t, o=out, i=in_, d=ident):
            t.transpose(o, i, d)
        self.emit("tensor", _g)
        return s.n

    def act(self, out, in_, func, bias=0.0, scale=1.0, accum=None, inc=True):
        s = self.sem["cS"]

        def _f(e, o=out, i=in_, f=func, b=bias, sc=scale, a=accum, h=s.h, ic=inc):
            ins = e.activation(o, i, f, bias=b, scale=sc, accum_out=a)
            if ic:
                ins.then_inc(h, 1)
        self.emit("scalar", _f)
        if inc:
            s.n += 1
        return s.n

    def scopy(self, out, in_, inc=True):
        s = self.sem["cS"]

        def _f(e, o=out, i=in_, h=s.h, ic=inc):
            ins = e.copy(o, i)
            if ic:
                ins.then_inc(h, 1)
        self.emit("scalar", _f)
        if inc:
            s.n += 1
        return s.n

    def v(self, name, *args, inc=True, **kw):
        s = self.sem["cV"]

        def _f(e, nm=name, a=args, k=kw, h=s.h, ic=inc):
            ins = getattr(e, nm)(*a, **k)
            if ic:
                ins.then_inc(h, 1)
        self.emit("vector", _f)
        if inc:
            s.n += 1
        return s.n

    def g(self, name, *args, inc=True, **kw):
        s = self.sem["cG"]

        def _f(e, nm=name, a=args, k=kw, h=s.h, ic=inc):
            ins = getattr(e, nm)(*a, **k)
            if ic:
                ins.then_inc(h, 1)
        self.emit("gpsimd", _f)
        if inc:
            s.n += 1
        return s.n

    def fence(self, qeng, dsem_name, dval):
        self.wait(qeng, dsem_name, dval)
        st = self.sem["stm"]

        def _f(e, h=st.h):
            e.sem_inc(h, 1)
        self.emit(qeng, _f)
        st.n += 1
        return st.n

    def pe_mark(self):
        s = self.sem["cP"]

        def _f(t, h=s.h):
            t.nop().then_inc(h, 1)
        self.emit("tensor", _f)
        s.n += 1
        return s.n


def magic_rsqrt(bb, out, inp, t1, t2, eps):
    cV = bb.sem["cV"].n
    bb.wait("scalar", "cV", cV)
    actc = bb._actc
    p = inp.shape[0]
    lnr = bb.act(t1, inp, AF.Ln, bias=actc[0:p, 6:7], scale=1.0)
    bb.wait("scalar", "cS", lnr)
    r = bb.act(out, t1, AF.Exp, bias=0.0, scale=actc[0:p, 7:8])
    bb.wait("vector", "cS", r)
    return r


def build_program():
    nc = bass.Bass("TRN2", target_bir_lowering=False, detect_race_conditions=False)

    def inp(name, shape, dt=F32):
        return nc.dram_tensor(name, shape, dt, kind="ExternalInput").ap()

    xT = inp("xT", [D, NT * TW])
    c2 = inp("c2", [2, D])
    wmod = inp("wmod", [D, 6 * D])
    wq_h = inp("wq", [D, 512]); wk_h = inp("wk", [D, 512])
    wv_h = inp("wv", [D, 1024]); wg_h = inp("wg", [D, 1024]); wo_h = inp("wo", [D, 1024])
    wgk1_h = inp("wgk1", [D, 16]); wgk2_h = inp("wgk2", [16, 512])
    bgk2_h = inp("bgk2", [512]); gnw_h = inp("gnw", [256])
    ln1g_h = inp("ln1g", [D]); ln1b_h = inp("ln1b", [D])
    w1_h = inp("w1", [4096, 1024]); b1_h = inp("b1", [4096])
    ln2g_h = inp("ln2g", [4096]); ln2b_h = inp("ln2b", [4096])
    w2_h = inp("w2", [1024, 4096]); b2_h = inp("b2", [1024])
    identb_h = inp("identb", [128, 128], BF16)
    identf_h = inp("identf", [128, 128], F32)
    triu_h = inp("triu", [128, 128], BF16)
    ones1_h = inp("ones1", [1, 128], BF16)
    onesr_h = inp("onesr", [1, 512], BF16)
    onessc_h = inp("onessc", [128, 128], F32)
    actc_h = inp("actc", [128, 10], F32)

    outT = nc.dram_tensor("outT", [D, NT * TW], F32, kind="ExternalOutput").ap()
    dbgf = nc.dram_tensor("dbgf", [128, 8192], F32, kind="ExternalOutput").ap()
    dbgb = nc.dram_tensor("dbgb", [128, 16384], BF16, kind="ExternalOutput").ap()

    s_wq = nc.dram_tensor("s_wq", [D, 512], BF16).ap()
    s_wk = nc.dram_tensor("s_wk", [D, 512], BF16).ap()
    s_wv = nc.dram_tensor("s_wv", [D, 1024], BF16).ap()
    s_wg = nc.dram_tensor("s_wg", [D, 1024], BF16).ap()
    s_wo = nc.dram_tensor("s_wo", [D, 1024], BF16).ap()
    s_w1q = nc.dram_tensor("s_w1q", [4096, 1024], BF16).ap()
    s_w1qT = nc.dram_tensor("s_w1qT", [1024, 4096], BF16).ap()
    s_w2q = nc.dram_tensor("s_w2q", [1024, 4096], BF16).ap()
    s_w2qT = nc.dram_tensor("s_w2qT", [4096, 1024], BF16).ap()

    ctx = ExitStack()
    e = ctx.enter_context

    def sb(name, shape, dt):
        return e(nc.sbuf_tensor("sb_" + name, shape, dt)).ap()

    identb = sb("identb", [128, 128], BF16)
    identf = sb("identf", [128, 128], F32)
    triu = sb("triu", [128, 128], BF16)
    ones1 = sb("ones1", [1, 128], BF16)
    onesr = sb("onesr", [1, 512], BF16)
    onessc = sb("onessc", [128, 128], F32)
    actc = sb("actc", [128, 10], F32)
    wgk1s = sb("wgk1s", [128, 128], BF16)
    wgk2s = sb("wgk2s", [16, 512], BF16)
    nbgk2 = sb("nbgk2", [128, 4], F32)
    gnwc = sb("gnwc", [128, 2], F32)
    b1row = sb("b1row", [1, 4096], BF16)
    b2row = sb("b2row", [1, 1024], BF16)
    g1B = sb("g1B", [128, 1024], BF16)
    b1B = sb("b1B", [128, 1024], BF16)
    g2B = sb("g2B", [128, 4096], BF16)
    b2B = sb("b2B", [128, 4096], BF16)
    cbB = [sb(f"cbB{s}", [128, 1024], BF16) for s in range(2)]
    sbB = [sb(f"sbB{s}", [128, 1024], BF16) for s in range(2)]
    modrows = sb("modrows", [1, 4096], BF16)
    gcols = sb("gcols", [128, 32], F32)
    rowsb = sb("rowsb", [1, 1024], BF16)
    wmean = sb("wmean", [128, 2], F32)
    wscale = sb("wscale", [128, 2], F32)
    macc = sb("macc", [128, 40], F32)
    colA = sb("colA", [128, 48], F32)
    colB = sb("colB", [128, 16], F32)
    sumP = sb("sumP", [128, 16], F32)
    ygsum = sb("ygsum", [128, 32], F32)
    glall = sb("glall", [128, 16], F32)
    ssqo = sb("ssqo", [128, 16], F32)
    scT = sb("scT", [128, 16], F32)

    ring = [sb(f"ring{i}", [128, 4096], BF16) for i in range(4)]
    ringf = [r.bitcast(F32) for r in ring]

    def cm_src(ap2d):
        return ap2d.rearrange("(c p) j -> p c j", p=128)

    def cm_dst(sb_ap, C):
        return sb_ap.rearrange("p (c j) -> p c j", c=C)

    x_f = sb("x_f", [128, 4096], F32)
    x_b = [sb("x_b0", [128, 4096], BF16)]
    h_b = sb("h_b", [128, 4096], BF16)
    hscr = h_b.bitcast(F32)
    actA = sb("actA", [128, 16384], BF16)
    scrA = sb("scrA", [128, 512], F32)
    scrB = sb("scrB", [128, 512], F32)
    scrAb = scrA.bitcast(BF16)
    kgtm = sb("kgtm", [128, 512], BF16)
    atb = sb("atb", [128, 512], BF16)
    o_n = sb("o_n", [128, 1024], BF16)
    S_f = sb("S_f", [128, 1024], F32)
    S_b = sb("S_b", [128, 1024], BF16)
    ttm4 = sb("ttm4", [128, 4096], BF16)
    dqtm = sb("dqtm", [128, 1024], BF16)
    h2qT = sb("h2qT", [128, 4096], BF16)
    y1qT = sb("y1qT", [128, 8192], BF16)
    stage1 = y1qT[0:1, 0:4096]
    xtm4 = y1qT[:, 0:4096]
    rowsf = y1qT.bitcast(F32)[0:1, 2048:4096]

    qg = actA[:, 0:2048]
    ki = actA[:, 2048:4096]
    GpT = sb("GpT", [128, 2048], F32)
    Gp = GpT[:, :]
    v_tm = actA[:, 8192:12288]
    gsig = actA[:, 12288:16384]
    y1g = [actA[:, s * 4096:(s + 1) * 4096] for s in range(NSUB)]
    modraw = actA[0:2, 0:12288].bitcast(F32)
    prow = actA[0:2, 12288:16384].bitcast(F32)

    pf = [e(nc.psum_tensor(f"pf{i}", [128, 512], F32)).ap() for i in range(6)]
    pb = [e(nc.psum_tensor(f"pbb{i}", [128, 1024], BF16)).ap() for i in range(2)]

    bb = B(nc, ctx)
    bb._actc = actc
    cP, cV, cS, cG = "cP", "cV", "cS", "cG"

    def max_tree(out_col, buf_bf, w):
        while w > 1:
            h = w // 2
            bb.v("tensor_tensor", buf_bf[:, 0:h], buf_bf[:, 0:h], buf_bf[:, h:w],
                 op=ALU.max, inc=False)
            w = h
        return bb.v("tensor_copy", out_col, buf_bf[:, 0:1])

    def add_fold(out_col, buf_f32, w):
        while w > 1:
            h = w // 2
            bb.v("tensor_tensor", buf_f32[:, 0:h], buf_f32[:, 0:h], buf_f32[:, h:w],
                 op=ALU.add, inc=False)
            w = h
        return bb.v("tensor_copy", out_col, buf_f32[:, 0:1])

    def acc_fence():
        return bb.scopy(macc[0:1, 39:40], actc[0:1, 0:1])

    tapn = {"f": 0, "b": 0}
    tap_offsets = {}

    def tap(name, src_ap, wait_sem=None, wait_val=0, bf=False):
        if name not in DBG_TAPS:
            return
        if wait_sem is not None:
            bb.wait("sync", wait_sem, wait_val)
        w = src_ap.shape[1]
        key = "b" if bf else "f"
        dst = (dbgb if bf else dbgf)
        o = tapn[key]
        tapn[key] += w
        tap_offsets[name] = (key, o, w, src_ap.shape[0])
        st = bb.dma("sync", "dSt", dst[0:src_ap.shape[0], o:o + w], src_ap)
        fv = bb.fence("sync", "dSt", st)
        for eng in ("vector", "scalar", "tensor", "gpsimd"):
            bb.wait(eng, "stm", fv)

    slot_rel = [("cP", 0)] * 4
    slot_idx = [0]

    def ring_load(qeng, src_ap, cols=4096, f32view=False):
        si = slot_idx[0] % 4
        slot_idx[0] += 1
        rsem, rval = slot_rel[si]
        bb.wait(qeng, rsem, rval)
        sname = (f"rg{si}" if qeng == "gpsimd" else f"rs{si}")
        dst = ringf[si][:, 0:cols] if f32view else ring[si][:, 0:cols]
        ld = bb.dma(qeng, sname, dst, src_ap)
        return si, sname, ld

    def ring_release(si, sname, val):
        slot_rel[si] = (sname, val)

    for dst, src in ((identb, identb_h), (identf, identf_h), (triu, triu_h),
                     (ones1, ones1_h), (onesr, onesr_h), (onessc, onessc_h),
                     (actc, actc_h)):
        bb.dma("sync", "dC", dst, src)
    bb.dma("sync", "dC", nbgk2, bgk2_h.rearrange("(c p) -> p c", p=128),
           allow_slow_non_contiguous=True)
    bb.dma("sync", "dC", gnwc, gnw_h.rearrange("(c p) -> p c", p=128),
           allow_slow_non_contiguous=True)
    consts_done = bb.dma("sync", "dC", prow[:, 0:1024], c2)

    prep_on = lambda k: DBG_PREP >= k
    bb.dma("gpsimd", "dGc", wgk1s, cm_src(wgk1_h))
    bb.dma("gpsimd", "dGc", wgk2s, wgk2_h)
    bb.dma("gpsimd", "dGc", b1row, b1_h[None, :])
    gk_loaded = bb.dma("gpsimd", "dGc", b2row, b2_h[None, :])

    bb.wait("vector", "dC", consts_done)
    nbg_done = bb.v("tensor_scalar", nbgk2, nbgk2, -1.0, None, op0=ALU.mult)

    if prep_on(1):
        bb.wait("scalar", "dC", consts_done)
        silu_done = bb.act(prow[:, 0:1024], prow[:, 0:1024], AF.Silu)
        prev_ev = 0
        for c in range(8):
            bb.wait("tensor", cS, silu_done)
            if c >= 2:
                bb.wait("tensor", cV, prev_ev)
            pe_t = bb.trans(pf[c % 2][:, 0:2], prow[:, c * 128:(c + 1) * 128],
                            identf[0:2, 0:2], inc=True)
            bb.wait("vector", cP, pe_t)
            prev_ev = bb.v("tensor_copy", scT[:, c * 2:(c + 1) * 2], pf[c % 2][:, 0:2])
        scT_done = prev_ev

    if prep_on(2):
        mod_banks = [pf[2], pf[3], pf[4], pf[5]]
        for ng in range(3):
            for c in range(8):
                si, sn, ld = ring_load("sync",
                                   wmod[c * 128:(c + 1) * 128, ng * 2048:(ng + 1) * 2048],
                                   cols=2048, f32view=True)
                bb.wait("tensor", sn, ld)
                bb.wait("tensor", cV, scT_done)
                for n in range(4):
                    pm = bb.mm(mod_banks[n][0:2, :], scT[:, c * 2:(c + 1) * 2],
                               ringf[si][:, n * 512:(n + 1) * 512],
                               start=(c == 0), stop=(c == 7), inc=(n == 3))
                ring_release(si, cP, pm)
            bb.wait("vector", cP, bb.sem["cP"].n)
            for n in range(4):
                ev = bb.v("tensor_copy",
                          modraw[:, ng * 2048 + n * 512: ng * 2048 + (n + 1) * 512],
                          mod_banks[n][0:2, :])
            bb.wait("tensor", cV, ev)
        mod_rows_done = bb.sem["cV"].n
    tap("scT", scT, "cV", scT_done)
    tap("mod", modraw, "cV", mod_rows_done)

    if prep_on(3):
        sa_r = modraw[:, 0:1024]
        ca_r = modraw[:, 1024:2048]
        ga_r = modraw[:, 2048:3072]
        cb_r = modraw[:, 4096:5120]
        sb_r = modraw[:, 3072:4096]
        gb_r = modraw[:, 5120:6144]
        bb.wait("scalar", cV, mod_rows_done)
        for item_ap in (ga_r, gb_r):
            bb.act(prow[:, 0:1024], item_ap, AF.Sign)
            sgn = bb.act(prow[:, 1024:2048], item_ap, AF.Abs)
            bb.wait("vector", cS, sgn)
            bb.v("tensor_scalar", item_ap, prow[:, 1024:2048], 0.25, None, op0=ALU.max)
            cl_done = bb.v("tensor_tensor", item_ap, item_ap, prow[:, 0:1024], op=ALU.mult)
            bb.wait("scalar", cV, cl_done)
        bb.v("tensor_scalar", prow[:, 0:1024], ca_r, 1.0, None, op0=ALU.add)
        pr_done = bb.v("tensor_scalar", prow[:, 1024:2048], sa_r, -1.0, None,
                       op0=ALU.mult)
        bb.wait("gpsimd", cV, pr_done)
        for sq in range(2):
            modrows_dma = bb.dma("gpsimd", "dMr", modrows[:, sq * 2048:(sq + 1) * 2048],
                                 prow[sq:sq + 1, 0:2048])
        modrows_done = modrows_dma

        prev_ev = 0
        for item_ap, base in ((ga_r, 0), (gb_r, 16)):
            for c in range(8):
                bb.wait("tensor", cV, cl_done if prev_ev == 0 else prev_ev)
                pe_t = bb.trans(pf[0][:, 0:2], item_ap[:, c * 128:(c + 1) * 128],
                                identf[0:2, 0:2], inc=True)
                bb.wait("vector", cP, pe_t)
                prev_ev = bb.v("tensor_copy", gcols[:, base + c * 2:base + c * 2 + 2],
                               pf[0][:, 0:2])
        gcols_done = prev_ev

    if prep_on(4):
        def bcast_fill(dst, src_row_bf, width, rows_ready_sem, rows_ready):
            last = 0
            for n0 in range(0, width, 512):
                w = min(512, width - n0)
                bb.wait("tensor", rows_ready_sem, rows_ready)
                if last:
                    bb.wait("tensor", cV, last)
                pm = bb.mm(pf[1][:, 0:w], ones1, src_row_bf[:, n0:n0 + w],
                           start=True, stop=True, inc=True)
                bb.wait("vector", cP, pm)
                last = bb.v("tensor_copy", dst[:, n0:n0 + w], pf[1][:, 0:w])
            return last

        bb.wait("vector", "dMr", modrows_done)
        ocb_done = bb.v("tensor_scalar", prow[:, 0:1024], cb_r, 1.0, None, op0=ALU.add)
        for s in range(2):
            bb.wait("gpsimd", cV, ocb_done)
            r1 = bb.dma("gpsimd", "dStage", stage1[:, 0:1024], prow[s:s + 1, 0:1024])
            lastb = bcast_fill(cbB[s], stage1[:, 0:1024], 1024, "dStage", r1)
            bb.wait("gpsimd", cV, lastb)
            r2 = bb.dma("gpsimd", "dStage", stage1[:, 0:1024], sb_r[s:s + 1, :])
            lastb = bcast_fill(sbB[s], stage1[:, 0:1024], 1024, "dStage", r2)
            bb.wait("gpsimd", cV, lastb)
        for dst, src_h, width in ((g1B, ln1g_h, 1024), (b1B, ln1b_h, 1024),
                                  (g2B, ln2g_h, 4096), (b2B, ln2b_h, 4096)):
            ld = bb.dma("gpsimd", "dStage", stage1[:, 0:width], src_h[None, :])
            lastb = bcast_fill(dst, stage1[:, 0:width], width, "dStage", ld)
            bb.wait("gpsimd", cV, lastb)

    if prep_on(5):
        def cast_roundtrip(src_ap, dst_ap):
            si, sn, ld = ring_load("gpsimd", cm_src(src_ap))
            bb.wait("sync", sn, ld)
            st = bb.dma("sync", "dSt", cm_src(dst_ap), ring[si])
            fv = bb.fence("sync", "dSt", st)
            ring_release(si, "stm", fv)
            return st

        cast_roundtrip(wq_h, s_wq)
        cast_roundtrip(wk_h, s_wk)
        cast_roundtrip(wv_h[:, 0:512], s_wv[:, 0:512])
        cast_roundtrip(wv_h[:, 512:1024], s_wv[:, 512:1024])
        cast_roundtrip(wg_h[:, 0:512], s_wg[:, 0:512])
        cast_roundtrip(wg_h[:, 512:1024], s_wg[:, 512:1024])
        for half in range(2):
            si, sn, ld = ring_load("gpsimd",
                               cm_src(wo_h[:, half * 512:(half + 1) * 512]))
            bb.wait("vector", sn, ld)
            for c in range(8):
                sc_done = bb.v("tensor_scalar", ring[si][:, c * 512:(c + 1) * 512],
                               ring[si][:, c * 512:(c + 1) * 512],
                               gnwc[:, c % 2:c % 2 + 1], None, op0=ALU.mult)
            bb.wait("sync", cV, sc_done)
            st = bb.dma("sync", "dSt", cm_src(s_wo[:, half * 512:(half + 1) * 512]),
                        ring[si])
            fv = bb.fence("sync", "dSt", st)
            ring_release(si, "stm", fv)

    if prep_on(6):
        def quant(w_hbm, rows, cols, s_q, widx):
            if not hasattr(quant, "store_mark"):
                quant.store_mark = [0, 0, 0, 0]
            tiles = []
            if cols == 1024:
                for li in range(rows // 256):
                    tiles.append((cm_src(w_hbm[li * 256:(li + 1) * 256, :]),
                                  cm_src(s_q[li * 256:(li + 1) * 256, :])))
            else:
                for rc in range(rows // 128):
                    for hf in range(2):
                        tiles.append((w_hbm[rc * 128:(rc + 1) * 128,
                                            hf * 2048:(hf + 1) * 2048],
                                      s_q[rc * 128:(rc + 1) * 128,
                                          hf * 2048:(hf + 1) * 2048]))
            for li, (src, _d) in enumerate(tiles):
                si, sn, ld = ring_load("sync" if li % 2 == 0 else "scalar", src,
                                       cols=2048, f32view=True)
                bb.wait("vector", sn, ld)
                rd = bb.v("tensor_reduce", macc[:, li:li + 1], ringf[si][:, 0:2048],
                          axis=AX, op=ALU.add, apply_absolute_value=True)
                ring_release(si, cV, rd)
            bb.wait("vector", cV, rd)
            red_done = bb.v("tensor_reduce", macc[:, 33:34], macc[:, 0:16],
                            axis=AX, op=ALU.add)
            bb.wait("tensor", cV, red_done)
            tot_pe = bb.trans(pf[0][0:1, 0:128], macc[:, 33:34], identf, inc=True)
            bb.wait("vector", cP, tot_pe)
            m_ = bb.v("tensor_reduce", macc[0:1, 34:35], pf[0][0:1, 0:128], axis=AX,
                      op=ALU.add)
            bb.wait("vector", cV, m_)
            m_ = bb.v("tensor_scalar", macc[0:1, 35:36], macc[0:1, 34:35],
                 1.0 / (rows * cols), 1e-5, op0=ALU.mult, op1=ALU.max)
            bb.wait("vector", cV, m_)
            sc_done = bb.v("reciprocal", macc[0:1, 36:37], macc[0:1, 35:36])
            bb.wait("tensor", cV, sc_done)
            bb.mm(pf[1][:, 0:1], onessc[0:1, :], macc[0:1, 36:37], start=True, stop=True)
            bc_pe = bb.mm(pf[1][:, 1:2], onessc[0:1, :], macc[0:1, 35:36],
                          start=True, stop=True, inc=True)
            bb.wait("vector", cP, bc_pe)
            bb.v("tensor_copy", wscale[:, widx:widx + 1], pf[1][:, 0:1])
            bcast_done = bb.v("tensor_copy", wmean[:, widx:widx + 1], pf[1][:, 1:2])
            last_store = 0
            for li, (src, dst) in enumerate(tiles):
                si, sn, ld = ring_load("sync" if li % 2 == 0 else "gpsimd", src,
                                       cols=2048, f32view=True)
                bb.wait("vector", "sst", quant.store_mark[si])
                bb.wait("scalar", sn, ld)
                bb.wait("scalar", cV, bcast_done)
                bb.act(ringf[si][:, 0:2048], ringf[si][:, 0:2048], AF.Identity,
                       bias=actc[:, 0:1], scale=wscale[:, widx:widx + 1])
                acc_fence()
                bb.wait("vector", cS, bb.sem["cS"].n)
                bb.v("tensor_scalar", actA[:, si * 2048:si * 2048 + 2048],
                     ringf[si][:, 0:2048], CRND, None, op0=ALU.subtract)
                qd = bb.v("tensor_scalar", actA[:, si * 2048:si * 2048 + 2048],
                          actA[:, si * 2048:si * 2048 + 2048],
                          1.0, -1.0, op0=ALU.min, op1=ALU.max)
                bb.wait("scalar", cV, qd)
                last_store = bb.dma("scalar", "sst", dst,
                                    actA[:, si * 2048:si * 2048 + 2048])
                quant.store_mark[si] = last_store
                ring_release(si, cV, qd)
            return last_store

    if prep_on(7):
        q1_done = quant(w1_h, 4096, 1024, s_w1q, 0)
        q2_done = quant(w2_h, 1024, 4096, s_w2q, 1)

    if prep_on(8):
        bb.wait("scalar", "sst", q1_done)
        for c in range(8):
            si = slot_idx[0] % 4
            slot_idx[0] += 1
            rsem, rval = slot_rel[si]
            bb.wait("scalar", rsem, rval)
            ld = bb.dmat("scalar", f"rs{si}", ring[si], s_w1q[:, c * 128:(c + 1) * 128])
            bb.wait("sync", f"rs{si}", ld)
            st = bb.dma("sync", f"rs{si}", s_w1qT[c * 128:(c + 1) * 128, :], ring[si])
            slot_rel[si] = (f"rs{si}", st)
        bb.wait("scalar", "sst", q2_done)
        for hc in range(32):
            si = slot_idx[0] % 4
            slot_idx[0] += 1
            rsem, rval = slot_rel[si]
            bb.wait("scalar", rsem, rval)
            ld = bb.dmat("scalar", f"rs{si}", ring[si][:, 0:1024],
                         s_w2q[:, hc * 128:(hc + 1) * 128])
            bb.wait("sync", f"rs{si}", ld)
            st = bb.dma("sync", f"rs{si}", s_w2qT[hc * 128:(hc + 1) * 128, :],
                        ring[si][:, 0:1024])
            slot_rel[si] = (f"rs{si}", st)

    tap("w1q0", s_w1q[0:128, :], "stm", bb.sem["stm"].n, bf=True)
    tap("w1qT0", s_w1qT[0:128, :], "stm", bb.sem["stm"].n, bf=True)
    bb.wait("sync", "sst", bb.sem["sst"].n)
    for _si in range(4):
        bb.wait("sync", f"rs{_si}", bb.sem[f"rs{_si}"].n)
    all_st = bb.fence("sync", "dSt", bb.sem["dSt"].n)
    for _si in range(4):
        if slot_rel[_si][0] == "dSt":
            slot_rel[_si] = ("stm", all_st)

    def slice_src(idx):
        if idx == 0:
            return cm_src(s_wq)
        if idx == 1:
            return cm_src(s_wk)
        if idx in (2, 3):
            return cm_src(s_wv[:, (idx - 2) * 512:(idx - 1) * 512])
        if idx in (4, 5):
            return cm_src(s_wg[:, (idx - 4) * 512:(idx - 3) * 512])
        if idx in (6, 7):
            return cm_src(s_wo[:, (idx - 6) * 512:(idx - 5) * 512])
        if 8 <= idx < 16:
            nsl = idx - 8
            return cm_src(s_w1qT[:, nsl * 512:(nsl + 1) * 512])
        ec = (idx - 16) % 8
        return cm_src(s_w2qT[:, ec * 128:(ec + 1) * 128])

    def wslice(idx):
        si, sn, ld = ring_load("sync", slice_src(idx))
        bb.wait("tensor", sn, ld)
        return si

    def wslice_end(si):
        ring_release(si, cP, bb.sem["cP"].n)

    out_store_done = {}
    xb_loaded = {}
    xf_loaded = {}
    h_done_prev = [0]

    bb.dma("gpsimd", "sxb0", x_b[0], xT[:, 0:TW].rearrange("(c p) t -> p c t", p=128))
    xb_loaded[0] = ("sxb0", bb.sem["sxb0"].n)
    bb.dma("scalar", "sxf", x_f, xT[:, 0:TW].rearrange("(c p) t -> p c t", p=128))
    xf_loaded[0] = bb.sem["sxf"].n

    for t in range(min(NT, DBG_TILES)):
        seq = t // 4
        col0 = t * TW
        xb = x_b[0]

        if t % 4 == 0:
            bb.v("memset", S_f, 0.0)
            bb.v("memset", S_b, 0.0)
        s_bf_ready = bb.sem["cV"].n

        if DBG_PHASE < 1:
            continue
        bb.wait("scalar", xb_loaded[t][0], xb_loaded[t][1])
        bb.wait("tensor", xb_loaded[t][0], xb_loaded[t][1])
        ssq_pe = {}
        for c in range(8):
            if c >= 2:
                bb.wait("scalar", cP, ssq_pe[c - 2])
            sq = bb.act(scrAb[:, (c % 2) * 512:(c % 2) * 512 + 512],
                        xb[:, c * 512:(c + 1) * 512], AF.Square)
            bb.wait("tensor", cS, sq)
            bb.mm(pf[0][0:1, :], triu[:, 127:128], xb[:, c * 512:(c + 1) * 512],
                  start=(c == 0), stop=(c == 7))
            ssq_pe[c] = bb.mm(pf[1][0:1, :], triu[:, 127:128],
                              scrAb[:, (c % 2) * 512:(c % 2) * 512 + 512],
                              start=(c == 0), stop=(c == 7), inc=True)
        stats_done = bb.sem["cP"].n
        bb.wait("vector", cP, stats_done)
        mu_r = rowsf[:, 0:512]
        ve_r = rowsf[:, 512:1024]
        t1_r = rowsf[:, 1024:1536]
        t2_r = rowsf[:, 1536:2048]
        bb.v("tensor_scalar", mu_r, pf[0][0:1, :], 1.0 / D, None, op0=ALU.mult)
        bb.v("tensor_scalar", ve_r, pf[1][0:1, :], 1.0 / D, None, op0=ALU.mult)
        bb.v("tensor_tensor", t1_r, mu_r, mu_r, op=ALU.mult)
        bb.v("tensor_tensor", ve_r, ve_r, t1_r, op=ALU.subtract)
        magic_rsqrt(bb, ve_r, ve_r, t1_r, t2_r, EPS)
        bb.v("tensor_tensor", t1_r, mu_r, ve_r, op=ALU.mult)
        bb.v("tensor_copy", rowsb[:, 0:512], ve_r)
        rows_bf_done = bb.v("tensor_copy", rowsb[:, 512:1024], t1_r)

        ocar = modrows[:, seq * 2048:seq * 2048 + 1024]
        nsar = modrows[:, seq * 2048 + 1024:seq * 2048 + 2048]
        bb.wait("tensor", "dMr", modrows_done)
        lnapply = {}
        for c in range(8):
            bA = pf[2 + 2 * (c % 2)]
            bB2 = pf[3 + 2 * (c % 2)]
            bb.wait("tensor", cV, rows_bf_done)
            if c >= 2:
                bb.wait("tensor", cV, lnapply[c - 2])
            bb.mm(bA, ocar[:, c * 128:(c + 1) * 128], rowsb[:, 0:512],
                  start=True, stop=True)
            bb.mm(bB2, ocar[:, c * 128:(c + 1) * 128], rowsb[:, 512:1024],
                  start=True, stop=False)
            pe_d = bb.mm(bB2, nsar[:, c * 128:(c + 1) * 128], onesr,
                         start=False, stop=True, inc=True)
            bb.wait("vector", cP, pe_d)
            bb.v("tensor_tensor", h_b[:, c * 512:(c + 1) * 512],
                 xb[:, c * 512:(c + 1) * 512], bA, op=ALU.mult)
            lnapply[c] = bb.v("scalar_tensor_tensor", h_b[:, c * 512:(c + 1) * 512],
                              bB2, -1.0, h_b[:, c * 512:(c + 1) * 512],
                              op0=ALU.mult, op1=ALU.add)
        h_done = bb.sem["cV"].n
        h_done_prev[0] = h_done
        if t + 1 < NT:
            bb.wait("gpsimd", cV, h_done)
            bb.dma("gpsimd", "sxb0", x_b[0],
                   xT[:, (t + 1) * TW:(t + 2) * TW].rearrange("(c p) t -> p c t", p=128))
            xb_loaded[t + 1] = ("sxb0", bb.sem["sxb0"].n)
        if t == 0:
            tap("xb", xb, "cV", h_done, bf=True)
            tap("h", h_b, "cV", h_done, bf=True)
            tap("rows", rowsb, "cV", h_done, bf=True)

        if DBG_PHASE < 2:
            continue
        evac_tick = [0]

        def evac(dst, src_ps, pewait):
            k = evac_tick[0]
            evac_tick[0] += 1
            if k % 2 == 0:
                bb.wait("vector", cP, pewait)
                return ("cV", bb.v("tensor_copy", dst, src_ps))
            bb.wait("scalar", cP, pewait)
            return ("cS", bb.scopy(dst, src_ps))

        def pe_wait(mark):
            bb.wait("tensor", mark[0], mark[1])

        bb.wait("tensor", cV, h_done)
        marks = []
        for which, dstbuf, slc in ((0, qg, 0), (1, ki, 1)):
            si = wslice(slc)
            for m in range(4):
                bank = pf[(which * 2) + m % 2]
                if len(marks) >= 2:
                    pe_wait(marks[-2])
                for c in range(8):
                    pe_d = bb.mm(bank,
                                 ring[si][:, c * 512 + m * 128: c * 512 + (m + 1) * 128],
                                 h_b[:, c * 512:(c + 1) * 512],
                                 start=(c == 0), stop=(c == 7), inc=(c == 7))
                marks.append(evac(dstbuf[:, m * 512:(m + 1) * 512], bank, pe_d))
            wslice_end(si)
        bb.wait("tensor", "dGc", gk_loaded)
        for c in range(8):
            pe_d = bb.mm(pf[4][0:16, :], wgk1s[:, c * 16:(c + 1) * 16],
                         h_b[:, c * 512:(c + 1) * 512],
                         start=(c == 0), stop=(c == 7), inc=(c == 7))
        zmark = evac(atb[0:16, 0:512], pf[4][0:16, :], pe_d)
        pe_wait(zmark)
        bb.v("tensor_tensor_scan", colB[:, 12:14], onessc[:, 0:2], onessc[:, 0:2],
             0.0, op0=ALU.mult, op1=ALU.add, inc=False)
        lgm = {}
        for m in range(4):
            bank = pf[5] if m % 2 == 0 else pf[4]
            if m >= 2:
                bb.wait("tensor", cV, lgm[m - 2])
            pe_d = bb.mm(bank, wgk2s[:, m * 128:(m + 1) * 128], atb[0:16, 0:512],
                         start=True, stop=True, inc=True)
            bb.wait("scalar", cP, pe_d)
            if m >= 1:
                bb.wait("scalar", cV, lgm[m - 1])
            bb.act(scrA, bank, AF.Exp, bias=nbgk2[:, m:m + 1], scale=actc[:, 3:4])
            lg = bb.act(scrB, scrA, AF.Ln, bias=actc[:, 1:2], scale=1.0)
            if t == 0 and m == 0:
                tap("scrA", scrA, "cS", lg)
                tap("scrB", scrB, "cS", lg)
            bb.wait("vector", cS, lg)
            for sg in range(4):
                bb.v("tensor_tensor_scan",
                     Gp[:, m * 512 + sg * 128: m * 512 + (sg + 1) * 128],
                     onessc, scrB[:, sg * 128:(sg + 1) * 128], 0.0,
                     op0=ALU.mult, op1=ALU.add, inc=(sg == 3))
            lgm[m] = bb.sem["cV"].n
        gp_done = bb.sem["cV"].n
        if t == 0:
            tap("gp", Gp, "cV", gp_done)
            tap("qkraw", qg, "cV", gp_done, bf=True)
            tap("vtm", v_tm, "cV", gp_done, bf=True)
        vmarks = []
        for half in range(2):
            si = wslice(2 + half)
            for s in range(NSUB):
                bank = pf[s % 2]
                if len(vmarks) >= 2:
                    pe_wait(vmarks[-2])
                for c in range(8):
                    pe_d = bb.mm(bank,
                                 h_b[:, c * 512 + s * 128: c * 512 + (s + 1) * 128],
                                 ring[si][:, c * 512:(c + 1) * 512],
                                 start=(c == 0), stop=(c == 7), inc=(c == 7))
                vmarks.append(evac(v_tm[:, s * 1024 + half * 512:
                                        s * 1024 + (half + 1) * 512], bank, pe_d))
            wslice_end(si)
        gmarks = []
        for half in range(2):
            si = wslice(4 + half)
            for m in range(4):
                mc = half * 4 + m
                bank = pf[2 + mc % 2]
                if len(gmarks) >= 2:
                    pe_wait(gmarks[-2])
                for c in range(8):
                    pe_d = bb.mm(bank,
                                 ring[si][:, c * 512 + m * 128: c * 512 + (m + 1) * 128],
                                 h_b[:, c * 512:(c + 1) * 512],
                                 start=(c == 0), stop=(c == 7), inc=(c == 7))
                gmarks.append(evac(gsig[:, mc * 512:(mc + 1) * 512], bank, pe_d))
            wslice_end(si)

        if DBG_PHASE < 3:
            continue
        bb.wait("scalar", cV, gp_done)
        qgki = 0
        for m in range(4):
            eq = bb.act(scrA, Gp[:, m * 512:(m + 1) * 512], AF.Exp,
                        scale=actc[:, 4:5], bias=actc[:, 2:3])
            bb.wait("vector", cS, eq)
            bb.v("tensor_tensor", qg[:, m * 512:(m + 1) * 512],
                 qg[:, m * 512:(m + 1) * 512], scrA, op=ALU.mult)
            ek = bb.act(scrB, Gp[:, m * 512:(m + 1) * 512], AF.Exp,
                        scale=actc[:, 5:6], bias=0.0)
            bb.wait("vector", cS, ek)
            qgki = bb.v("tensor_tensor", ki[:, m * 512:(m + 1) * 512],
                        ki[:, m * 512:(m + 1) * 512], scrB, op=ALU.mult)
            bb.wait("scalar", cV, qgki)
        glp = bass.AP(Gp.tensor, Gp.offset + 127,
                      [[2048, 128], [128, 4], [512, 4]])
        glo = bass.AP(glall.tensor, glall.offset, [[16, 128], [4, 4], [1, 4]])
        pk = bb.v("tensor_copy", glo, glp)
        bb.wait("scalar", cV, pk)
        bb.act(glall[:, 0:16], glall[:, 0:16], AF.Exp, scale=actc[:, 4:5], bias=0.0)
        gl_done = bb.sem["cS"].n

        o_pe_prev = 0
        for s in range(NSUB):
            bb.wait("vector", cS, gl_done)
            if s >= 1:
                bb.wait("vector", cP, o_pe_prev)
            for hh in range(H):
                kg_d = bb.v("tensor_scalar", atb[:, hh * 128:(hh + 1) * 128],
                            ki[:, hh * 512 + s * 128: hh * 512 + (s + 1) * 128],
                            glall[:, s * 4 + hh:s * 4 + hh + 1], None, op0=ALU.mult,
                            inc=(hh == H - 1))
            bb.wait("tensor", cV, kg_d)
            for hh in range(H):
                kgt_pe = bb.trans(pb[0][:, hh * 128:(hh + 1) * 128],
                                  atb[:, hh * 128:(hh + 1) * 128], identb,
                                  inc=(hh == H - 1))
            bb.wait("vector", cP, kgt_pe)
            kgtm_d = bb.v("tensor_copy", kgtm, pb[0][:, 0:512])

            atm = {}
            for hh in range(H):
                bank = pf[2 + hh % 2]
                if hh >= 2:
                    bb.wait("tensor", cV, atm[hh - 2])
                at_pe = bb.mm(bank[:, 0:128],
                              ki[:, hh * 512 + s * 128: hh * 512 + (s + 1) * 128],
                              qg[:, hh * 512 + s * 128: hh * 512 + (s + 1) * 128],
                              start=True, stop=True, inc=True)
                bb.wait("vector", cP, at_pe)
                atm[hh] = bb.v("tensor_tensor", atb[:, hh * 128:(hh + 1) * 128],
                               bank[:, 0:128], triu, op=ALU.mult)
            at_done = bb.sem["cV"].n

            bb.wait("tensor", cV, at_done)
            for hh in range(H):
                bank = pf[hh // 2]
                off = (hh % 2) * 256
                bb.mm(bank[:, off:off + 256], atb[:, hh * 128:(hh + 1) * 128],
                      v_tm[:, s * 1024 + hh * 256: s * 1024 + (hh + 1) * 256],
                      start=True, stop=False)
                o_pe = bb.mm(bank[:, off:off + 256],
                             qg[:, hh * 512 + s * 128: hh * 512 + (s + 1) * 128],
                             S_b[:, hh * 256:(hh + 1) * 256], start=False, stop=True,
                             inc=(hh == H - 1))
            o_pe_prev = o_pe
            bb.wait("tensor", cV, kgtm_d)
            for hh in range(H):
                bank = pf[4 + hh // 2]
                off = (hh % 2) * 256
                kv_pe = bb.mm(bank[:, off:off + 256], kgtm[:, hh * 128:(hh + 1) * 128],
                              v_tm[:, s * 1024 + hh * 256: s * 1024 + (hh + 1) * 256],
                              start=True, stop=True, inc=(hh == H - 1))

            bb.wait("scalar", cP, o_pe)
            for hh in range(H):
                bank = pf[hh // 2]
                off = (hh % 2) * 256
                gn = bb.act(scrAb[:, 0:256], bank[:, off:off + 256], AF.Square,
                            accum=ssqo[:, s * 4 + hh:s * 4 + hh + 1])
            gn = acc_fence()
            bb.wait("vector", cS, gn)
            bb.v("tensor_scalar", colA[:, s * 4:(s + 1) * 4],
                 ssqo[:, s * 4:(s + 1) * 4], 1.0 / DVH, None, op0=ALU.mult)
            magic_rsqrt(bb, colA[:, s * 4:(s + 1) * 4], colA[:, s * 4:(s + 1) * 4],
                        colB[:, 0:4], colB[:, 4:8], EPS)
            for hh in range(H):
                bank = pf[hh // 2]
                off = (hh % 2) * 256
                on_d = bb.v("tensor_scalar", o_n[:, hh * 256:(hh + 1) * 256],
                            bank[:, off:off + 256],
                            colA[:, s * 4 + hh:s * 4 + hh + 1], None, op0=ALU.mult,
                            inc=(hh == H - 1))
            bb.wait("vector", cP, kv_pe)
            for hh in range(H):
                bank = pf[4 + hh // 2]
                off = (hh % 2) * 256
                bb.v("tensor_scalar", S_f[:, hh * 256:(hh + 1) * 256],
                     S_f[:, hh * 256:(hh + 1) * 256],
                     glall[:, s * 4 + hh:s * 4 + hh + 1], None, op0=ALU.mult)
                bb.v("tensor_tensor", S_f[:, hh * 256:(hh + 1) * 256],
                     S_f[:, hh * 256:(hh + 1) * 256], bank[:, off:off + 256],
                     op=ALU.add)
            s_bf_ready = bb.v("tensor_copy", S_b, S_f)

            if s == 0:
                for c in range(8):
                    pe_wait_mark = gmarks[c]
                    if pe_wait_mark[0] == "cV":
                        bb.wait("scalar", cV, pe_wait_mark[1])
                    gs = bb.act(gsig[:, c * 512:(c + 1) * 512],
                                gsig[:, c * 512:(c + 1) * 512], AF.Silu)
                gsil_done = gs
                bb.wait("vector", cS, gsil_done)

            bb.wait("tensor", cV, on_d)
            for cblk in range(8):
                ont_pe = bb.trans(pb[1][:, cblk * 128:(cblk + 1) * 128],
                                  o_n[:, cblk * 128:(cblk + 1) * 128], identb,
                                  inc=(cblk == 7))
            bb.wait("vector", cP, ont_pe)
            if s == 0:
                bb.wait("vector", cS, gsil_done)
            og_src = bass.AP(pb[1].tensor, pb[1].offset, [[1024, 128], [128, 8], [1, 128]])
            og_gs = bass.AP(gsig.tensor, gsig.offset + s * 128,
                            [[16384, 128], [512, 8], [1, 128]])
            og_gs2 = bass.AP(gsig.tensor, gsig.offset + s * 128,
                             [[16384, 128], [512, 8], [1, 128]])
            og_d = bb.v("tensor_tensor", og_gs, og_src, og_gs2, op=ALU.mult)
            bb.wait("tensor", cV, og_d)
        og_all = bb.sem["cV"].n

        if DBG_PHASE < 4:
            continue
        womarks = []
        bb.wait("tensor", cV, og_all)
        for half in range(2):
            si = wslice(6 + half)
            for m in range(4):
                mc = half * 4 + m
                bank = pf[mc % 2]
                if len(womarks) >= 2:
                    bb.wait("tensor", cV, womarks[-2])
                for c in range(8):
                    wo_pe = bb.mm(bank,
                                  ring[si][:, c * 512 + m * 128: c * 512 + (m + 1) * 128],
                                  gsig[:, c * 512:(c + 1) * 512],
                                  start=(c == 0), stop=(c == 7), inc=(c == 7))
                bb.wait("vector", cP, wo_pe)
                bb.wait("vector", "sxf", xf_loaded[t])
                womarks.append(bb.v("scalar_tensor_tensor",
                                    x_f[:, mc * 512:(mc + 1) * 512], bank,
                                    gcols[:, mc * 2 + seq:mc * 2 + seq + 1],
                                    x_f[:, mc * 512:(mc + 1) * 512],
                                    op0=ALU.mult, op1=ALU.add))
            wslice_end(si)
        x1_done = bb.sem["cV"].n
        if t == 0:
            tap("x1", x_f, "cV", x1_done)
            tap("og", gsig, "cV", x1_done, bf=True)
            tap("ssqo", ssqo, "cV", x1_done)
            tap("glall", glall, "cV", x1_done)
            tap("colA", colA, "cV", x1_done)
            tap("Sf", S_f, "cV", x1_done)
            tap("on", o_n, "cV", x1_done, bf=True)

        if DBG_PHASE < 5:
            continue
        evm = {}
        xcm = {}
        for s in range(NSUB):
            xc_src = bass.AP(x_f.tensor, x_f.offset + s * 128,
                             [[4096, 128], [512, 8], [1, 128]])
            xcm[s] = bb.v("tensor_copy", ttm4[:, s * 1024:(s + 1) * 1024], xc_src)
        for s in range(NSUB):
            bb.wait("tensor", cV, xcm[s])
            if s >= 2:
                bb.wait("tensor", cV, evm[s - 2])
            for blk in range(8):
                x1t_pe = bb.trans(pb[s % 2][:, blk * 128:(blk + 1) * 128],
                                  ttm4[:, s * 1024 + blk * 128:
                                       s * 1024 + (blk + 1) * 128],
                                  identb, inc=(blk == 7))
            bb.wait("vector", cP, x1t_pe)
            evm[s] = bb.v("tensor_copy", xtm4[:, s * 1024:(s + 1) * 1024],
                          pb[s % 2][:, 0:1024])
        for s in range(NSUB):
            bb.wait("scalar", cV, evm[s])
            bb.act(scrAb[:, 0:1024], xtm4[:, s * 1024:(s + 1) * 1024], AF.Square,
                   accum=sumP[:, 0 + s:1 + s])
            bb.act(scrAb[:, 0:1024], xtm4[:, s * 1024:(s + 1) * 1024], AF.Copy,
                   accum=sumP[:, 4 + s:5 + s])
        ssq2 = acc_fence()
        bb.wait("vector", cS, ssq2)
        bb.v("tensor_scalar", colA[:, 0:4], sumP[:, 4:8], 1.0 / D, None,
             op0=ALU.mult)
        m_ = bb.v("tensor_scalar", colA[:, 4:8], sumP[:, 0:4], 1.0 / D, None,
                  op0=ALU.mult)
        bb.wait("vector", cV, m_)
        m_ = bb.v("tensor_tensor", colB[:, 8:12], colA[:, 0:4], colA[:, 0:4],
                  op=ALU.mult)
        bb.wait("vector", cV, m_)
        bb.v("tensor_tensor", colA[:, 4:8], colA[:, 4:8], colB[:, 8:12],
             op=ALU.subtract)
        magic_rsqrt(bb, colA[:, 4:8], colA[:, 4:8], colB[:, 8:12],
                    colB[:, 12:16], EPS)
        h2m = {}
        for s in range(NSUB):
            sl = slice(s * 1024, (s + 1) * 1024)
            bb.v("tensor_scalar", ttm4[:, sl], xtm4[:, sl], colA[:, s:s + 1],
                 colA[:, 4 + s:5 + s], op0=ALU.subtract, op1=ALU.mult)
            bb.v("tensor_tensor", ttm4[:, sl], ttm4[:, sl], cbB[seq], op=ALU.mult)
            h2m[s] = bb.v("scalar_tensor_tensor", ttm4[:, sl], ttm4[:, sl], 1.0,
                          sbB[seq], op0=ALU.mult, op1=ALU.add)
        for s in range(NSUB):
            bb.wait("scalar", cV, h2m[s])
            bb.act(scrAb[:, 0:1024], ttm4[:, s * 1024:(s + 1) * 1024], AF.Square,
                   accum=sumP[:, 8 + s:9 + s])
            bb.act(scrAb[:, 0:1024], ttm4[:, s * 1024:(s + 1) * 1024], AF.Copy,
                   accum=sumP[:, 12 + s:13 + s])
        s3 = acc_fence()
        bb.wait("vector", cS, s3)
        bb.v("tensor_scalar", colA[:, 8:12], sumP[:, 12:16], 1.0 / D, None,
             op0=ALU.mult)
        m_ = bb.v("tensor_scalar", colA[:, 12:16], sumP[:, 8:12], 1.0 / D, None,
                  op0=ALU.mult)
        bb.wait("vector", cV, m_)
        m_ = bb.v("tensor_tensor", colB[:, 8:12], colA[:, 8:12], colA[:, 8:12],
                  op=ALU.mult)
        bb.wait("vector", cV, m_)
        bb.v("tensor_tensor", colA[:, 12:16], colA[:, 12:16], colB[:, 8:12],
             op=ALU.subtract)
        magic_rsqrt(bb, colA[:, 12:16], colA[:, 12:16], colB[:, 8:12],
                    colB[:, 12:16], 1e-5)
        h3m = {}
        for s in range(NSUB):
            sl = slice(s * 1024, (s + 1) * 1024)
            bb.v("tensor_scalar", ttm4[:, sl], ttm4[:, sl], colA[:, 8 + s:9 + s],
                 colA[:, 12 + s:13 + s], op0=ALU.subtract, op1=ALU.mult)
            bb.v("tensor_tensor", ttm4[:, sl], ttm4[:, sl], g1B, op=ALU.mult)
            h3m[s] = bb.v("tensor_tensor", ttm4[:, sl], ttm4[:, sl], b1B, op=ALU.add)
        for pair in ((0, 1), (2, 3)):
            bb.wait("scalar", cV, h3m[pair[1]])
            for k, s in enumerate(pair):
                bb.act(hscr[:, k * 1024:k * 1024 + 1024],
                       ttm4[:, s * 1024:(s + 1) * 1024], AF.Abs)
            absm = acc_fence()
            bb.wait("vector", cS, absm)
            for k, s in enumerate(pair):
                bb.v("reduce_max", ssqo[:, 2 * s:2 * s + 1],
                     hscr[:, k * 1024:k * 1024 + 512], axis=AX)
                bb.v("reduce_max", ssqo[:, 2 * s + 1:2 * s + 2],
                     hscr[:, k * 1024 + 512:k * 1024 + 1024], axis=AX)
        m_ = bb.sem["cV"].n
        bb.wait("vector", cV, m_)
        for s in range(NSUB):
            mxd = bb.v("scalar_tensor_tensor", colA[:, 16 + s:17 + s],
                       ssqo[:, 2 * s:2 * s + 1], 1e-5, ssqo[:, 2 * s + 1:2 * s + 2],
                       op0=ALU.max, op1=ALU.max)
        bb.wait("scalar", cV, mxd)
        lnr = bb.act(colB[:, 8:12], colA[:, 16:20], AF.Ln)
        bb.wait("scalar", cS, lnr)
        sxp = bb.act(colA[:, 20:24], colB[:, 8:12], AF.Exp,
                     scale=actc[:, 3:4], bias=actc[:, 8:9])
        bb.wait("vector", cS, sxp)
        bb.v("tensor_scalar", colB[:, 12:16], colA[:, 16:20], wmean[:, 0:1],
             1.0 / 127.0, op0=ALU.mult, op1=ALU.mult)
        for s in range(NSUB):
            sl = slice(s * 1024, (s + 1) * 1024)
            bb.v("tensor_scalar", hscr[:, 0:1024], ttm4[:, sl], colA[:, 20 + s:21 + s],
                 CRND, op0=ALU.mult, op1=ALU.add)
            dq_d = bb.v("tensor_scalar", dqtm, hscr[:, 0:1024], CRND,
                        colB[:, 12 + s:13 + s], op0=ALU.subtract, op1=ALU.mult)
            if t == 0 and s == 0:
                tap("midcols", colA, "cV", dq_d)
                tap("h3", ttm4, "cV", dq_d, bf=True)
                tap("dq0", dqtm, "cV", dq_d, bf=True)
            bb.wait("tensor", cV, dq_d)
            for cblk in range(8):
                dqt_pe = bb.trans(pb[s % 2][:, cblk * 128:(cblk + 1) * 128],
                                  dqtm[:, cblk * 128:(cblk + 1) * 128], identb,
                                  inc=(cblk == 7))
            bb.wait("vector", cP, dqt_pe)
            hq_src = bass.AP(pb[s % 2].tensor, pb[s % 2].offset,
                             [[1024, 128], [128, 8], [1, 128]])
            hq_dst = bass.AP(h2qT.tensor, h2qT.offset + s * 128,
                             [[4096, 128], [512, 8], [1, 128]])
            hq_d = bb.v("tensor_copy", hq_dst, hq_src)
            bb.wait("tensor", cV, hq_d)
        h2qT_done = bb.sem["cV"].n
        if t == 0:
            tap("h2qT", h2qT, "cV", h2qT_done, bf=True)

        if DBG_PHASE < 6:
            continue
        gelum = {}
        for nsl in range(8):
            si = wslice(8 + nsl)
            if nsl == 0:
                bb.wait("tensor", cV, h2qT_done)
                if t == 0:
                    tap("w1slot", ring[si], f"rs{si}", bb.sem[f"rs{si}"].n, bf=True)
                    tap("b1row", b1row, None, 0, bf=True)
            for s in range(NSUB):
                bank = pf[s]
                if nsl >= 1:
                    bb.wait("tensor", cS, gelum[(nsl - 1, s)])
                bb.mm(bank, ones1, b1row[:, nsl * 512:(nsl + 1) * 512],
                      start=True, stop=False)
                for c in range(8):
                    y1_pe = bb.mm(bank,
                                  h2qT[:, c * 512 + s * 128: c * 512 + (s + 1) * 128],
                                  ring[si][:, c * 512:(c + 1) * 512],
                                  start=False, stop=(c == 7), inc=(c == 7))
                bb.wait("scalar", cP, y1_pe)
                if t == 0 and nsl == 0 and s == 0 and "y1pe" in DBG_TAPS:
                    cpm = bb.scopy(scrA[:, 0:512], bank)
                    tap("y1pe", scrA, "cS", cpm)
                gelum[(nsl, s)] = bb.act(y1g[s][:, nsl * 512:(nsl + 1) * 512], bank,
                                         AF.Gelu,
                                         accum=ygsum[:, s * 8 + nsl:s * 8 + nsl + 1])
            wslice_end(si)
            if nsl % 2 == 1:
                q4 = nsl // 2
                for s in range(NSUB):
                    bb.act(scrAb[:, 0:1024], y1g[s][:, q4 * 1024:(q4 + 1) * 1024],
                           AF.Square, accum=sumP[:, s * 4 + q4:s * 4 + q4 + 1],
                           inc=(s == 3))

        if DBG_PHASE < 7:
            continue
        if t == 0:
            if "y1g0" in DBG_TAPS:
                bb.wait("sync", "cS", gelum[(7, 0)])
            tap("y1g0", y1g[0], None, 0, bf=True)
        ttm4f = ttm4.bitcast(F32)
        bb.wait("scalar", cS, gelum[(7, 3)])
        acc_fence()
        bb.wait("vector", cS, bb.sem["cS"].n)
        for s in range(NSUB):
            bb.v("tensor_reduce", colA[:, 24 + s:25 + s],
                 ygsum[:, s * 8:(s + 1) * 8], axis=AX, op=ALU.add)
            bb.v("tensor_reduce", colA[:, 28 + s:29 + s],
                 sumP[:, s * 4:(s + 1) * 4], axis=AX, op=ALU.add)
        bb.wait("vector", cV, bb.sem["cV"].n)
        bb.v("tensor_scalar", colA[:, 24:28], colA[:, 24:28], 1.0 / 4096.0, None,
             op0=ALU.mult)
        m_ = bb.v("tensor_scalar", colA[:, 28:32], colA[:, 28:32], 1.0 / 4096.0,
                  None, op0=ALU.mult)
        bb.wait("vector", cV, m_)
        m_ = bb.v("tensor_tensor", colB[:, 8:12], colA[:, 24:28], colA[:, 24:28],
                  op=ALU.mult)
        bb.wait("vector", cV, m_)
        bb.v("tensor_tensor", colA[:, 28:32], colA[:, 28:32], colB[:, 8:12],
             op=ALU.subtract)
        magic_rsqrt(bb, colA[:, 28:32], colA[:, 28:32], colB[:, 8:12],
                    colB[:, 12:16], 1e-5)
        h4m = {}
        for s in range(NSUB):
            bb.v("tensor_scalar", y1g[s], y1g[s], colA[:, 24 + s:25 + s],
                 colA[:, 28 + s:29 + s], op0=ALU.subtract, op1=ALU.mult)
            bb.v("tensor_tensor", y1g[s], y1g[s], g2B, op=ALU.mult)
            h4m[s] = bb.v("tensor_tensor", y1g[s], y1g[s], b2B, op=ALU.add)
        w2marks = []
        for half in range(2):
            scr0 = hscr if half == 0 else GpT
            for s in (2 * half, 2 * half + 1):
                bb.wait("scalar", cV, h4m[s])
                bb.act(scr0[:, 0:1024], y1g[s][:, 0:1024], AF.Abs)
                bb.act(scr0[:, 1024:2048], y1g[s][:, 1024:2048], AF.Abs)
                bb.act(ttm4f[:, 0:1024], y1g[s][:, 2048:3072], AF.Abs)
                a4 = bb.act(ttm4f[:, 1024:2048], y1g[s][:, 3072:4096], AF.Abs)
                bb.wait("vector", cS, a4)
                for q4, src in enumerate((scr0[:, 0:1024], scr0[:, 1024:2048],
                                          ttm4f[:, 0:1024], ttm4f[:, 1024:2048])):
                    rm = bb.v("reduce_max", ssqo[:, s * 4 + q4:s * 4 + q4 + 1], src,
                              axis=AX)
                bb.wait("scalar", cV, rm)
            bb.wait("vector", cV, bb.sem["cV"].n)
            for s in (2 * half, 2 * half + 1):
                bb.v("scalar_tensor_tensor", colB[:, 8 + s:9 + s],
                     ssqo[:, s * 4:s * 4 + 1], 1e-5, ssqo[:, s * 4 + 1:s * 4 + 2],
                     op0=ALU.max, op1=ALU.max)
                m_ = bb.v("scalar_tensor_tensor", colB[:, 12 + s:13 + s],
                          ssqo[:, s * 4 + 2:s * 4 + 3], 1e-5,
                          ssqo[:, s * 4 + 3:s * 4 + 4], op0=ALU.max, op1=ALU.max)
            bb.wait("vector", cV, m_)
            h2 = 2 * half
            mxd = bb.v("tensor_tensor", colA[:, 32 + h2:34 + h2],
                       colB[:, 8 + h2:10 + h2], colB[:, 12 + h2:14 + h2],
                       op=ALU.max)
            bb.wait("scalar", cV, mxd)
            lnr = bb.act(colB[:, 8 + h2:10 + h2], colA[:, 32 + h2:34 + h2], AF.Ln)
            bb.wait("scalar", cS, lnr)
            sxp = bb.act(colA[:, 36 + h2:38 + h2], colB[:, 8 + h2:10 + h2], AF.Exp,
                         scale=actc[:, 3:4], bias=actc[:, 8:9])
            bb.wait("vector", cS, sxp)
            ks2m = bb.v("tensor_scalar", colA[:, 40 + h2:42 + h2],
                        colA[:, 32 + h2:34 + h2], wmean[:, 1:2],
                        1.0 / 127.0, op0=ALU.mult, op1=ALU.mult)
            if t == 0 and half == 0:
                tap("jcols", colA, "cV", ks2m)
            for s in (2 * half, 2 * half + 1):
                for hf in range(4):
                    bb.v("tensor_scalar", hscr[:, (hf % 2) * 1024:(hf % 2) * 1024 + 1024],
                         y1g[s][:, hf * 1024:(hf + 1) * 1024],
                         colA[:, 36 + s:37 + s], CRND, op0=ALU.mult, op1=ALU.add)
                    dq2_d = bb.v("tensor_scalar",
                                 y1g[s][:, hf * 1024:(hf + 1) * 1024],
                                 hscr[:, (hf % 2) * 1024:(hf % 2) * 1024 + 1024], CRND,
                                 colA[:, 40 + s:41 + s],
                                 op0=ALU.subtract, op1=ALU.mult)
                bb.wait("tensor", cV, dq2_d)
                for grp in range(4):
                    for blk in range(8):
                        hcix = grp * 8 + blk
                        tp = bb.trans(pb[grp % 2][:, blk * 128:(blk + 1) * 128],
                                      y1g[s][:, hcix * 128:(hcix + 1) * 128], identb,
                                      inc=(blk == 7))
                    bb.wait("vector", cP, tp)
                    yt_src = bass.AP(pb[grp % 2].tensor, pb[grp % 2].offset,
                                     [[1024, 128], [128, 8], [1, 128]])
                    yt_dst = bass.AP(y1qT.tensor,
                                     y1qT.offset + grp * 8 * 256 + (s % 2) * 128,
                                     [[8192, 128], [256, 8], [1, 128]])
                    yt_d = bb.v("tensor_copy", yt_dst, yt_src)
                    bb.wait("tensor", cV, yt_d)
            y1qT_half = bb.sem["cV"].n
            if t == 0 and half == 0:
                tap("y1qT", y1qT, "cV", y1qT_half, bf=True)

            for ec in range(8):
                si = wslice(16 + half * 8 + ec)
                if ec == 0:
                    bb.wait("tensor", cV, y1qT_half)
                bank = pf[4 + ec % 2]
                if len(w2marks) >= 2:
                    bb.wait("tensor", cV, w2marks[-2])
                bb.mm(bank[:, 0:256], b2row[:, ec * 128:(ec + 1) * 128],
                      onesr[:, 0:256], start=True, stop=False)
                for hc in range(32):
                    y2_pe = bb.mm(bank[:, 0:256],
                                  ring[si][:, hc * 128:(hc + 1) * 128],
                                  y1qT[:, hc * 256:(hc + 1) * 256],
                                  start=False, stop=(hc == 31), inc=(hc == 31))
                bb.wait("vector", cP, y2_pe)
                w2marks.append(bb.v("scalar_tensor_tensor",
                                    x_f[:, ec * 512 + half * 256:
                                        ec * 512 + half * 256 + 256],
                                    bank[:, 0:256],
                                    gcols[:, 16 + ec * 2 + seq:16 + ec * 2 + seq + 1],
                                    x_f[:, ec * 512 + half * 256:
                                        ec * 512 + half * 256 + 256],
                                    op0=ALU.mult, op1=ALU.add))
                wslice_end(si)
        x2_done = bb.sem["cV"].n

        bb.wait("scalar", cV, x2_done)
        out_store_done[t] = bb.dma("scalar", "sst",
                                   outT[:, col0:col0 + TW]
                                   .rearrange("(c p) t -> p c t", p=128), x_f)
        if t + 1 < NT:
            bb.wait("scalar", "sst", out_store_done[t])
            bb.dma("scalar", "sxf", x_f,
                   xT[:, (t + 1) * TW:(t + 2) * TW]
                   .rearrange("(c p) t -> p c t", p=128))
            xf_loaded[t + 1] = bb.sem["sxf"].n

    if out_store_done:
        bb.wait("scalar", "sst", out_store_done[max(out_store_done)])
    else:
        mk = bb.dma("scalar", "sst", outT[:, 0:TW].rearrange("(c p) t -> p c t", p=128),
                    x_f)
        bb.wait("scalar", "sst", mk)

    for _sn in ("dC", "dSt", "rs0", "rs1", "rs2", "rs3"):
        bb.wait("sync", _sn, bb.sem[_sn].n)
    for _sn in ("rg0", "rg1", "rg2", "rg3", "dGc", "dStage", "dMr", "sxb0", "sxb1"):
        bb.wait("gpsimd", _sn, bb.sem[_sn].n)
    for _sn in ("sxf", "sst"):
        bb.wait("scalar", _sn, bb.sem[_sn].n)

    with nc.Block() as block:
        @block.sync
        def _(eng):
            for fn in bb.ops["sync"]:
                fn(eng)

        @block.tensor
        def _(eng):
            for fn in bb.ops["tensor"]:
                fn(eng)

        @block.vector
        def _(eng):
            for fn in bb.ops["vector"]:
                fn(eng)

        @block.scalar
        def _(eng):
            for fn in bb.ops["scalar"]:
                fn(eng)

        @block.gpsimd
        def _(eng):
            for fn in bb.ops["gpsimd"]:
                fn(eng)

    ctx.close()
    nc._tap_offsets = tap_offsets
    return nc


def kernel(**inputs):
    if "prog" not in _CACHED:
        _CACHED["prog"] = build_program()
    nc = _CACHED["prog"]

    x = np.ascontiguousarray(np.asarray(inputs["L0_attn_x"], np.float32))
    cfull = np.ascontiguousarray(np.asarray(inputs["L0_c"], np.float32))

    shared = dict(
        wmod=np.ascontiguousarray(inputs["L0_wmod"], np.float32),
        wq=np.ascontiguousarray(inputs["L0_attn_Wq"], np.float32),
        wk=np.ascontiguousarray(inputs["L0_attn_Wk"], np.float32),
        wv=np.ascontiguousarray(inputs["L0_attn_Wv"], np.float32),
        wg=np.ascontiguousarray(inputs["L0_attn_Wg"], np.float32),
        wo=np.ascontiguousarray(inputs["L0_attn_Wo"], np.float32),
        wgk1=np.ascontiguousarray(inputs["L0_attn_Wgk1"], np.float32),
        wgk2=np.ascontiguousarray(inputs["L0_attn_Wgk2"], np.float32),
        bgk2=np.ascontiguousarray(inputs["L0_attn_bgk2"], np.float32),
        gnw=np.ascontiguousarray(inputs["L0_attn_gnw"], np.float32),
        ln1g=np.ascontiguousarray(inputs["L0_ffn_ln1_g"], np.float32),
        ln1b=np.ascontiguousarray(inputs["L0_ffn_ln1_b"], np.float32),
        w1=np.ascontiguousarray(inputs["L0_ffn_w1"], np.float32),
        b1=np.ascontiguousarray(inputs["L0_ffn_b1"], np.float32),
        ln2g=np.ascontiguousarray(inputs["L0_ffn_ln2_g"], np.float32),
        ln2b=np.ascontiguousarray(inputs["L0_ffn_ln2_b"], np.float32),
        w2=np.ascontiguousarray(inputs["L0_ffn_w2"], np.float32),
        b2=np.ascontiguousarray(inputs["L0_ffn_b2"], np.float32),
        identb=np.eye(128, dtype=BF),
        identf=np.eye(128, dtype=np.float32),
        triu=np.triu(np.ones((128, 128), np.float32)).astype(BF),
        ones1=np.ones((1, 128), BF),
        onesr=np.ones((1, 512), BF),
        onessc=np.ones((128, 128), np.float32),
        actc=np.tile(np.array([[CRND, 1.0, LNSCALE, -1.0, -1.0 / 16.0, 1.0 / 16.0,
                                1e-5, -0.5, float(np.log(127.0)), 0.5]],
                              np.float32), (128, 1)),
    )
    in_maps = []
    for core in range(8):
        xTc = np.ascontiguousarray(
            np.concatenate([x[2 * core].T, x[2 * core + 1].T], axis=1))
        m = dict(shared)
        m["xT"] = xTc
        m["c2"] = np.ascontiguousarray(cfull[2 * core:2 * core + 2])
        in_maps.append(m)

    res = run_bass_kernel_spmd(nc, in_maps, core_ids=list(range(8)))
    out = np.empty_like(x)
    for core in range(8):
        oT = res.results[core]["outT"]
        out[2 * core] = oT[:, :2048].T
        out[2 * core + 1] = oT[:, 2048:].T
    return out


# version 15 (final)
# speedup vs baseline: 1.2203x; 1.0305x over previous
import numpy as np
import ml_dtypes
import concourse.bass as bass
import concourse.mybir as mybir
from concourse.bass_isa import ReduceOp
from concourse.bass_utils import run_bass_kernel_spmd
from contextlib import ExitStack

F32 = mybir.dt.float32
BF16 = mybir.dt.bfloat16
I32 = mybir.dt.int32
AF = mybir.ActivationFunctionType
ALU = mybir.AluOpType
AX = mybir.AxisListType.X

D = 1024
TW = 512
NT = 8
SUB = 128
NSUB = 4
H, DKH, DVH = 4, 128, 256
EPS = 1e-5
CRND = float(3 * (2 ** 22))
LNSCALE = float(np.log(np.float32(np.float32(128.0) ** np.float32(-0.5))))
BF = ml_dtypes.bfloat16

_CACHED = {}
DBG_TILES = 8
DBG_PHASE = 99
DBG_PREP = 99
DBG_TAPS = ()


class Sem:
    def __init__(self, handle):
        self.h = handle
        self.n = 0


class B:
    ENG = ("sync", "tensor", "vector", "scalar", "gpsimd")

    def __init__(self, nc, ctx):
        self.nc = nc
        self.ops = {e: [] for e in self.ENG}
        self.sem = {}
        for name in ("dC", "dB2", "dGc", "dStage", "dMr", "dSt",
                     "rs0", "rs1", "rs2", "rs3", "rg0", "rg1", "rg2", "rg3",
                     "sxb0", "sxb1", "sxf", "sst", "stm",
                     "cP", "cV", "cS", "cG"):
            self.sem[name] = Sem(ctx.enter_context(nc.semaphore(name)))
        self._lastw = {}

    def wait(self, eng, sname, val):
        if val <= 0:
            return
        key = (eng, sname)
        if self._lastw.get(key, -1) >= val:
            return
        self._lastw[key] = val
        s = self.sem[sname]
        self.ops[eng].append(lambda e, h=s.h, v=val: e.wait_ge(h, v))

    def emit(self, eng, fn):
        self.ops[eng].append(fn)

    def dma(self, qeng, sname, out_ap, in_ap, **kw):
        s = self.sem[sname]

        def _f(e, o=out_ap, i=in_ap, h=s.h, kw=kw):
            e.dma_start(out=o, in_=i, **kw).then_inc(h, 16)
        self.emit(qeng, _f)
        s.n += 16
        return s.n

    def dmat(self, qeng, sname, out_ap, in_ap):
        s = self.sem[sname]

        def _f(e, o=out_ap, i=in_ap, h=s.h):
            e.dma_start_transpose(o, i).then_inc(h, 16)
        self.emit(qeng, _f)
        s.n += 16
        return s.n

    def mm(self, out, lhsT, rhs, start, stop, inc=False):
        s = self.sem["cP"]
        if inc:
            def _f(t, o=out, l=lhsT, r=rhs, a=start, b=stop, h=s.h):
                t.matmul(o, l, r, start=a, stop=b).then_inc(h, 1)
            self.emit("tensor", _f)
            s.n += 1
            return s.n

        def _g(t, o=out, l=lhsT, r=rhs, a=start, b=stop):
            t.matmul(o, l, r, start=a, stop=b)
        self.emit("tensor", _g)
        return s.n

    def trans(self, out, in_, ident, inc=False):
        s = self.sem["cP"]
        if inc:
            def _f(t, o=out, i=in_, d=ident, h=s.h):
                t.transpose(o, i, d).then_inc(h, 1)
            self.emit("tensor", _f)
            s.n += 1
            return s.n

        def _g(t, o=out, i=in_, d=ident):
            t.transpose(o, i, d)
        self.emit("tensor", _g)
        return s.n

    def act(self, out, in_, func, bias=0.0, scale=1.0, accum=None, inc=True):
        s = self.sem["cS"]

        def _f(e, o=out, i=in_, f=func, b=bias, sc=scale, a=accum, h=s.h, ic=inc):
            ins = e.activation(o, i, f, bias=b, scale=sc, accum_out=a)
            if ic:
                ins.then_inc(h, 1)
        self.emit("scalar", _f)
        if inc:
            s.n += 1
        return s.n

    def scopy(self, out, in_, inc=True):
        s = self.sem["cS"]

        def _f(e, o=out, i=in_, h=s.h, ic=inc):
            ins = e.copy(o, i)
            if ic:
                ins.then_inc(h, 1)
        self.emit("scalar", _f)
        if inc:
            s.n += 1
        return s.n

    def v(self, name, *args, inc=True, **kw):
        s = self.sem["cV"]

        def _f(e, nm=name, a=args, k=kw, h=s.h, ic=inc):
            ins = getattr(e, nm)(*a, **k)
            if ic:
                ins.then_inc(h, 1)
        self.emit("vector", _f)
        if inc:
            s.n += 1
        return s.n

    def g(self, name, *args, inc=True, **kw):
        s = self.sem["cG"]

        def _f(e, nm=name, a=args, k=kw, h=s.h, ic=inc):
            ins = getattr(e, nm)(*a, **k)
            if ic:
                ins.then_inc(h, 1)
        self.emit("gpsimd", _f)
        if inc:
            s.n += 1
        return s.n

    def fence(self, qeng, dsem_name, dval):
        self.wait(qeng, dsem_name, dval)
        st = self.sem["stm"]

        def _f(e, h=st.h):
            e.sem_inc(h, 1)
        self.emit(qeng, _f)
        st.n += 1
        return st.n

    def pe_mark(self):
        s = self.sem["cP"]

        def _f(t, h=s.h):
            t.nop().then_inc(h, 1)
        self.emit("tensor", _f)
        s.n += 1
        return s.n


def magic_rsqrt(bb, out, inp, t1, t2, eps):
    cV = bb.sem["cV"].n
    bb.wait("scalar", "cV", cV)
    actc = bb._actc
    p = inp.shape[0]
    lnr = bb.act(t1, inp, AF.Ln, bias=actc[0:p, 6:7], scale=1.0)
    bb.wait("scalar", "cS", lnr)
    r = bb.act(out, t1, AF.Exp, bias=0.0, scale=actc[0:p, 7:8])
    bb.wait("vector", "cS", r)
    return r


def build_program():
    nc = bass.Bass("TRN2", target_bir_lowering=False, detect_race_conditions=False)

    def inp(name, shape, dt=F32):
        return nc.dram_tensor(name, shape, dt, kind="ExternalInput").ap()

    xT = inp("xT", [D, NT * TW])
    c2 = inp("c2", [2, D])
    wmod = inp("wmod", [D, 6 * D])
    wq_h = inp("wq", [D, 512]); wk_h = inp("wk", [D, 512])
    wv_h = inp("wv", [D, 1024]); wg_h = inp("wg", [D, 1024]); wo_h = inp("wo", [D, 1024])
    wgk1_h = inp("wgk1", [D, 16]); wgk2_h = inp("wgk2", [16, 512])
    bgk2_h = inp("bgk2", [512]); gnw_h = inp("gnw", [256])
    ln1g_h = inp("ln1g", [D]); ln1b_h = inp("ln1b", [D])
    w1_h = inp("w1", [4096, 1024]); b1_h = inp("b1", [4096])
    ln2g_h = inp("ln2g", [4096]); ln2b_h = inp("ln2b", [4096])
    w2_h = inp("w2", [1024, 4096]); b2_h = inp("b2", [1024])
    identb_h = inp("identb", [128, 128], BF16)
    identf_h = inp("identf", [128, 128], F32)
    triu_h = inp("triu", [128, 128], BF16)
    ones1_h = inp("ones1", [1, 128], BF16)
    onesr_h = inp("onesr", [1, 512], BF16)
    onessc_h = inp("onessc", [128, 128], F32)
    actc_h = inp("actc", [128, 10], F32)

    outT = nc.dram_tensor("outT", [D, NT * TW], F32, kind="ExternalOutput").ap()
    dbgf = nc.dram_tensor("dbgf", [128, 8192], F32, kind="ExternalOutput").ap()
    dbgb = nc.dram_tensor("dbgb", [128, 16384], BF16, kind="ExternalOutput").ap()

    s_wq = nc.dram_tensor("s_wq", [D, 512], BF16).ap()
    s_wk = nc.dram_tensor("s_wk", [D, 512], BF16).ap()
    s_wv = nc.dram_tensor("s_wv", [D, 1024], BF16).ap()
    s_wg = nc.dram_tensor("s_wg", [D, 1024], BF16).ap()
    s_wo = nc.dram_tensor("s_wo", [D, 1024], BF16).ap()
    s_w1q = nc.dram_tensor("s_w1q", [4096, 1024], BF16).ap()
    s_w1qT = nc.dram_tensor("s_w1qT", [1024, 4096], BF16).ap()
    s_w2q = nc.dram_tensor("s_w2q", [1024, 4096], BF16).ap()
    s_w2qT = nc.dram_tensor("s_w2qT", [4096, 1024], BF16).ap()

    ctx = ExitStack()
    e = ctx.enter_context

    def sb(name, shape, dt):
        return e(nc.sbuf_tensor("sb_" + name, shape, dt)).ap()

    identb = sb("identb", [128, 128], BF16)
    identf = sb("identf", [128, 128], F32)
    triu = sb("triu", [128, 128], BF16)
    ones1 = sb("ones1", [1, 128], BF16)
    onesr = sb("onesr", [1, 512], BF16)
    onessc = sb("onessc", [128, 128], F32)
    actc = sb("actc", [128, 10], F32)
    wgk1s = sb("wgk1s", [128, 128], BF16)
    wgk2s = sb("wgk2s", [16, 512], BF16)
    nbgk2 = sb("nbgk2", [128, 4], F32)
    gnwc = sb("gnwc", [128, 2], F32)
    b1row = sb("b1row", [1, 4096], BF16)
    b2row = sb("b2row", [1, 1024], BF16)
    g1B = sb("g1B", [128, 1024], BF16)
    b1B = sb("b1B", [128, 1024], BF16)
    g2B = sb("g2B", [128, 4096], BF16)
    b2B = sb("b2B", [128, 4096], BF16)
    cbB = [sb(f"cbB{s}", [128, 1024], BF16) for s in range(2)]
    sbB = [sb(f"sbB{s}", [128, 1024], BF16) for s in range(2)]
    modrows = sb("modrows", [1, 4096], BF16)
    gcols = sb("gcols", [128, 32], F32)
    rowsb = sb("rowsb", [1, 1024], BF16)
    wmean = sb("wmean", [128, 2], F32)
    wscale = sb("wscale", [128, 2], F32)
    macc = sb("macc", [128, 40], F32)
    colA = sb("colA", [128, 48], F32)
    colB = sb("colB", [128, 16], F32)
    sumP = sb("sumP", [128, 16], F32)
    ygsum = sb("ygsum", [128, 32], F32)
    glall = sb("glall", [128, 16], F32)
    ssqo = sb("ssqo", [128, 16], F32)
    scT = sb("scT", [128, 16], F32)

    ring = [sb(f"ring{i}", [128, 4096], BF16) for i in range(4)]
    ringf = [r.bitcast(F32) for r in ring]

    def cm_src(ap2d):
        return ap2d.rearrange("(c p) j -> p c j", p=128)

    def cm_dst(sb_ap, C):
        return sb_ap.rearrange("p (c j) -> p c j", c=C)

    x_f = sb("x_f", [128, 4096], F32)
    x_b = [sb("x_b0", [128, 4096], BF16)]
    h_b = sb("h_b", [128, 4096], BF16)
    hscr = h_b.bitcast(F32)
    actA = sb("actA", [128, 16384], BF16)
    scrA = sb("scrA", [128, 512], F32)
    scrB = sb("scrB", [128, 512], F32)
    scrAb = scrA.bitcast(BF16)
    kgtm = sb("kgtm", [128, 512], BF16)
    atb = sb("atb", [128, 512], BF16)
    o_n = sb("o_n", [128, 1024], BF16)
    S_f = sb("S_f", [128, 1024], F32)
    S_b = sb("S_b", [128, 1024], BF16)
    ttm4 = sb("ttm4", [128, 4096], BF16)
    dqtm = sb("dqtm", [128, 1024], BF16)
    h2qT = sb("h2qT", [128, 4096], BF16)
    y1qT = sb("y1qT", [128, 8192], BF16)
    stage1 = y1qT[0:1, 0:4096]
    xtm4 = y1qT[:, 0:4096]
    rowsf = y1qT.bitcast(F32)[0:1, 2048:4096]

    qg = actA[:, 0:2048]
    ki = actA[:, 2048:4096]
    GpT = sb("GpT", [128, 2048], F32)
    Gp = GpT[:, :]
    v_tm = actA[:, 8192:12288]
    gsig = actA[:, 12288:16384]
    y1g = [actA[:, s * 4096:(s + 1) * 4096] for s in range(NSUB)]
    modraw = actA[0:2, 0:12288].bitcast(F32)
    prow = actA[0:2, 12288:16384].bitcast(F32)

    pf = [e(nc.psum_tensor(f"pf{i}", [128, 512], F32)).ap() for i in range(6)]
    pb = [e(nc.psum_tensor(f"pbb{i}", [128, 1024], BF16)).ap() for i in range(2)]

    bb = B(nc, ctx)
    bb._actc = actc
    cP, cV, cS, cG = "cP", "cV", "cS", "cG"

    def max_tree(out_col, buf_bf, w):
        while w > 1:
            h = w // 2
            bb.v("tensor_tensor", buf_bf[:, 0:h], buf_bf[:, 0:h], buf_bf[:, h:w],
                 op=ALU.max, inc=False)
            w = h
        return bb.v("tensor_copy", out_col, buf_bf[:, 0:1])

    def add_fold(out_col, buf_f32, w):
        while w > 1:
            h = w // 2
            bb.v("tensor_tensor", buf_f32[:, 0:h], buf_f32[:, 0:h], buf_f32[:, h:w],
                 op=ALU.add, inc=False)
            w = h
        return bb.v("tensor_copy", out_col, buf_f32[:, 0:1])

    def acc_fence():
        return bb.scopy(macc[0:1, 39:40], actc[0:1, 0:1])

    tapn = {"f": 0, "b": 0}
    tap_offsets = {}

    def tap(name, src_ap, wait_sem=None, wait_val=0, bf=False):
        if name not in DBG_TAPS:
            return
        if wait_sem is not None:
            bb.wait("sync", wait_sem, wait_val)
        w = src_ap.shape[1]
        key = "b" if bf else "f"
        dst = (dbgb if bf else dbgf)
        o = tapn[key]
        tapn[key] += w
        tap_offsets[name] = (key, o, w, src_ap.shape[0])
        st = bb.dma("sync", "dSt", dst[0:src_ap.shape[0], o:o + w], src_ap)
        fv = bb.fence("sync", "dSt", st)
        for eng in ("vector", "scalar", "tensor", "gpsimd"):
            bb.wait(eng, "stm", fv)

    slot_rel = [("cP", 0)] * 4
    slot_idx = [0]

    def ring_load(qeng, src_ap, cols=4096, f32view=False):
        si = slot_idx[0] % 4
        slot_idx[0] += 1
        rsem, rval = slot_rel[si]
        bb.wait(qeng, rsem, rval)
        sname = (f"rg{si}" if qeng == "gpsimd" else f"rs{si}")
        dst = ringf[si][:, 0:cols] if f32view else ring[si][:, 0:cols]
        ld = bb.dma(qeng, sname, dst, src_ap)
        return si, sname, ld

    def ring_release(si, sname, val):
        slot_rel[si] = (sname, val)

    for dst, src in ((identb, identb_h), (identf, identf_h), (triu, triu_h),
                     (ones1, ones1_h), (onesr, onesr_h), (onessc, onessc_h),
                     (actc, actc_h)):
        bb.dma("sync", "dC", dst, src)
    bb.dma("sync", "dC", nbgk2, bgk2_h.rearrange("(c p) -> p c", p=128),
           allow_slow_non_contiguous=True)
    bb.dma("sync", "dC", gnwc, gnw_h.rearrange("(c p) -> p c", p=128),
           allow_slow_non_contiguous=True)
    consts_done = bb.dma("sync", "dC", prow[:, 0:1024], c2)

    prep_on = lambda k: DBG_PREP >= k
    bb.dma("gpsimd", "dGc", wgk1s, cm_src(wgk1_h))
    bb.dma("gpsimd", "dGc", wgk2s, wgk2_h)
    bb.dma("gpsimd", "dGc", b1row, b1_h[None, :])
    gk_loaded = bb.dma("gpsimd", "dGc", b2row, b2_h[None, :])

    bb.wait("vector", "dC", consts_done)
    nbg_done = bb.v("tensor_scalar", nbgk2, nbgk2, -1.0, None, op0=ALU.mult)

    if prep_on(1):
        bb.wait("scalar", "dC", consts_done)
        silu_done = bb.act(prow[:, 0:1024], prow[:, 0:1024], AF.Silu)
        prev_ev = 0
        for c in range(8):
            bb.wait("tensor", cS, silu_done)
            if c >= 2:
                bb.wait("tensor", cV, prev_ev)
            pe_t = bb.trans(pf[c % 2][:, 0:2], prow[:, c * 128:(c + 1) * 128],
                            identf[0:2, 0:2], inc=True)
            bb.wait("vector", cP, pe_t)
            prev_ev = bb.v("tensor_copy", scT[:, c * 2:(c + 1) * 2], pf[c % 2][:, 0:2])
        scT_done = prev_ev

    if prep_on(2):
        mod_banks = [pf[2], pf[3], pf[4], pf[5]]
        for ng in range(3):
            for c in range(8):
                si, sn, ld = ring_load("sync",
                                   wmod[c * 128:(c + 1) * 128, ng * 2048:(ng + 1) * 2048],
                                   cols=2048, f32view=True)
                bb.wait("tensor", sn, ld)
                bb.wait("tensor", cV, scT_done)
                for n in range(4):
                    pm = bb.mm(mod_banks[n][0:2, :], scT[:, c * 2:(c + 1) * 2],
                               ringf[si][:, n * 512:(n + 1) * 512],
                               start=(c == 0), stop=(c == 7), inc=(n == 3))
                ring_release(si, cP, pm)
            bb.wait("vector", cP, bb.sem["cP"].n)
            for n in range(4):
                ev = bb.v("tensor_copy",
                          modraw[:, ng * 2048 + n * 512: ng * 2048 + (n + 1) * 512],
                          mod_banks[n][0:2, :])
            bb.wait("tensor", cV, ev)
        mod_rows_done = bb.sem["cV"].n
    tap("scT", scT, "cV", scT_done)
    tap("mod", modraw, "cV", mod_rows_done)

    if prep_on(3):
        sa_r = modraw[:, 0:1024]
        ca_r = modraw[:, 1024:2048]
        ga_r = modraw[:, 2048:3072]
        cb_r = modraw[:, 4096:5120]
        sb_r = modraw[:, 3072:4096]
        gb_r = modraw[:, 5120:6144]
        bb.wait("scalar", cV, mod_rows_done)
        for item_ap in (ga_r, gb_r):
            bb.act(prow[:, 0:1024], item_ap, AF.Sign)
            sgn = bb.act(prow[:, 1024:2048], item_ap, AF.Abs)
            bb.wait("vector", cS, sgn)
            bb.v("tensor_scalar", item_ap, prow[:, 1024:2048], 0.25, None, op0=ALU.max)
            cl_done = bb.v("tensor_tensor", item_ap, item_ap, prow[:, 0:1024], op=ALU.mult)
            bb.wait("scalar", cV, cl_done)
        bb.v("tensor_scalar", prow[:, 0:1024], ca_r, 1.0, None, op0=ALU.add)
        pr_done = bb.v("tensor_scalar", prow[:, 1024:2048], sa_r, -1.0, None,
                       op0=ALU.mult)
        bb.wait("gpsimd", cV, pr_done)
        for sq in range(2):
            modrows_dma = bb.dma("gpsimd", "dMr", modrows[:, sq * 2048:(sq + 1) * 2048],
                                 prow[sq:sq + 1, 0:2048])
        modrows_done = modrows_dma

        prev_ev = 0
        for item_ap, base in ((ga_r, 0), (gb_r, 16)):
            for c in range(8):
                bb.wait("tensor", cV, cl_done if prev_ev == 0 else prev_ev)
                pe_t = bb.trans(pf[0][:, 0:2], item_ap[:, c * 128:(c + 1) * 128],
                                identf[0:2, 0:2], inc=True)
                bb.wait("vector", cP, pe_t)
                prev_ev = bb.v("tensor_copy", gcols[:, base + c * 2:base + c * 2 + 2],
                               pf[0][:, 0:2])
        gcols_done = prev_ev

    if prep_on(4):
        def bcast_fill(dst, src_row_bf, width, rows_ready_sem, rows_ready):
            last = 0
            for n0 in range(0, width, 512):
                w = min(512, width - n0)
                bb.wait("tensor", rows_ready_sem, rows_ready)
                if last:
                    bb.wait("tensor", cV, last)
                pm = bb.mm(pf[1][:, 0:w], ones1, src_row_bf[:, n0:n0 + w],
                           start=True, stop=True, inc=True)
                bb.wait("vector", cP, pm)
                last = bb.v("tensor_copy", dst[:, n0:n0 + w], pf[1][:, 0:w])
            return last

        bb.wait("vector", "dMr", modrows_done)
        ocb_done = bb.v("tensor_scalar", prow[:, 0:1024], cb_r, 1.0, None, op0=ALU.add)
        for s in range(2):
            bb.wait("gpsimd", cV, ocb_done)
            r1 = bb.dma("gpsimd", "dStage", stage1[:, 0:1024], prow[s:s + 1, 0:1024])
            lastb = bcast_fill(cbB[s], stage1[:, 0:1024], 1024, "dStage", r1)
            bb.wait("gpsimd", cV, lastb)
            r2 = bb.dma("gpsimd", "dStage", stage1[:, 0:1024], sb_r[s:s + 1, :])
            lastb = bcast_fill(sbB[s], stage1[:, 0:1024], 1024, "dStage", r2)
            bb.wait("gpsimd", cV, lastb)
        for dst, src_h, width in ((g1B, ln1g_h, 1024), (b1B, ln1b_h, 1024),
                                  (g2B, ln2g_h, 4096), (b2B, ln2b_h, 4096)):
            ld = bb.dma("gpsimd", "dStage", stage1[:, 0:width], src_h[None, :])
            lastb = bcast_fill(dst, stage1[:, 0:width], width, "dStage", ld)
            bb.wait("gpsimd", cV, lastb)

    if prep_on(5):
        def cast_roundtrip(src_ap, dst_ap):
            si, sn, ld = ring_load("gpsimd", cm_src(src_ap))
            bb.wait("sync", sn, ld)
            st = bb.dma("sync", "dSt", cm_src(dst_ap), ring[si])
            fv = bb.fence("sync", "dSt", st)
            ring_release(si, "stm", fv)
            return st

        cast_roundtrip(wq_h, s_wq)
        cast_roundtrip(wk_h, s_wk)
        cast_roundtrip(wv_h[:, 0:512], s_wv[:, 0:512])
        cast_roundtrip(wv_h[:, 512:1024], s_wv[:, 512:1024])
        cast_roundtrip(wg_h[:, 0:512], s_wg[:, 0:512])
        cast_roundtrip(wg_h[:, 512:1024], s_wg[:, 512:1024])
        for half in range(2):
            si, sn, ld = ring_load("gpsimd",
                               cm_src(wo_h[:, half * 512:(half + 1) * 512]))
            bb.wait("vector", sn, ld)
            for c in range(8):
                sc_done = bb.v("tensor_scalar", ring[si][:, c * 512:(c + 1) * 512],
                               ring[si][:, c * 512:(c + 1) * 512],
                               gnwc[:, c % 2:c % 2 + 1], None, op0=ALU.mult)
            bb.wait("sync", cV, sc_done)
            st = bb.dma("sync", "dSt", cm_src(s_wo[:, half * 512:(half + 1) * 512]),
                        ring[si])
            fv = bb.fence("sync", "dSt", st)
            ring_release(si, "stm", fv)

    if prep_on(6):
        def quant(w_hbm, rows, cols, s_q, widx):
            if not hasattr(quant, "store_mark"):
                quant.store_mark = [0, 0, 0, 0]
            tiles = []
            if cols == 1024:
                for li in range(rows // 256):
                    tiles.append((cm_src(w_hbm[li * 256:(li + 1) * 256, :]),
                                  cm_src(s_q[li * 256:(li + 1) * 256, :])))
            else:
                for rc in range(rows // 128):
                    for hf in range(2):
                        tiles.append((w_hbm[rc * 128:(rc + 1) * 128,
                                            hf * 2048:(hf + 1) * 2048],
                                      s_q[rc * 128:(rc + 1) * 128,
                                          hf * 2048:(hf + 1) * 2048]))
            for li, (src, _d) in enumerate(tiles):
                si, sn, ld = ring_load("sync" if li % 2 == 0 else "scalar", src,
                                       cols=2048, f32view=True)
                bb.wait("vector", sn, ld)
                rd = bb.v("tensor_reduce", macc[:, li:li + 1], ringf[si][:, 0:2048],
                          axis=AX, op=ALU.add, apply_absolute_value=True)
                ring_release(si, cV, rd)
            bb.wait("vector", cV, rd)
            red_done = bb.v("tensor_reduce", macc[:, 33:34], macc[:, 0:16],
                            axis=AX, op=ALU.add)
            bb.wait("tensor", cV, red_done)
            tot_pe = bb.trans(pf[0][0:1, 0:128], macc[:, 33:34], identf, inc=True)
            bb.wait("vector", cP, tot_pe)
            m_ = bb.v("tensor_reduce", macc[0:1, 34:35], pf[0][0:1, 0:128], axis=AX,
                      op=ALU.add)
            bb.wait("vector", cV, m_)
            m_ = bb.v("tensor_scalar", macc[0:1, 35:36], macc[0:1, 34:35],
                 1.0 / (rows * cols), 1e-5, op0=ALU.mult, op1=ALU.max)
            bb.wait("vector", cV, m_)
            sc_done = bb.v("reciprocal", macc[0:1, 36:37], macc[0:1, 35:36])
            bb.wait("tensor", cV, sc_done)
            bb.mm(pf[1][:, 0:1], onessc[0:1, :], macc[0:1, 36:37], start=True, stop=True)
            bc_pe = bb.mm(pf[1][:, 1:2], onessc[0:1, :], macc[0:1, 35:36],
                          start=True, stop=True, inc=True)
            bb.wait("vector", cP, bc_pe)
            bb.v("tensor_copy", wscale[:, widx:widx + 1], pf[1][:, 0:1])
            bcast_done = bb.v("tensor_copy", wmean[:, widx:widx + 1], pf[1][:, 1:2])
            last_store = 0
            for li, (src, dst) in enumerate(tiles):
                si, sn, ld = ring_load("sync" if li % 2 == 0 else "gpsimd", src,
                                       cols=2048, f32view=True)
                bb.wait("vector", "sst", quant.store_mark[si])
                bb.wait("scalar", sn, ld)
                bb.wait("scalar", cV, bcast_done)
                bb.act(ringf[si][:, 0:2048], ringf[si][:, 0:2048], AF.Identity,
                       bias=actc[:, 0:1], scale=wscale[:, widx:widx + 1])
                acc_fence()
                bb.wait("vector", cS, bb.sem["cS"].n)
                bb.v("tensor_scalar", actA[:, si * 2048:si * 2048 + 2048],
                     ringf[si][:, 0:2048], CRND, None, op0=ALU.subtract)
                qd = bb.v("tensor_scalar", actA[:, si * 2048:si * 2048 + 2048],
                          actA[:, si * 2048:si * 2048 + 2048],
                          1.0, -1.0, op0=ALU.min, op1=ALU.max)
                bb.wait("scalar", cV, qd)
                last_store = bb.dma("scalar", "sst", dst,
                                    actA[:, si * 2048:si * 2048 + 2048])
                quant.store_mark[si] = last_store
                ring_release(si, cV, qd)
            return last_store

    if prep_on(7):
        q1_done = quant(w1_h, 4096, 1024, s_w1q, 0)
        q2_done = quant(w2_h, 1024, 4096, s_w2q, 1)

    if prep_on(8):
        bb.wait("scalar", "sst", q1_done)
        for c in range(8):
            si = slot_idx[0] % 4
            slot_idx[0] += 1
            rsem, rval = slot_rel[si]
            bb.wait("scalar", rsem, rval)
            ld = bb.dmat("scalar", f"rs{si}", ring[si], s_w1q[:, c * 128:(c + 1) * 128])
            bb.wait("sync", f"rs{si}", ld)
            st = bb.dma("sync", f"rs{si}", s_w1qT[c * 128:(c + 1) * 128, :], ring[si])
            slot_rel[si] = (f"rs{si}", st)
        bb.wait("scalar", "sst", q2_done)
        for hc in range(32):
            si = slot_idx[0] % 4
            slot_idx[0] += 1
            rsem, rval = slot_rel[si]
            bb.wait("scalar", rsem, rval)
            ld = bb.dmat("scalar", f"rs{si}", ring[si][:, 0:1024],
                         s_w2q[:, hc * 128:(hc + 1) * 128])
            bb.wait("sync", f"rs{si}", ld)
            st = bb.dma("sync", f"rs{si}", s_w2qT[hc * 128:(hc + 1) * 128, :],
                        ring[si][:, 0:1024])
            slot_rel[si] = (f"rs{si}", st)

    tap("w1q0", s_w1q[0:128, :], "stm", bb.sem["stm"].n, bf=True)
    tap("w1qT0", s_w1qT[0:128, :], "stm", bb.sem["stm"].n, bf=True)
    bb.wait("sync", "sst", bb.sem["sst"].n)
    for _si in range(4):
        bb.wait("sync", f"rs{_si}", bb.sem[f"rs{_si}"].n)
    all_st = bb.fence("sync", "dSt", bb.sem["dSt"].n)
    for _si in range(4):
        if slot_rel[_si][0] == "dSt":
            slot_rel[_si] = ("stm", all_st)

    def slice_src(idx):
        if idx == 0:
            return cm_src(s_wq)
        if idx == 1:
            return cm_src(s_wk)
        if idx in (2, 3):
            return cm_src(s_wv[:, (idx - 2) * 512:(idx - 1) * 512])
        if idx in (4, 5):
            return cm_src(s_wg[:, (idx - 4) * 512:(idx - 3) * 512])
        if idx in (6, 7):
            return cm_src(s_wo[:, (idx - 6) * 512:(idx - 5) * 512])
        if 8 <= idx < 16:
            nsl = idx - 8
            return cm_src(s_w1qT[:, nsl * 512:(nsl + 1) * 512])
        ec = (idx - 16) % 8
        return cm_src(s_w2qT[:, ec * 128:(ec + 1) * 128])

    def wslice(idx):
        si, sn, ld = ring_load("sync", slice_src(idx))
        bb.wait("tensor", sn, ld)
        return si

    def wslice_end(si):
        ring_release(si, cP, bb.sem["cP"].n)

    out_store_done = {}
    xb_loaded = {}
    xf_loaded = {}
    h_done_prev = [0]

    bb.dma("gpsimd", "sxb0", x_b[0], xT[:, 0:TW].rearrange("(c p) t -> p c t", p=128))
    xb_loaded[0] = ("sxb0", bb.sem["sxb0"].n)
    bb.dma("scalar", "sxf", x_f, xT[:, 0:TW].rearrange("(c p) t -> p c t", p=128))
    xf_loaded[0] = bb.sem["sxf"].n

    for t in range(min(NT, DBG_TILES)):
        seq = t // 4
        col0 = t * TW
        xb = x_b[0]

        if t % 4 == 0:
            bb.v("memset", S_f, 0.0)
            bb.v("memset", S_b, 0.0)
        s_bf_ready = bb.sem["cV"].n

        if DBG_PHASE < 1:
            continue
        bb.wait("scalar", xb_loaded[t][0], xb_loaded[t][1])
        bb.wait("tensor", xb_loaded[t][0], xb_loaded[t][1])
        ssq_pe = {}
        for c in range(8):
            if c >= 2:
                bb.wait("scalar", cP, ssq_pe[c - 2])
            sq = bb.act(scrAb[:, (c % 2) * 512:(c % 2) * 512 + 512],
                        xb[:, c * 512:(c + 1) * 512], AF.Square)
            bb.wait("tensor", cS, sq)
            bb.mm(pf[0][0:1, :], triu[:, 127:128], xb[:, c * 512:(c + 1) * 512],
                  start=(c == 0), stop=(c == 7))
            ssq_pe[c] = bb.mm(pf[1][0:1, :], triu[:, 127:128],
                              scrAb[:, (c % 2) * 512:(c % 2) * 512 + 512],
                              start=(c == 0), stop=(c == 7), inc=True)
        stats_done = bb.sem["cP"].n
        bb.wait("vector", cP, stats_done)
        mu_r = rowsf[:, 0:512]
        ve_r = rowsf[:, 512:1024]
        t1_r = rowsf[:, 1024:1536]
        t2_r = rowsf[:, 1536:2048]
        bb.v("tensor_scalar", mu_r, pf[0][0:1, :], 1.0 / D, None, op0=ALU.mult)
        bb.v("tensor_scalar", ve_r, pf[1][0:1, :], 1.0 / D, None, op0=ALU.mult)
        bb.v("tensor_tensor", t1_r, mu_r, mu_r, op=ALU.mult)
        bb.v("tensor_tensor", ve_r, ve_r, t1_r, op=ALU.subtract)
        magic_rsqrt(bb, ve_r, ve_r, t1_r, t2_r, EPS)
        bb.v("tensor_tensor", t1_r, mu_r, ve_r, op=ALU.mult)
        bb.v("tensor_copy", rowsb[:, 0:512], ve_r)
        rows_bf_done = bb.v("tensor_copy", rowsb[:, 512:1024], t1_r)

        ocar = modrows[:, seq * 2048:seq * 2048 + 1024]
        nsar = modrows[:, seq * 2048 + 1024:seq * 2048 + 2048]
        bb.wait("tensor", "dMr", modrows_done)
        lnapply = {}
        for c in range(8):
            bA = pf[2 + 2 * (c % 2)]
            bB2 = pf[3 + 2 * (c % 2)]
            bb.wait("tensor", cV, rows_bf_done)
            if c >= 2:
                bb.wait("tensor", cV, lnapply[c - 2])
            bb.mm(bA, ocar[:, c * 128:(c + 1) * 128], rowsb[:, 0:512],
                  start=True, stop=True)
            bb.mm(bB2, ocar[:, c * 128:(c + 1) * 128], rowsb[:, 512:1024],
                  start=True, stop=False)
            pe_d = bb.mm(bB2, nsar[:, c * 128:(c + 1) * 128], onesr,
                         start=False, stop=True, inc=True)
            bb.wait("vector", cP, pe_d)
            bb.v("tensor_tensor", h_b[:, c * 512:(c + 1) * 512],
                 xb[:, c * 512:(c + 1) * 512], bA, op=ALU.mult)
            lnapply[c] = bb.v("scalar_tensor_tensor", h_b[:, c * 512:(c + 1) * 512],
                              bB2, -1.0, h_b[:, c * 512:(c + 1) * 512],
                              op0=ALU.mult, op1=ALU.add)
        h_done = bb.sem["cV"].n
        h_done_prev[0] = h_done
        if t + 1 < NT:
            bb.wait("gpsimd", cV, h_done)
            bb.dma("gpsimd", "sxb0", x_b[0],
                   xT[:, (t + 1) * TW:(t + 2) * TW].rearrange("(c p) t -> p c t", p=128))
            xb_loaded[t + 1] = ("sxb0", bb.sem["sxb0"].n)
        if t == 0:
            tap("xb", xb, "cV", h_done, bf=True)
            tap("h", h_b, "cV", h_done, bf=True)
            tap("rows", rowsb, "cV", h_done, bf=True)

        if DBG_PHASE < 2:
            continue
        evac_tick = [0]

        def evac(dst, src_ps, pewait):
            k = evac_tick[0]
            evac_tick[0] += 1
            if k % 2 == 0:
                bb.wait("vector", cP, pewait)
                return ("cV", bb.v("tensor_copy", dst, src_ps))
            bb.wait("scalar", cP, pewait)
            return ("cS", bb.scopy(dst, src_ps))

        def pe_wait(mark):
            bb.wait("tensor", mark[0], mark[1])

        bb.wait("tensor", cV, h_done)
        marks = []
        for which, dstbuf, slc in ((0, qg, 0), (1, ki, 1)):
            si = wslice(slc)
            for m in range(4):
                bank = pf[(which * 2) + m % 2]
                if len(marks) >= 2:
                    pe_wait(marks[-2])
                for c in range(8):
                    pe_d = bb.mm(bank,
                                 ring[si][:, c * 512 + m * 128: c * 512 + (m + 1) * 128],
                                 h_b[:, c * 512:(c + 1) * 512],
                                 start=(c == 0), stop=(c == 7), inc=(c == 7))
                marks.append(evac(dstbuf[:, m * 512:(m + 1) * 512], bank, pe_d))
            wslice_end(si)
        bb.wait("tensor", "dGc", gk_loaded)
        for c in range(8):
            pe_d = bb.mm(pf[4][0:16, :], wgk1s[:, c * 16:(c + 1) * 16],
                         h_b[:, c * 512:(c + 1) * 512],
                         start=(c == 0), stop=(c == 7), inc=(c == 7))
        zmark = evac(atb[0:16, 0:512], pf[4][0:16, :], pe_d)
        pe_wait(zmark)
        bb.v("tensor_tensor_scan", colB[:, 12:14], onessc[:, 0:2], onessc[:, 0:2],
             0.0, op0=ALU.mult, op1=ALU.add, inc=False)
        lgm = {}
        for m in range(4):
            bank = pf[5] if m % 2 == 0 else pf[4]
            if m >= 2:
                bb.wait("tensor", cV, lgm[m - 2])
            pe_d = bb.mm(bank, wgk2s[:, m * 128:(m + 1) * 128], atb[0:16, 0:512],
                         start=True, stop=True, inc=True)
            bb.wait("scalar", cP, pe_d)
            if m >= 1:
                bb.wait("scalar", cV, lgm[m - 1])
            bb.act(scrA, bank, AF.Exp, bias=nbgk2[:, m:m + 1], scale=actc[:, 3:4])
            lg = bb.act(scrB, scrA, AF.Ln, bias=actc[:, 1:2], scale=1.0)
            if t == 0 and m == 0:
                tap("scrA", scrA, "cS", lg)
                tap("scrB", scrB, "cS", lg)
            bb.wait("vector", cS, lg)
            for sg in range(4):
                bb.v("tensor_tensor_scan",
                     Gp[:, m * 512 + sg * 128: m * 512 + (sg + 1) * 128],
                     onessc, scrB[:, sg * 128:(sg + 1) * 128], 0.0,
                     op0=ALU.mult, op1=ALU.add, inc=(sg == 3))
            lgm[m] = bb.sem["cV"].n
        gp_done = bb.sem["cV"].n
        if t == 0:
            tap("gp", Gp, "cV", gp_done)
            tap("qkraw", qg, "cV", gp_done, bf=True)
            tap("vtm", v_tm, "cV", gp_done, bf=True)
        vmarks = []
        for half in range(2):
            si = wslice(2 + half)
            for s in range(NSUB):
                bank = pf[s % 2]
                if len(vmarks) >= 2:
                    pe_wait(vmarks[-2])
                for c in range(8):
                    pe_d = bb.mm(bank,
                                 h_b[:, c * 512 + s * 128: c * 512 + (s + 1) * 128],
                                 ring[si][:, c * 512:(c + 1) * 512],
                                 start=(c == 0), stop=(c == 7), inc=(c == 7))
                vmarks.append(evac(v_tm[:, s * 1024 + half * 512:
                                        s * 1024 + (half + 1) * 512], bank, pe_d))
            wslice_end(si)
        gmarks = []
        for half in range(2):
            si = wslice(4 + half)
            for m in range(4):
                mc = half * 4 + m
                bank = pf[2 + mc % 2]
                if len(gmarks) >= 2:
                    pe_wait(gmarks[-2])
                for c in range(8):
                    pe_d = bb.mm(bank,
                                 ring[si][:, c * 512 + m * 128: c * 512 + (m + 1) * 128],
                                 h_b[:, c * 512:(c + 1) * 512],
                                 start=(c == 0), stop=(c == 7), inc=(c == 7))
                gmarks.append(evac(gsig[:, mc * 512:(mc + 1) * 512], bank, pe_d))
            wslice_end(si)

        if DBG_PHASE < 3:
            continue
        bb.wait("scalar", cV, gp_done)
        qgki = 0
        for m in range(4):
            eq = bb.act(scrA, Gp[:, m * 512:(m + 1) * 512], AF.Exp,
                        scale=actc[:, 4:5], bias=actc[:, 2:3])
            bb.wait("vector", cS, eq)
            bb.v("tensor_tensor", qg[:, m * 512:(m + 1) * 512],
                 qg[:, m * 512:(m + 1) * 512], scrA, op=ALU.mult)
            ek = bb.act(scrB, Gp[:, m * 512:(m + 1) * 512], AF.Exp,
                        scale=actc[:, 5:6], bias=0.0)
            bb.wait("vector", cS, ek)
            qgki = bb.v("tensor_tensor", ki[:, m * 512:(m + 1) * 512],
                        ki[:, m * 512:(m + 1) * 512], scrB, op=ALU.mult)
            bb.wait("scalar", cV, qgki)
        glp = bass.AP(Gp.tensor, Gp.offset + 127,
                      [[2048, 128], [128, 4], [512, 4]])
        glo = bass.AP(glall.tensor, glall.offset, [[16, 128], [4, 4], [1, 4]])
        pk = bb.v("tensor_copy", glo, glp)
        bb.wait("scalar", cV, pk)
        bb.act(glall[:, 0:16], glall[:, 0:16], AF.Exp, scale=actc[:, 4:5], bias=0.0)
        gl_done = bb.sem["cS"].n

        o_pe_prev = 0
        for s in range(NSUB):
            bb.wait("vector", cS, gl_done)
            if s >= 1:
                bb.wait("vector", cP, o_pe_prev)
            for hh in range(H):
                kg_d = bb.v("tensor_scalar", atb[:, hh * 128:(hh + 1) * 128],
                            ki[:, hh * 512 + s * 128: hh * 512 + (s + 1) * 128],
                            glall[:, s * 4 + hh:s * 4 + hh + 1], None, op0=ALU.mult,
                            inc=(hh == H - 1))
            bb.wait("tensor", cV, kg_d)
            for hh in range(H):
                kgt_pe = bb.trans(pb[0][:, hh * 128:(hh + 1) * 128],
                                  atb[:, hh * 128:(hh + 1) * 128], identb,
                                  inc=(hh == H - 1))
            bb.wait("vector", cP, kgt_pe)
            kgtm_d = bb.v("tensor_copy", kgtm, pb[0][:, 0:512])

            atm = {}
            for hh in range(H):
                bank = pf[2 + hh % 2]
                if hh >= 2:
                    bb.wait("tensor", cV, atm[hh - 2])
                at_pe = bb.mm(bank[:, 0:128],
                              ki[:, hh * 512 + s * 128: hh * 512 + (s + 1) * 128],
                              qg[:, hh * 512 + s * 128: hh * 512 + (s + 1) * 128],
                              start=True, stop=True, inc=True)
                bb.wait("vector", cP, at_pe)
                atm[hh] = bb.v("tensor_tensor", atb[:, hh * 128:(hh + 1) * 128],
                               bank[:, 0:128], triu, op=ALU.mult)
            at_done = bb.sem["cV"].n

            bb.wait("tensor", cV, at_done)
            for hh in range(H):
                bank = pf[hh // 2]
                off = (hh % 2) * 256
                bb.mm(bank[:, off:off + 256], atb[:, hh * 128:(hh + 1) * 128],
                      v_tm[:, s * 1024 + hh * 256: s * 1024 + (hh + 1) * 256],
                      start=True, stop=False)
                o_pe = bb.mm(bank[:, off:off + 256],
                             qg[:, hh * 512 + s * 128: hh * 512 + (s + 1) * 128],
                             S_b[:, hh * 256:(hh + 1) * 256], start=False, stop=True,
                             inc=(hh == H - 1))
            o_pe_prev = o_pe
            bb.wait("tensor", cV, kgtm_d)
            for hh in range(H):
                bank = pf[4 + hh // 2]
                off = (hh % 2) * 256
                kv_pe = bb.mm(bank[:, off:off + 256], kgtm[:, hh * 128:(hh + 1) * 128],
                              v_tm[:, s * 1024 + hh * 256: s * 1024 + (hh + 1) * 256],
                              start=True, stop=True, inc=(hh == H - 1))

            bb.wait("scalar", cP, o_pe)
            for hh in range(H):
                bank = pf[hh // 2]
                off = (hh % 2) * 256
                gn = bb.act(scrAb[:, 0:256], bank[:, off:off + 256], AF.Square,
                            accum=ssqo[:, s * 4 + hh:s * 4 + hh + 1])
            gn = acc_fence()
            bb.wait("vector", cS, gn)
            bb.v("tensor_scalar", colA[:, s * 4:(s + 1) * 4],
                 ssqo[:, s * 4:(s + 1) * 4], 1.0 / DVH, None, op0=ALU.mult)
            magic_rsqrt(bb, colA[:, s * 4:(s + 1) * 4], colA[:, s * 4:(s + 1) * 4],
                        colB[:, 0:4], colB[:, 4:8], EPS)
            for hh in range(H):
                bank = pf[hh // 2]
                off = (hh % 2) * 256
                on_d = bb.v("tensor_scalar", o_n[:, hh * 256:(hh + 1) * 256],
                            bank[:, off:off + 256],
                            colA[:, s * 4 + hh:s * 4 + hh + 1], None, op0=ALU.mult,
                            inc=(hh == H - 1))
            bb.wait("vector", cP, kv_pe)
            for hh in range(H):
                bank = pf[4 + hh // 2]
                off = (hh % 2) * 256
                bb.v("tensor_scalar", S_f[:, hh * 256:(hh + 1) * 256],
                     S_f[:, hh * 256:(hh + 1) * 256],
                     glall[:, s * 4 + hh:s * 4 + hh + 1], None, op0=ALU.mult)
                bb.v("tensor_tensor", S_f[:, hh * 256:(hh + 1) * 256],
                     S_f[:, hh * 256:(hh + 1) * 256], bank[:, off:off + 256],
                     op=ALU.add)
            s_bf_ready = bb.v("tensor_copy", S_b, S_f)

            if s == 0:
                for c in range(8):
                    pe_wait_mark = gmarks[c]
                    if pe_wait_mark[0] == "cV":
                        bb.wait("scalar", cV, pe_wait_mark[1])
                    gs = bb.act(gsig[:, c * 512:(c + 1) * 512],
                                gsig[:, c * 512:(c + 1) * 512], AF.Silu)
                gsil_done = gs
                bb.wait("vector", cS, gsil_done)

            bb.wait("tensor", cV, on_d)
            for cblk in range(8):
                ont_pe = bb.trans(pb[1][:, cblk * 128:(cblk + 1) * 128],
                                  o_n[:, cblk * 128:(cblk + 1) * 128], identb,
                                  inc=(cblk == 7))
            bb.wait("vector", cP, ont_pe)
            if s == 0:
                bb.wait("vector", cS, gsil_done)
            og_src = bass.AP(pb[1].tensor, pb[1].offset, [[1024, 128], [128, 8], [1, 128]])
            og_gs = bass.AP(gsig.tensor, gsig.offset + s * 128,
                            [[16384, 128], [512, 8], [1, 128]])
            og_gs2 = bass.AP(gsig.tensor, gsig.offset + s * 128,
                             [[16384, 128], [512, 8], [1, 128]])
            og_d = bb.v("tensor_tensor", og_gs, og_src, og_gs2, op=ALU.mult)
            bb.wait("tensor", cV, og_d)
        og_all = bb.sem["cV"].n

        if DBG_PHASE < 4:
            continue
        womarks = []
        bb.wait("tensor", cV, og_all)
        for half in range(2):
            si = wslice(6 + half)
            for m in range(4):
                mc = half * 4 + m
                bank = pf[mc % 2]
                if len(womarks) >= 2:
                    bb.wait("tensor", cV, womarks[-2])
                for c in range(8):
                    wo_pe = bb.mm(bank,
                                  ring[si][:, c * 512 + m * 128: c * 512 + (m + 1) * 128],
                                  gsig[:, c * 512:(c + 1) * 512],
                                  start=(c == 0), stop=(c == 7), inc=(c == 7))
                bb.wait("vector", cP, wo_pe)
                bb.wait("vector", "sxf", xf_loaded[t])
                womarks.append(bb.v("scalar_tensor_tensor",
                                    x_f[:, mc * 512:(mc + 1) * 512], bank,
                                    gcols[:, mc * 2 + seq:mc * 2 + seq + 1],
                                    x_f[:, mc * 512:(mc + 1) * 512],
                                    op0=ALU.mult, op1=ALU.add))
            wslice_end(si)
        x1_done = bb.sem["cV"].n
        if t == 0:
            tap("x1", x_f, "cV", x1_done)
            tap("og", gsig, "cV", x1_done, bf=True)
            tap("ssqo", ssqo, "cV", x1_done)
            tap("glall", glall, "cV", x1_done)
            tap("colA", colA, "cV", x1_done)
            tap("Sf", S_f, "cV", x1_done)
            tap("on", o_n, "cV", x1_done, bf=True)

        if DBG_PHASE < 5:
            continue
        evm = {}
        xcm = {}
        for s in range(NSUB):
            xc_src = bass.AP(x_f.tensor, x_f.offset + s * 128,
                             [[4096, 128], [512, 8], [1, 128]])
            xcm[s] = bb.v("tensor_copy", ttm4[:, s * 1024:(s + 1) * 1024], xc_src)
        for s in range(NSUB):
            bb.wait("tensor", cV, xcm[s])
            if s >= 2:
                bb.wait("tensor", cV, evm[s - 2])
            for blk in range(8):
                x1t_pe = bb.trans(pb[s % 2][:, blk * 128:(blk + 1) * 128],
                                  ttm4[:, s * 1024 + blk * 128:
                                       s * 1024 + (blk + 1) * 128],
                                  identb, inc=(blk == 7))
            bb.wait("vector", cP, x1t_pe)
            evm[s] = bb.v("tensor_copy", xtm4[:, s * 1024:(s + 1) * 1024],
                          pb[s % 2][:, 0:1024])
        for s in range(NSUB):
            bb.wait("scalar", cV, evm[s])
            bb.act(scrAb[:, 0:1024], xtm4[:, s * 1024:(s + 1) * 1024], AF.Square,
                   accum=sumP[:, 0 + s:1 + s])
            bb.act(scrAb[:, 0:1024], xtm4[:, s * 1024:(s + 1) * 1024], AF.Copy,
                   accum=sumP[:, 4 + s:5 + s])
        ssq2 = acc_fence()
        bb.wait("vector", cS, ssq2)
        bb.v("tensor_scalar", colA[:, 0:4], sumP[:, 4:8], 1.0 / D, None,
             op0=ALU.mult)
        m_ = bb.v("tensor_scalar", colA[:, 4:8], sumP[:, 0:4], 1.0 / D, None,
                  op0=ALU.mult)
        bb.wait("vector", cV, m_)
        m_ = bb.v("tensor_tensor", colB[:, 8:12], colA[:, 0:4], colA[:, 0:4],
                  op=ALU.mult)
        bb.wait("vector", cV, m_)
        bb.v("tensor_tensor", colA[:, 4:8], colA[:, 4:8], colB[:, 8:12],
             op=ALU.subtract)
        magic_rsqrt(bb, colA[:, 4:8], colA[:, 4:8], colB[:, 8:12],
                    colB[:, 12:16], EPS)
        h2m = {}
        for s in range(NSUB):
            sl = slice(s * 1024, (s + 1) * 1024)
            bb.v("tensor_scalar", ttm4[:, sl], xtm4[:, sl], colA[:, s:s + 1],
                 colA[:, 4 + s:5 + s], op0=ALU.subtract, op1=ALU.mult)
            bb.v("tensor_tensor", ttm4[:, sl], ttm4[:, sl], cbB[seq], op=ALU.mult)
            h2m[s] = bb.v("scalar_tensor_tensor", ttm4[:, sl], ttm4[:, sl], 1.0,
                          sbB[seq], op0=ALU.mult, op1=ALU.add)
        for s in range(NSUB):
            bb.wait("scalar", cV, h2m[s])
            bb.act(scrAb[:, 0:1024], ttm4[:, s * 1024:(s + 1) * 1024], AF.Square,
                   accum=sumP[:, 8 + s:9 + s])
            bb.act(scrAb[:, 0:1024], ttm4[:, s * 1024:(s + 1) * 1024], AF.Copy,
                   accum=sumP[:, 12 + s:13 + s])
        s3 = acc_fence()
        bb.wait("vector", cS, s3)
        bb.v("tensor_scalar", colA[:, 8:12], sumP[:, 12:16], 1.0 / D, None,
             op0=ALU.mult)
        m_ = bb.v("tensor_scalar", colA[:, 12:16], sumP[:, 8:12], 1.0 / D, None,
                  op0=ALU.mult)
        bb.wait("vector", cV, m_)
        m_ = bb.v("tensor_tensor", colB[:, 8:12], colA[:, 8:12], colA[:, 8:12],
                  op=ALU.mult)
        bb.wait("vector", cV, m_)
        bb.v("tensor_tensor", colA[:, 12:16], colA[:, 12:16], colB[:, 8:12],
             op=ALU.subtract)
        magic_rsqrt(bb, colA[:, 12:16], colA[:, 12:16], colB[:, 8:12],
                    colB[:, 12:16], 1e-5)
        h3m = {}
        for s in range(NSUB):
            sl = slice(s * 1024, (s + 1) * 1024)
            bb.v("tensor_scalar", ttm4[:, sl], ttm4[:, sl], colA[:, 8 + s:9 + s],
                 colA[:, 12 + s:13 + s], op0=ALU.subtract, op1=ALU.mult)
            bb.v("tensor_tensor", ttm4[:, sl], ttm4[:, sl], g1B, op=ALU.mult)
            h3m[s] = bb.v("tensor_tensor", ttm4[:, sl], ttm4[:, sl], b1B, op=ALU.add)
        for pair in ((0, 1), (2, 3)):
            bb.wait("scalar", cV, h3m[pair[1]])
            for k, s in enumerate(pair):
                bb.act(hscr[:, k * 1024:k * 1024 + 1024],
                       ttm4[:, s * 1024:(s + 1) * 1024], AF.Abs)
            absm = acc_fence()
            bb.wait("vector", cS, absm)
            for k, s in enumerate(pair):
                bb.v("reduce_max", ssqo[:, 2 * s:2 * s + 1],
                     hscr[:, k * 1024:k * 1024 + 512], axis=AX)
                bb.v("reduce_max", ssqo[:, 2 * s + 1:2 * s + 2],
                     hscr[:, k * 1024 + 512:k * 1024 + 1024], axis=AX)
        m_ = bb.sem["cV"].n
        bb.wait("vector", cV, m_)
        for s in range(NSUB):
            mxd = bb.v("scalar_tensor_tensor", colA[:, 16 + s:17 + s],
                       ssqo[:, 2 * s:2 * s + 1], 1e-5, ssqo[:, 2 * s + 1:2 * s + 2],
                       op0=ALU.max, op1=ALU.max)
        bb.wait("scalar", cV, mxd)
        lnr = bb.act(colB[:, 8:12], colA[:, 16:20], AF.Ln)
        bb.wait("scalar", cS, lnr)
        sxp = bb.act(colA[:, 20:24], colB[:, 8:12], AF.Exp,
                     scale=actc[:, 3:4], bias=actc[:, 8:9])
        bb.wait("vector", cS, sxp)
        bb.v("tensor_scalar", colB[:, 12:16], colA[:, 16:20], wmean[:, 0:1],
             1.0 / 127.0, op0=ALU.mult, op1=ALU.mult)
        hqd_prev = {}
        for s in range(NSUB):
            sl = slice(s * 1024, (s + 1) * 1024)
            bb.v("tensor_scalar", hscr[:, 0:1024], ttm4[:, sl], colA[:, 20 + s:21 + s],
                 CRND, op0=ALU.mult, op1=ALU.add)
            dq_d = bb.v("tensor_scalar", dqtm, hscr[:, 0:1024], CRND,
                        colB[:, 12 + s:13 + s], op0=ALU.subtract, op1=ALU.mult)
            if t == 0 and s == 0:
                tap("midcols", colA, "cV", dq_d)
                tap("h3", ttm4, "cV", dq_d, bf=True)
                tap("dq0", dqtm, "cV", dq_d, bf=True)
            bb.wait("tensor", cV, dq_d)
            if s >= 2:
                bb.wait("tensor", cV, hqd_prev[s - 2])
            for cblk in range(8):
                dqt_pe = bb.trans(pb[s % 2][:, cblk * 128:(cblk + 1) * 128],
                                  dqtm[:, cblk * 128:(cblk + 1) * 128], identb,
                                  inc=(cblk == 7))
            bb.wait("vector", cP, dqt_pe)
            hq_src = bass.AP(pb[s % 2].tensor, pb[s % 2].offset,
                             [[1024, 128], [128, 8], [1, 128]])
            hq_dst = bass.AP(h2qT.tensor, h2qT.offset + s * 128,
                             [[4096, 128], [512, 8], [1, 128]])
            hqd_prev[s] = bb.v("tensor_copy", hq_dst, hq_src)
            if s == NSUB - 1:
                bb.wait("tensor", cV, hqd_prev[s])
        h2qT_done = bb.sem["cV"].n
        if t == 0:
            tap("h2qT", h2qT, "cV", h2qT_done, bf=True)

        if DBG_PHASE < 6:
            continue
        gelum = {}
        for nsl in range(8):
            si = wslice(8 + nsl)
            if nsl == 0:
                bb.wait("tensor", cV, h2qT_done)
                if t == 0:
                    tap("w1slot", ring[si], f"rs{si}", bb.sem[f"rs{si}"].n, bf=True)
                    tap("b1row", b1row, None, 0, bf=True)
            for s in range(NSUB):
                bank = pf[s]
                if nsl >= 1:
                    bb.wait("tensor", cS, gelum[(nsl - 1, s)])
                bb.mm(bank, ones1, b1row[:, nsl * 512:(nsl + 1) * 512],
                      start=True, stop=False)
                for c in range(8):
                    y1_pe = bb.mm(bank,
                                  h2qT[:, c * 512 + s * 128: c * 512 + (s + 1) * 128],
                                  ring[si][:, c * 512:(c + 1) * 512],
                                  start=False, stop=(c == 7), inc=(c == 7))
                bb.wait("scalar", cP, y1_pe)
                if t == 0 and nsl == 0 and s == 0 and "y1pe" in DBG_TAPS:
                    cpm = bb.scopy(scrA[:, 0:512], bank)
                    tap("y1pe", scrA, "cS", cpm)
                gelum[(nsl, s)] = bb.act(y1g[s][:, nsl * 512:(nsl + 1) * 512], bank,
                                         AF.Gelu,
                                         accum=ygsum[:, s * 8 + nsl:s * 8 + nsl + 1])
            wslice_end(si)
            if nsl % 2 == 1:
                q4 = nsl // 2
                for s in range(NSUB):
                    bb.act(scrAb[:, 0:1024], y1g[s][:, q4 * 1024:(q4 + 1) * 1024],
                           AF.Square, accum=sumP[:, s * 4 + q4:s * 4 + q4 + 1],
                           inc=(s == 3))

        if DBG_PHASE < 7:
            continue
        if t == 0:
            if "y1g0" in DBG_TAPS:
                bb.wait("sync", "cS", gelum[(7, 0)])
            tap("y1g0", y1g[0], None, 0, bf=True)
        ttm4f = ttm4.bitcast(F32)
        bb.wait("scalar", cS, gelum[(7, 3)])
        acc_fence()
        bb.wait("vector", cS, bb.sem["cS"].n)
        for s in range(NSUB):
            bb.v("tensor_reduce", colA[:, 24 + s:25 + s],
                 ygsum[:, s * 8:(s + 1) * 8], axis=AX, op=ALU.add)
            bb.v("tensor_reduce", colA[:, 28 + s:29 + s],
                 sumP[:, s * 4:(s + 1) * 4], axis=AX, op=ALU.add)
        bb.wait("vector", cV, bb.sem["cV"].n)
        bb.v("tensor_scalar", colA[:, 24:28], colA[:, 24:28], 1.0 / 4096.0, None,
             op0=ALU.mult)
        m_ = bb.v("tensor_scalar", colA[:, 28:32], colA[:, 28:32], 1.0 / 4096.0,
                  None, op0=ALU.mult)
        bb.wait("vector", cV, m_)
        m_ = bb.v("tensor_tensor", colB[:, 8:12], colA[:, 24:28], colA[:, 24:28],
                  op=ALU.mult)
        bb.wait("vector", cV, m_)
        bb.v("tensor_tensor", colA[:, 28:32], colA[:, 28:32], colB[:, 8:12],
             op=ALU.subtract)
        magic_rsqrt(bb, colA[:, 28:32], colA[:, 28:32], colB[:, 8:12],
                    colB[:, 12:16], 1e-5)
        h4m = {}
        for s in range(NSUB):
            bb.v("tensor_scalar", y1g[s], y1g[s], colA[:, 24 + s:25 + s],
                 colA[:, 28 + s:29 + s], op0=ALU.subtract, op1=ALU.mult)
            bb.v("tensor_tensor", y1g[s], y1g[s], g2B, op=ALU.mult)
            h4m[s] = bb.v("tensor_tensor", y1g[s], y1g[s], b2B, op=ALU.add)
        w2marks = []
        ytd_prev = [0, 0]
        for half in range(2):
            scr0 = hscr if half == 0 else GpT
            for s in (2 * half, 2 * half + 1):
                bb.wait("scalar", cV, h4m[s])
                bb.act(scr0[:, 0:1024], y1g[s][:, 0:1024], AF.Abs)
                bb.act(scr0[:, 1024:2048], y1g[s][:, 1024:2048], AF.Abs)
                bb.act(ttm4f[:, 0:1024], y1g[s][:, 2048:3072], AF.Abs)
                a4 = bb.act(ttm4f[:, 1024:2048], y1g[s][:, 3072:4096], AF.Abs)
                bb.wait("vector", cS, a4)
                for q4, src in enumerate((scr0[:, 0:1024], scr0[:, 1024:2048],
                                          ttm4f[:, 0:1024], ttm4f[:, 1024:2048])):
                    rm = bb.v("reduce_max", ssqo[:, s * 4 + q4:s * 4 + q4 + 1], src,
                              axis=AX)
                bb.wait("scalar", cV, rm)
            bb.wait("vector", cV, bb.sem["cV"].n)
            for s in (2 * half, 2 * half + 1):
                bb.v("scalar_tensor_tensor", colB[:, 8 + s:9 + s],
                     ssqo[:, s * 4:s * 4 + 1], 1e-5, ssqo[:, s * 4 + 1:s * 4 + 2],
                     op0=ALU.max, op1=ALU.max)
                m_ = bb.v("scalar_tensor_tensor", colB[:, 12 + s:13 + s],
                          ssqo[:, s * 4 + 2:s * 4 + 3], 1e-5,
                          ssqo[:, s * 4 + 3:s * 4 + 4], op0=ALU.max, op1=ALU.max)
            bb.wait("vector", cV, m_)
            h2 = 2 * half
            mxd = bb.v("tensor_tensor", colA[:, 32 + h2:34 + h2],
                       colB[:, 8 + h2:10 + h2], colB[:, 12 + h2:14 + h2],
                       op=ALU.max)
            bb.wait("scalar", cV, mxd)
            lnr = bb.act(colB[:, 8 + h2:10 + h2], colA[:, 32 + h2:34 + h2], AF.Ln)
            bb.wait("scalar", cS, lnr)
            sxp = bb.act(colA[:, 36 + h2:38 + h2], colB[:, 8 + h2:10 + h2], AF.Exp,
                         scale=actc[:, 3:4], bias=actc[:, 8:9])
            bb.wait("vector", cS, sxp)
            ks2m = bb.v("tensor_scalar", colA[:, 40 + h2:42 + h2],
                        colA[:, 32 + h2:34 + h2], wmean[:, 1:2],
                        1.0 / 127.0, op0=ALU.mult, op1=ALU.mult)
            if t == 0 and half == 0:
                tap("jcols", colA, "cV", ks2m)
            for s in (2 * half, 2 * half + 1):
                for hf in range(4):
                    bb.v("tensor_scalar", hscr[:, (hf % 2) * 1024:(hf % 2) * 1024 + 1024],
                         y1g[s][:, hf * 1024:(hf + 1) * 1024],
                         colA[:, 36 + s:37 + s], CRND, op0=ALU.mult, op1=ALU.add)
                    dq2_d = bb.v("tensor_scalar",
                                 y1g[s][:, hf * 1024:(hf + 1) * 1024],
                                 hscr[:, (hf % 2) * 1024:(hf % 2) * 1024 + 1024], CRND,
                                 colA[:, 40 + s:41 + s],
                                 op0=ALU.subtract, op1=ALU.mult)
                bb.wait("tensor", cV, dq2_d)
                for grp in range(4):
                    if ytd_prev[1]:
                        bb.wait("tensor", cV, ytd_prev[1])
                    for blk in range(8):
                        hcix = grp * 8 + blk
                        tp = bb.trans(pb[grp % 2][:, blk * 128:(blk + 1) * 128],
                                      y1g[s][:, hcix * 128:(hcix + 1) * 128], identb,
                                      inc=(blk == 7))
                    bb.wait("vector", cP, tp)
                    yt_src = bass.AP(pb[grp % 2].tensor, pb[grp % 2].offset,
                                     [[1024, 128], [128, 8], [1, 128]])
                    yt_dst = bass.AP(y1qT.tensor,
                                     y1qT.offset + grp * 8 * 256 + (s % 2) * 128,
                                     [[8192, 128], [256, 8], [1, 128]])
                    yt_d = bb.v("tensor_copy", yt_dst, yt_src)
                    ytd_prev[1] = ytd_prev[0]
                    ytd_prev[0] = yt_d
                bb.wait("tensor", cV, yt_d)
            y1qT_half = bb.sem["cV"].n
            if t == 0 and half == 0:
                tap("y1qT", y1qT, "cV", y1qT_half, bf=True)

            for ec in range(8):
                si = wslice(16 + half * 8 + ec)
                if ec == 0:
                    bb.wait("tensor", cV, y1qT_half)
                bank = pf[4 + ec % 2]
                if len(w2marks) >= 2:
                    bb.wait("tensor", cV, w2marks[-2])
                bb.mm(bank[:, 0:256], b2row[:, ec * 128:(ec + 1) * 128],
                      onesr[:, 0:256], start=True, stop=False)
                for hc in range(32):
                    y2_pe = bb.mm(bank[:, 0:256],
                                  ring[si][:, hc * 128:(hc + 1) * 128],
                                  y1qT[:, hc * 256:(hc + 1) * 256],
                                  start=False, stop=(hc == 31), inc=(hc == 31))
                bb.wait("vector", cP, y2_pe)
                w2marks.append(bb.v("scalar_tensor_tensor",
                                    x_f[:, ec * 512 + half * 256:
                                        ec * 512 + half * 256 + 256],
                                    bank[:, 0:256],
                                    gcols[:, 16 + ec * 2 + seq:16 + ec * 2 + seq + 1],
                                    x_f[:, ec * 512 + half * 256:
                                        ec * 512 + half * 256 + 256],
                                    op0=ALU.mult, op1=ALU.add))
                wslice_end(si)
        x2_done = bb.sem["cV"].n

        bb.wait("scalar", cV, x2_done)
        out_store_done[t] = bb.dma("scalar", "sst",
                                   outT[:, col0:col0 + TW]
                                   .rearrange("(c p) t -> p c t", p=128), x_f)
        if t + 1 < NT:
            bb.wait("scalar", "sst", out_store_done[t])
            bb.dma("scalar", "sxf", x_f,
                   xT[:, (t + 1) * TW:(t + 2) * TW]
                   .rearrange("(c p) t -> p c t", p=128))
            xf_loaded[t + 1] = bb.sem["sxf"].n

    if out_store_done:
        bb.wait("scalar", "sst", out_store_done[max(out_store_done)])
    else:
        mk = bb.dma("scalar", "sst", outT[:, 0:TW].rearrange("(c p) t -> p c t", p=128),
                    x_f)
        bb.wait("scalar", "sst", mk)

    for _sn in ("dC", "dSt", "rs0", "rs1", "rs2", "rs3"):
        bb.wait("sync", _sn, bb.sem[_sn].n)
    for _sn in ("rg0", "rg1", "rg2", "rg3", "dGc", "dStage", "dMr", "sxb0", "sxb1"):
        bb.wait("gpsimd", _sn, bb.sem[_sn].n)
    for _sn in ("sxf", "sst"):
        bb.wait("scalar", _sn, bb.sem[_sn].n)

    with nc.Block() as block:
        @block.sync
        def _(eng):
            for fn in bb.ops["sync"]:
                fn(eng)

        @block.tensor
        def _(eng):
            for fn in bb.ops["tensor"]:
                fn(eng)

        @block.vector
        def _(eng):
            for fn in bb.ops["vector"]:
                fn(eng)

        @block.scalar
        def _(eng):
            for fn in bb.ops["scalar"]:
                fn(eng)

        @block.gpsimd
        def _(eng):
            for fn in bb.ops["gpsimd"]:
                fn(eng)

    ctx.close()
    nc._tap_offsets = tap_offsets
    return nc


def kernel(**inputs):
    if "prog" not in _CACHED:
        _CACHED["prog"] = build_program()
    nc = _CACHED["prog"]

    x = np.ascontiguousarray(np.asarray(inputs["L0_attn_x"], np.float32))
    cfull = np.ascontiguousarray(np.asarray(inputs["L0_c"], np.float32))

    shared = dict(
        wmod=np.ascontiguousarray(inputs["L0_wmod"], np.float32),
        wq=np.ascontiguousarray(inputs["L0_attn_Wq"], np.float32),
        wk=np.ascontiguousarray(inputs["L0_attn_Wk"], np.float32),
        wv=np.ascontiguousarray(inputs["L0_attn_Wv"], np.float32),
        wg=np.ascontiguousarray(inputs["L0_attn_Wg"], np.float32),
        wo=np.ascontiguousarray(inputs["L0_attn_Wo"], np.float32),
        wgk1=np.ascontiguousarray(inputs["L0_attn_Wgk1"], np.float32),
        wgk2=np.ascontiguousarray(inputs["L0_attn_Wgk2"], np.float32),
        bgk2=np.ascontiguousarray(inputs["L0_attn_bgk2"], np.float32),
        gnw=np.ascontiguousarray(inputs["L0_attn_gnw"], np.float32),
        ln1g=np.ascontiguousarray(inputs["L0_ffn_ln1_g"], np.float32),
        ln1b=np.ascontiguousarray(inputs["L0_ffn_ln1_b"], np.float32),
        w1=np.ascontiguousarray(inputs["L0_ffn_w1"], np.float32),
        b1=np.ascontiguousarray(inputs["L0_ffn_b1"], np.float32),
        ln2g=np.ascontiguousarray(inputs["L0_ffn_ln2_g"], np.float32),
        ln2b=np.ascontiguousarray(inputs["L0_ffn_ln2_b"], np.float32),
        w2=np.ascontiguousarray(inputs["L0_ffn_w2"], np.float32),
        b2=np.ascontiguousarray(inputs["L0_ffn_b2"], np.float32),
        identb=np.eye(128, dtype=BF),
        identf=np.eye(128, dtype=np.float32),
        triu=np.triu(np.ones((128, 128), np.float32)).astype(BF),
        ones1=np.ones((1, 128), BF),
        onesr=np.ones((1, 512), BF),
        onessc=np.ones((128, 128), np.float32),
        actc=np.tile(np.array([[CRND, 1.0, LNSCALE, -1.0, -1.0 / 16.0, 1.0 / 16.0,
                                1e-5, -0.5, float(np.log(127.0)), 0.5]],
                              np.float32), (128, 1)),
    )
    in_maps = []
    for core in range(8):
        xTc = np.ascontiguousarray(
            np.concatenate([x[2 * core].T, x[2 * core + 1].T], axis=1))
        m = dict(shared)
        m["xT"] = xTc
        m["c2"] = np.ascontiguousarray(cfull[2 * core:2 * core + 2])
        in_maps.append(m)

    res = run_bass_kernel_spmd(nc, in_maps, core_ids=list(range(8)))
    out = np.empty_like(x)
    for core in range(8):
        oT = res.results[core]["outT"]
        out[2 * core] = oT[:, :2048].T
        out[2 * core + 1] = oT[:, 2048:].T
    return out
